# version 17
# baseline (speedup 1.0000x reference)
import contextlib

import numpy as np
import ml_dtypes

import concourse.bass as bass
import concourse.mybir as mybir
from concourse.bass_utils import run_bass_kernel_spmd

F32 = mybir.dt.float32
BF16 = mybir.dt.bfloat16
I32 = mybir.dt.int32
AF = mybir.ActivationFunctionType
ALU = mybir.AluOpType
AX = mybir.AxisListType
BFNP = ml_dtypes.bfloat16

D = 1024
HID = 3072
KC = 8
HCN = 24
EPS = 1e-5
SDQ = 1.0 / 127.0
SCL = 0.125
NCORES = 8
UNITS = 8
ENGS = ("tensor", "vector", "scalar", "sync", "gpsimd")

MV = 0
RST = 16
NMR = 24
MV3 = 48
LT1 = 64
LT4 = 72
RS4 = 76
NM4 = 80
MV4 = 84


def MV2P(p):
    return 32 + 2 * p


def LT2P(p):
    return 36 + p


def RS2P(p):
    return 38 + p


def NM2P(p):
    return 42 + p


class Ctr:
    def __init__(self, name):
        self.name = name
        self.n = 0
        self.sem = None


class Rec:
    def __init__(self):
        self.streams = {e: [] for e in ENGS}
        self.lastw = {}

    def op(self, e, f, inc=None, k=1):
        if inc is not None:
            inc.n += k

        def fn(eng, f=f, inc=inc, k=k):
            ins = f(eng)
            if inc is not None:
                ins.then_inc(inc.sem, k)

        self.streams[e].append(fn)
        return inc.n if inc is not None else None

    def wait(self, e, ctr, n=None):
        t = ctr.n if n is None else n
        if t <= 0 or t <= self.lastw.get((e, ctr.name), 0):
            return
        self.lastw[(e, ctr.name)] = t
        self.streams[e].append(lambda eng, c=ctr, t=t: eng.wait_ge(c.sem, t))


def build_nc(n_units=UNITS):
    nc = bass.Bass("TRN2", target_bir_lowering=False)

    x_d = nc.dram_tensor("x", [4, 2048, D], F32, kind="ExternalInput").ap()
    out_d = nc.dram_tensor("out", [4, 2048, D], F32, kind="ExternalOutput").ap()
    wq_d = nc.dram_tensor("wq", [D, D], F32, kind="ExternalInput").ap()
    wk_d = nc.dram_tensor("wk", [D, D], F32, kind="ExternalInput").ap()
    wv_d = nc.dram_tensor("wv", [D, D], F32, kind="ExternalInput").ap()
    wo_d = nc.dram_tensor("wo", [D, D], F32, kind="ExternalInput").ap()
    bq_d = nc.dram_tensor("bq", [D], F32, kind="ExternalInput").ap()
    bk_d = nc.dram_tensor("bk", [D], F32, kind="ExternalInput").ap()
    bv_d = nc.dram_tensor("bv", [D], F32, kind="ExternalInput").ap()
    bo_d = nc.dram_tensor("bo", [D], F32, kind="ExternalInput").ap()
    wg_d = nc.dram_tensor("wg", [HID, D], I32, kind="ExternalInput").ap()
    wu_d = nc.dram_tensor("wu", [HID, D], I32, kind="ExternalInput").ap()
    wd_d = nc.dram_tensor("wd", [D, HID], I32, kind="ExternalInput").ap()
    c_identb = nc.dram_tensor("c_identb", [128, 128], BF16, kind="ExternalInput").ap()
    c_ident8 = nc.dram_tensor("c_ident8", [8, 8], F32, kind="ExternalInput").ap()
    c_selh = nc.dram_tensor("c_selh", [128, 2], BF16, kind="ExternalInput").ap()
    c_bch = nc.dram_tensor("c_bch", [2, 128], BF16, kind="ExternalInput").ap()
    c_onesb = nc.dram_tensor("c_onesb", [1, 128], BF16, kind="ExternalInput").ap()
    scr_g = nc.dram_tensor("scr_g", [HCN, 128, D], BF16, kind="Internal").ap()
    scr_u = nc.dram_tensor("scr_u", [HCN, 128, D], BF16, kind="Internal").ap()
    scr_d = nc.dram_tensor("scr_d", [HCN, 128, D], BF16, kind="Internal").ap()
    scr_q = nc.dram_tensor("scr_q", [4, 128, 8192], BF16, kind="Internal").ap()

    with contextlib.ExitStack() as ctx:
        E = ctx.enter_context

        x_tm = E(nc.sbuf_tensor("x_tm", [128, 8 * D], F32))
        xh_tm = E(nc.sbuf_tensor("xh_tm", [128, 2 * D], BF16))
        xhB = E(nc.sbuf_tensor("xhB", [128, 8 * D], BF16))
        qkvo = E(nc.sbuf_tensor("qkvo", [128, 2 * 8192], BF16))
        ring_g = E(nc.sbuf_tensor("ring_g", [128, 4 * D], BF16))
        ring_u = E(nc.sbuf_tensor("ring_u", [128, 4 * D], BF16))
        ring_d = E(nc.sbuf_tensor("ring_d", [128, 4 * D], BF16))
        h_sb = E(nc.sbuf_tensor("h_sb", [128, HCN * D], BF16))
        q_sb = E(nc.sbuf_tensor("q_sb", [128, 512], BF16))
        k_sb = E(nc.sbuf_tensor("k_sb", [128, 512], BF16))
        v_sb = E(nc.sbuf_tensor("v_sb", [128, 2 * 512], BF16))
        qk_bf = E(nc.sbuf_tensor("qk_bf", [128, 2 * 512], BF16))
        a_bf = E(nc.sbuf_tensor("a_bf", [128, 2 * 512], BF16))
        p_sb = E(nc.sbuf_tensor("p_sb", [2, 2 * 512], BF16))
        smx = E(nc.sbuf_tensor("smx", [2, 32], F32))
        ao_sb = E(nc.sbuf_tensor("ao_sb", [128, 2 * D], BF16))
        t_sb = E(nc.sbuf_tensor("t_sb", [128, 2 * D], F32))
        y_sb = E(nc.sbuf_tensor("y_sb", [128, 4 * D], BF16))
        silu_sb = E(nc.sbuf_tensor("silu_sb", [128, 2 * 512], BF16))
        ostg = E(nc.sbuf_tensor("ostg", [128, 3 * D], F32))
        stats = E(nc.sbuf_tensor("stats", [128, 92], F32))
        bn_sb = E(nc.sbuf_tensor("bn_sb", [128, 48], F32))
        bn2_sb = E(nc.sbuf_tensor("bn2_sb", [128, 12], F32))
        bqT = E(nc.sbuf_tensor("bqT", [128, 8], F32))
        bkT = E(nc.sbuf_tensor("bkT", [128, 8], F32))
        bvT = E(nc.sbuf_tensor("bvT", [128, 8], F32))
        b8 = E(nc.sbuf_tensor("b8", [8, 128], F32))
        bo_rowb = E(nc.sbuf_tensor("bo_rowb", [1, D], BF16))
        identb = E(nc.sbuf_tensor("identb", [128, 128], BF16))
        ident8 = E(nc.sbuf_tensor("ident8", [8, 8], F32))
        selh = E(nc.sbuf_tensor("selh", [128, 2], BF16))
        bch = E(nc.sbuf_tensor("bch", [2, 128], BF16))
        onesb = E(nc.sbuf_tensor("onesb", [1, 128], BF16))
        eps_c = E(nc.sbuf_tensor("eps_c", [128, 1], F32))

        def mk(name):
            c = Ctr(name)
            c.sem = E(nc.semaphore("sem_" + name))
            return c

        dxs = [mk(f"dx{j}") for j in range(8)]
        dq = [mk(f"dq{i}") for i in range(2)]
        dsg = [mk(f"dsg{i}") for i in range(4)]
        drd = [mk(f"drd{i}") for i in range(4)]
        dos = [mk(f"do{i}") for i in range(3)]
        dw = mk("dw")
        dwq = [mk(f"dwq{i}") for i in range(2)]
        dtf = [mk(f"dtf{i}") for i in range(2)]
        dst2 = [mk(f"dst2_{i}") for i in range(2)]
        dstc = mk("dst")
        tc = mk("t")
        vc = mk("v")
        ac = mk("a")
        gc = mk("g")

        P = Rec()
        M = {}

        P.op("vector", lambda v: v.memset(eps_c[:], EPS), vc)

        def emit_xload(un, j, war=None):
            b_l, hf = un // 2, un % 2
            n = 2 * hf + (j % 2)
            rr = j // 2
            if war is not None:
                P.wait("gpsimd", war[0], war[1])
            P.op("gpsimd", lambda g, b=b_l, n=n, rr=rr, j=j: g.dma_start(
                out=x_tm[:, D * j:D * j + D],
                in_=x_d[b, n * 512 + rr:(n + 1) * 512:4, :]), dxs[j], 16)

        def emit_ln1_stats(j):
            P.wait("vector", dxs[j])
            P.wait("vector", vc)
            xj = x_tm[:, D * j:D * j + D]
            P.op("vector", lambda v, xj=xj: v.bn_stats(bn2_sb[:, 0:6], xj[:, 0:512]))
            P.op("vector", lambda v, xj=xj: v.bn_stats(bn2_sb[:, 6:12], xj[:, 512:1024]), vc)
            P.wait("vector", vc)
            P.op("vector", lambda v, j=j: v.bn_aggr(
                stats[:, MV + 2 * j:MV + 2 * j + 2], bn2_sb[:, 0:12]), vc)

        def emit_batch_rstd(src_off, lo, hi, lt0=LT1, rs0=RST, nm0=NMR):
            w = hi - lo
            P.wait("scalar", vc)
            P.op("scalar", lambda s, so=src_off, lo=lo, hi=hi, w=w, lt0=lt0: s.activation(
                stats[:, lt0 + lo:lt0 + lo + w],
                stats[:, so + 2 * lo + 1:so + 2 * hi:2],
                AF.Ln, bias=eps_c[:, 0:1]), ac)
            P.wait("scalar", ac)
            P.op("scalar", lambda s, lo=lo, w=w, lt0=lt0, rs0=rs0: s.activation(
                stats[:, rs0 + lo:rs0 + lo + w], stats[:, lt0 + lo:lt0 + lo + w],
                AF.Exp, scale=-0.5), ac)
            P.wait("vector", ac)
            P.op("vector", lambda v, so=src_off, lo=lo, hi=hi, w=w, rs0=rs0, nm0=nm0: v.scalar_tensor_tensor(
                stats[:, nm0 + lo:nm0 + lo + w], stats[:, so + 2 * lo:so + 2 * hi:2],
                -1.0, stats[:, rs0 + lo:rs0 + lo + w], ALU.mult, ALU.mult), vc)
            return vc.n

        def emit_xhat_tile(j, ptr_ps, rst_ready_v, war_t, war_v=0):
            sl = j % 2
            po = 1024 * sl
            P.wait("scalar", vc, rst_ready_v)
            P.wait("scalar", tc, M.get(("xh_slot", sl), 0))
            P.op("scalar", lambda s, j=j, sl=sl: s.activation(
                xh_tm[:, D * sl:D * sl + D], x_tm[:, D * j:D * j + D],
                AF.Identity, bias=stats[:, NMR + j:NMR + j + 1],
                scale=stats[:, RST + j:RST + j + 1]), ac)
            P.wait("tensor", ac)
            P.wait("tensor", vc, M.get(("ptr_evac", sl), 0))
            if war_v:
                P.wait("tensor", vc, war_v)
            for kcc in range(KC):
                P.op("tensor", lambda t, sl=sl, kcc=kcc, ps=ptr_ps, po=po: t.transpose(
                    ps[:, po + 128 * kcc:po + 128 * kcc + 128],
                    xh_tm[:, D * sl + 128 * kcc:D * sl + 128 * kcc + 128],
                    identb[:]), tc if kcc == KC - 1 else None)
            M[("xh_slot", sl)] = tc.n
            P.wait("vector", tc)
            P.wait("vector", tc, war_t)
            P.op("vector", lambda v, j=j, ps=ptr_ps, po=po: v.tensor_copy(
                xhB[:].rearrange("p (c x) -> p c x", x=1024)[:, :, 128 * j:128 * j + 128],
                ps[:, po:po + 1024].rearrange("p (c x) -> p c x", x=128)), vc)
            M[("ptr_evac", sl)] = vc.n
            M[("xhtile", j)] = vc.n

        def emit_store(u, j, js4):
            rr, nrel = j // 2, j % 2
            b_l, n = u // 2, 2 * (u % 2) + nrel
            P.wait("scalar", vc, M[("resid4", j)])
            P.op("scalar", lambda s, b=b_l, n=n, rr=rr, js4=js4: s.dma_start(
                out=out_d[b, n * 512 + rr:(n + 1) * 512:4, :],
                in_=ostg[:, D * js4:D * js4 + D]), dos[js4], 16)
            M[("ostore", u * 8 + j)] = dos[js4].n
            ost_war[js4] = (dos[js4], dos[js4].n)

        def emit_grp_tail(u, grp):
            nu = n_units
            for ji in range(4):
                P.wait("vector", vc, M[("yevac", ji)])
                P.op("vector", lambda v, ji=ji: v.bn_stats(
                    bn_sb[:, 12 * ji:12 * ji + 6], y_sb[:, D * ji:D * ji + 512]))
                P.op("vector", lambda v, ji=ji: v.bn_stats(
                    bn_sb[:, 12 * ji + 6:12 * ji + 12],
                    y_sb[:, D * ji + 512:D * ji + 1024]), vc)
            P.wait("vector", vc)
            P.wait("vector", ac, M.get(("ln4tail",), 0))
            for ji in range(4):
                P.op("vector", lambda v, ji=ji: v.bn_aggr(
                    stats[:, MV4 + 2 * ji:MV4 + 2 * ji + 2],
                    bn_sb[:, 12 * ji:12 * ji + 12]), vc)
            rv = emit_batch_rstd(MV4, 0, 4, lt0=LT4, rs0=RS4, nm0=NM4)
            for ji in range(4):
                j = 4 * grp + ji
                jsl = j % 2
                js4 = j % 3
                P.wait("scalar", vc, rv)
                P.wait("scalar", vc, M.get(("r4t", jsl), 0))
                P.wait("scalar", vc, M.get(("resid2", 6 + jsl), 0))
                P.op("scalar", lambda s, ji=ji, jsl=jsl: s.activation(
                    t_sb[:, D * jsl:D * jsl + D], y_sb[:, D * ji:D * ji + D],
                    AF.Identity, bias=stats[:, NM4 + ji:NM4 + ji + 1],
                    scale=stats[:, RS4 + ji:RS4 + ji + 1]), ac)
                M[("ln4apply", j)] = ac.n
                P.wait("vector", ac, M[("ln4apply", j)])
                P.wait("vector", vc, M[("resid2", j)])
                if js4 in ost_war:
                    P.wait("vector", ost_war[js4][0], ost_war[js4][1])
                P.op("vector", lambda v, j=j, jsl=jsl, js4=js4: v.tensor_tensor(
                    ostg[:, D * js4:D * js4 + D], t_sb[:, D * jsl:D * jsl + D],
                    x_tm[:, D * j:D * j + D], ALU.add), vc)
                M[("resid4", j)] = vc.n
                M[("r4t", jsl)] = vc.n
                if u + 1 < nu:
                    emit_xload(u + 1, j, (vc, M[("resid4", j)]))
                if ji > 0:
                    emit_store(u, j - 1, (j - 1) % 3)
            M[("ln4tail",)] = ac.n
            emit_store(u, 4 * grp + 3, (4 * grp + 3) % 3)
            if u + 1 < nu and grp == 0:
                for ji in range(4):
                    emit_ln1_stats(ji)
                M[("rst1a", u + 1)] = emit_batch_rstd(MV, 0, 4)

        P.op("gpsimd", lambda g: g.dma_start(out=identb[:], in_=c_identb[:, :]), dw, 16)
        P.op("gpsimd", lambda g: g.dma_start(out=ident8[:], in_=c_ident8[:, :]), dw, 16)
        P.op("gpsimd", lambda g: g.dma_start(out=selh[:], in_=c_selh[:, :]), dw, 16)
        P.op("gpsimd", lambda g: g.dma_start(out=bch[:], in_=c_bch[:, :]), dw, 16)
        P.op("gpsimd", lambda g: g.dma_start(out=onesb[:], in_=c_onesb[:, :]), dw, 16)
        P.op("gpsimd", lambda g: g.dma_start(
            out=bo_rowb[:], in_=bo_d.rearrange("(a d) -> a d", a=1)), dw, 16)
        for j in range(8):
            emit_xload(0, j)
            emit_ln1_stats(j)
        M[("rst1a", 0)] = emit_batch_rstd(MV, 0, 4)
        M[("rst1b", 0)] = emit_batch_rstd(MV, 4, 8)

        with nc.psum_tensor("p_b8", [128, 8], F32) as p_b8:
            for bias_src, bias_dst in ((bq_d, bqT), (bk_d, bkT), (bv_d, bvT)):
                P.op("gpsimd", lambda g, s=bias_src: g.dma_start(
                    out=b8[:], in_=s.rearrange("(g k) -> g k", k=128)), dw, 16)
                P.wait("tensor", dw)
                P.wait("tensor", vc)
                P.op("tensor", lambda t: t.transpose(p_b8[:], b8[:], ident8[:]), tc)
                P.wait("vector", tc)
                P.op("vector", lambda v, bdst=bias_dst: v.tensor_copy(bdst[:], p_b8[:]), vc)
                P.wait("gpsimd", tc)
            M[("tfE", -2)] = vc.n
            M[("tfE", -1)] = vc.n

        for rr in range(4):
            si = rr % 2
            slot = si * 8192
            if rr >= 2:
                P.wait("gpsimd", dstc)
            for wi, wsrc in enumerate((wq_d, wk_d, wv_d)):
                P.op("gpsimd", lambda g, s=wsrc, rr=rr, slot=slot, wi=wi: g.dma_start(
                    out=qkvo[:, slot + 2048 * wi: slot + 2048 * wi + 2048]
                        .rearrange("p (c o) -> p c o", o=256),
                    in_=s.rearrange("(c p) o -> p c o", p=128)[:, :, 256 * rr:256 * rr + 256]),
                    dwq[si], 16)
            P.op("gpsimd", lambda g, rr=rr, slot=slot: g.dma_start(
                out=qkvo[:, slot + 6144: slot + 8192]
                    .rearrange("p (c o) -> p c o", o=1024),
                in_=wo_d.rearrange("(c p) o -> p c o", p=128)[:, 2 * rr:2 * rr + 2, :]),
                dwq[si], 16)
            P.wait("sync", dwq[si])
            P.op("sync", lambda sy, rr=rr, slot=slot: sy.dma_start(
                out=scr_q[rr], in_=qkvo[:, slot:slot + 8192]), dstc, 16)
            M[("scrq", rr)] = 32 * ((rr // 2) + 1)

        with nc.psum_tensor("p_tr0", [128, D], BF16) as p_tr0, \
             nc.psum_tensor("p_tr1", [128, D], BF16) as p_tr1:
            p_tr = (p_tr0, p_tr1)
            for wsi, (w_src, w_scr, w_key) in enumerate(
                    ((wg_d, scr_g, "scrg"), (wu_d, scr_u, "scru"))):
                for hc in range(HCN):
                    gi = wsi * HCN + hc
                    sl2 = gi % 2
                    nat = xhB[:, sl2 * D:sl2 * D + D]
                    stg = xhB[:, 2048 + sl2 * D: 2048 + sl2 * D + D]
                    ps = p_tr[sl2]
                    P.wait("gpsimd", tc, M.get(("tfT", gi - 2), 0))
                    P.op("gpsimd", lambda g, s=w_src, hc=hc, nat=nat, sl2=sl2: g.dma_start(
                        out=nat, in_=s[128 * hc:128 * hc + 128, :]), dtf[sl2], 16)
                    P.wait("tensor", dtf[sl2])
                    P.wait("tensor", vc, M.get(("tfE", gi - 2), 0))
                    for kcc in range(KC):
                        P.op("tensor",
                             lambda t, ps=ps, kcc=kcc, nat=nat: t.transpose(
                                 ps[:, 128 * kcc:128 * kcc + 128],
                                 nat[:, 128 * kcc:128 * kcc + 128], identb[:]),
                             tc if kcc == KC - 1 else None)
                    M[("tfT", gi)] = tc.n
                    P.wait("vector", tc, M[("tfT", gi)])
                    P.wait("vector", dst2[sl2])
                    P.op("vector", lambda v, ps=ps, stg=stg: v.tensor_copy(stg, ps[:]), vc)
                    M[("tfE", gi)] = vc.n
                    P.wait("sync", vc, M[("tfE", gi)])
                    P.op("sync", lambda sy, scr=w_scr, hc=hc, stg=stg: sy.dma_start(
                        out=scr[hc], in_=stg), dst2[sl2], 16)
                    M[(w_key, hc)] = (dst2[sl2], dst2[sl2].n)
            for dc in range(KC):
                sl2 = dc % 2
                nat = h_sb[:, sl2 * HID:sl2 * HID + HID]
                stg = h_sb[:, 2 * HID + sl2 * HID: 2 * HID + sl2 * HID + HID]
                if dc < 2:
                    P.wait("gpsimd", tc, M.get(("tfT", 2 * HCN - 2 + dc), 0))
                P.wait("gpsimd", tc, M.get(("wdT", dc - 2), 0))
                P.op("gpsimd", lambda g, dc=dc, nat=nat: g.dma_start(
                    out=nat, in_=wd_d[128 * dc:128 * dc + 128, :]), dtf[sl2], 16)
                P.wait("tensor", dtf[sl2])
                for r in range(3):
                    ps = p_tr[r % 2]
                    P.wait("tensor", vc)
                    for i in range(8):
                        hc = 8 * r + i
                        P.op("tensor",
                             lambda t, ps=ps, i=i, hc=hc, nat=nat: t.transpose(
                                 ps[:, 128 * i:128 * i + 128],
                                 nat[:, 128 * hc:128 * hc + 128], identb[:]),
                             tc if i == 7 else None)
                    P.wait("vector", tc)
                    P.wait("vector", dst2[sl2])
                    P.op("vector", lambda v, ps=ps, stg=stg, r=r: v.tensor_copy(
                        stg[:, D * r:D * r + D], ps[:]), vc)
                M[("wdT", dc)] = tc.n
                P.wait("sync", vc)
                P.op("sync", lambda sy, dc=dc, stg=stg: sy.dma_start(
                    out=scr_d[:, :, 128 * dc:128 * dc + 128].rearrange("h p d -> p h d"),
                    in_=stg.rearrange("p (h d) -> p h d", d=128)), dst2[sl2], 16)

        startup_dst = dstc.n
        startup_v = vc.n
        ost_war = {}
        tail_pending = [None]
        M[("ptr_evac", 0)] = vc.n
        M[("ptr_evac", 1)] = vc.n

        for u in range(n_units):
            b_local, half = u // 2, u % 2
            nbase = 2 * half

            with nc.psum_tensor(f"ptr{u}", [128, 2048], BF16) as ptrU, \
                 contextlib.ExitStack() as pctx:
                pq = pctx.enter_context(nc.psum_tensor(f"pq{u}", [128, 512], F32))
                pkv = pctx.enter_context(nc.psum_tensor(f"pkv{u}", [128, 512], F32))
                psc = pctx.enter_context(nc.psum_tensor(f"psc{u}", [2, 512], F32))
                ppb = pctx.enter_context(nc.psum_tensor(f"ppb{u}", [128, 512], F32))
                pao = pctx.enter_context(nc.psum_tensor(f"pao{u}", [128, D], F32))

                def load_wqk(rr, u=u):
                    si = (u * 4 + rr) % 2
                    slot = si * 8192
                    if u * 4 + rr >= 2:
                        P.wait("sync", tc, M[("qkv_mm_done", u * 4 + rr - 2)])
                    else:
                        P.wait("sync", dstc, M[("scrq", 3)])
                    P.op("sync", lambda sy, rr=rr, slot=slot: sy.dma_start(
                        out=qkvo[:, slot:slot + 4096], in_=scr_q[rr][:, 0:4096]),
                        dq[si], 16)

                def load_wv(rr, u=u):
                    si = (u * 4 + rr) % 2
                    slot = si * 8192
                    if u * 4 + rr >= 2:
                        P.wait("sync", tc, M[("vmm", u * 4 + rr - 2)])
                    else:
                        P.wait("sync", dstc, M[("scrq", 3)])
                    P.op("sync", lambda sy, rr=rr, slot=slot: sy.dma_start(
                        out=qkvo[:, slot + 4096:slot + 6144], in_=scr_q[rr][:, 4096:6144]),
                        dq[si], 16)

                def load_wo_part(rr, u=u):
                    si = (u * 4 + rr) % 2
                    slot = si * 8192
                    if u * 4 + rr >= 2:
                        P.wait("sync", tc, M[("wo_mm_done", u * 4 + rr - 2)])
                    else:
                        P.wait("sync", dstc, M[("scrq", 3)])
                    P.op("sync", lambda sy, rr=rr, slot=slot: sy.dma_start(
                        out=qkvo[:, slot + 6144:slot + 8192], in_=scr_q[rr][:, 6144:8192]),
                        dq[si], 16)

                def emit_one_proj(rr, wi, dst_ap, u=u):
                    si = (u * 4 + rr) % 2
                    gslot = si * 8192
                    for ch in range(2):
                        for kcc in range(KC):
                            P.op("tensor", lambda t, d2=dst_ap, wi=wi, ch=ch, kcc=kcc,
                                 gslot=gslot, rr=rr: t.matmul(
                                     d2[:, 256 * ch:256 * ch + 256],
                                     qkvo[:, gslot + 2048 * wi + 256 * kcc + 128 * ch:
                                          gslot + 2048 * wi + 256 * kcc + 128 * ch + 128],
                                     xhB[:, 1024 * kcc + 256 * rr:
                                          1024 * kcc + 256 * rr + 256],
                                     start=(ch == 0 and kcc == 0),
                                     stop=(ch == 1 and kcc == KC - 1),
                                     skip_group_check=True),
                                 tc if (ch == 1 and kcc == KC - 1) else None)
                    M[("mm", "qkv"[wi], rr)] = tc.n

                def emit_qk_mm(rr, u=u, pq=pq, pkv=pkv):
                    si = (u * 4 + rr) % 2
                    P.wait("tensor", vc, M[("xhtile", 2 * rr + 1)])
                    P.wait("tensor", dq[si])
                    P.wait("tensor", ac, M.get(("evQ", u * 4 + rr - 1), 0))
                    P.wait("tensor", ac, M.get(("evV", u * 4 + rr - 1), 0))
                    emit_one_proj(rr, 0, pq[:])
                    emit_one_proj(rr, 1, pkv[:])
                    M[("qkv_mm_done", u * 4 + rr)] = tc.n

                load_wqk(0)
                load_wv(0)
                load_wo_part(0)
                load_wqk(1)
                load_wv(1)
                load_wo_part(1)
                b2war = M.get(("yevacG1", u - 1), 0)
                if u == 0:
                    P.wait("vector", dst2[0])
                    P.wait("vector", dst2[1])
                for j in (0, 1):
                    emit_xhat_tile(j, ptrU, M[("rst1a", u)], M.get(("b1_done", u - 1), 0), b2war)
                emit_qk_mm(0)
                load_wqk(2)
                for j in (2, 3):
                    emit_xhat_tile(j, ptrU, M[("rst1a", u)], M.get(("b1_done", u - 1), 0), b2war)
                if tail_pending[0] is not None:
                    tail_pending[0]()
                    tail_pending[0] = None
                P.wait("scalar", tc, M[("mm", "q", 0)])
                P.wait("scalar", gc, M.get(("qkprod", -1), 0))
                for ch in range(2):
                    P.op("scalar", lambda s, ch=ch, pq=pq: s.activation(
                        q_sb[:, 256 * ch:256 * ch + 256], pq[:, 256 * ch:256 * ch + 256],
                        AF.Identity, bias=bqT[:, ch:ch + 1]), ac)
                M[("evQ", u * 4)] = ac.n
                P.wait("scalar", tc, M[("mm", "k", 0)])
                for ch in range(2):
                    P.op("scalar", lambda s, ch=ch, pkv=pkv: s.activation(
                        k_sb[:, 256 * ch:256 * ch + 256], pkv[:, 256 * ch:256 * ch + 256],
                        AF.Identity, bias=bkT[:, ch:ch + 1]), ac)
                M[("evK", u * 4)] = ac.n
                if u > 0:
                    for j in (4, 5, 6, 7):
                        emit_ln1_stats(j)
                    M[("rst1b", u)] = emit_batch_rstd(MV, 4, 8)

                def emit_resid2(j, u=u):
                    jsl = j % 2
                    P.wait("vector", ac, M[("ln2apply", j)])
                    P.op("vector", lambda v, j=j, jsl=jsl: v.tensor_tensor(
                        x_tm[:, D * j:D * j + D], x_tm[:, D * j:D * j + D],
                        t_sb[:, D * jsl:D * jsl + D], ALU.add), vc)
                    M[("resid2", j)] = vc.n
                    P.wait("vector", vc)
                    xj = x_tm[:, D * j:D * j + D]
                    P.op("vector", lambda v, xj=xj: v.bn_stats(bn2_sb[:, 0:6], xj[:, 0:512]))
                    P.op("vector", lambda v, xj=xj: v.bn_stats(bn2_sb[:, 6:12], xj[:, 512:1024]), vc)
                    P.wait("vector", vc)
                    P.op("vector", lambda v, j=j: v.bn_aggr(
                        stats[:, MV3 + 2 * j:MV3 + 2 * j + 2], bn2_sb[:, 0:12]), vc)
                    if j == 3:
                        M[("rst3a", u)] = emit_batch_rstd(MV3, 0, 4)
                    elif j == 5:
                        M[("rst3b1", u)] = emit_batch_rstd(MV3, 4, 6)
                    elif j == 7:
                        M[("rst3b2", u)] = emit_batch_rstd(MV3, 6, 8)

                for rr in range(4):
                    si = (u * 4 + rr) % 2
                    gslot = si * 8192
                    vsl = rr % 2
                    P.wait("tensor", ac, M[("evK", u * 4 + rr)])
                    P.wait("tensor", dq[si])
                    emit_one_proj(rr, 2, pkv[:])
                    M[("vmm", u * 4 + rr)] = tc.n
                    if rr + 2 < 4:
                        load_wv(rr + 2)
                    P.wait("scalar", tc, M[("mm", "v", rr)])
                    P.wait("scalar", vc, M.get(("Amul", rr - 2), 0))
                    for ch in range(2):
                        P.op("scalar", lambda s, ch=ch, rr=rr, vsl=vsl, pkv=pkv: s.activation(
                            v_sb[:, 512 * vsl + 256 * ch:512 * vsl + 256 * ch + 256],
                            pkv[:, 256 * ch:256 * ch + 256],
                            AF.Identity, bias=bvT[:, 2 * rr + ch:2 * rr + ch + 1]), ac)
                    M[("evV", u * 4 + rr)] = ac.n

                    P.wait("gpsimd", ac, M[("evK", u * 4 + rr)])
                    P.wait("gpsimd", tc, M.get(("sc", rr - 2), 0))
                    P.op("gpsimd", lambda g, vsl=vsl: g.tensor_mul(
                        qk_bf[:, 512 * vsl:512 * vsl + 512], q_sb[:], k_sb[:]), gc)
                    M[("qkprod", rr)] = gc.n

                    P.wait("tensor", gc, M[("qkprod", rr)])
                    P.wait("tensor", ac, M.get(("exps", rr - 1), 0))
                    for ch in range(2):
                        P.op("tensor", lambda t, ch=ch, vsl=vsl, psc=psc: t.matmul(
                            psc[:, 256 * ch:256 * ch + 256], selh[:],
                            qk_bf[:, 512 * vsl + 256 * ch:512 * vsl + 256 * ch + 256],
                            start=(ch == 0), stop=(ch == 1), skip_group_check=True),
                            tc if ch == 1 else None)
                    M[("sc", rr)] = tc.n

                    so = 16 * vsl
                    P.wait("vector", tc, M[("sc", rr)])
                    P.wait("vector", ac, M.get(("exps", rr - 2), 0))
                    P.op("vector", lambda v, so=so, psc=psc: v.reduce_max(
                        smx[:, so:so + 4], psc[:].rearrange("p (b s) -> p b s", s=128),
                        axis=AX.X), vc)
                    P.wait("vector", vc)
                    P.op("vector", lambda v, so=so: v.tensor_scalar(
                        smx[:, so + 4:so + 8], smx[:, so:so + 4], -SCL, None, ALU.mult), vc)
                    P.wait("scalar", vc)
                    P.wait("scalar", tc, M.get(("pb", rr - 2), 0))
                    for c in range(4):
                        P.op("scalar", lambda s, c=c, so=so, vsl=vsl, psc=psc: s.activation(
                            p_sb[:, 512 * vsl + 128 * c:512 * vsl + 128 * c + 128],
                            psc[:, 128 * c:128 * c + 128],
                            AF.Exp, scale=SCL, bias=smx[:, so + 4 + c:so + 5 + c],
                            accum_out=smx[:, so + 8 + c:so + 9 + c]), ac)
                    M[("exps", rr)] = ac.n
                    if rr + 1 < 4:
                        emit_qk_mm(rr + 1)
                        P.wait("scalar", tc, M[("mm", "q", rr + 1)])
                        P.wait("scalar", gc, M.get(("qkprod", rr), 0))
                        for ch in range(2):
                            P.op("scalar", lambda s, ch=ch, rr2=rr + 1, pq=pq: s.activation(
                                q_sb[:, 256 * ch:256 * ch + 256], pq[:, 256 * ch:256 * ch + 256],
                                AF.Identity, bias=bqT[:, 2 * rr2 + ch:2 * rr2 + ch + 1]), ac)
                        M[("evQ", u * 4 + rr + 1)] = ac.n
                        P.wait("scalar", tc, M[("mm", "k", rr + 1)])
                        for ch in range(2):
                            P.op("scalar", lambda s, ch=ch, rr2=rr + 1, pkv=pkv: s.activation(
                                k_sb[:, 256 * ch:256 * ch + 256], pkv[:, 256 * ch:256 * ch + 256],
                                AF.Identity, bias=bkT[:, 2 * rr2 + ch:2 * rr2 + ch + 1]), ac)
                        M[("evK", u * 4 + rr + 1)] = ac.n
                        if rr < 2:
                            b2war = M.get(("yevacG1", u - 1), 0)
                            for j in (4 + 2 * rr, 5 + 2 * rr):
                                emit_xhat_tile(j, ptrU, M[("rst1b", u)],
                                               M.get(("b1_done", u - 1), 0), b2war)
                        if rr + 3 < 4:
                            load_wqk(rr + 3)
                    if rr == 3:
                        for j3 in (0, 1, 2, 3):
                            emit_xhat_tile(j3, ptrU, M[("rst3a", u)],
                                           M[("qkv_mm_done", u * 4 + j3 // 2)])
                        M[("xh3nt", 0)] = vc.n

                    P.wait("vector", ac, M[("exps", rr)])
                    P.op("vector", lambda v, so=so: v.reciprocal(
                        smx[:, so + 12:so + 16], smx[:, so + 8:so + 12]), vc)
                    P.wait("vector", vc)
                    for c in range(4):
                        P.op("vector", lambda v, c=c, so=so, vsl=vsl: v.tensor_scalar(
                            p_sb[:, 512 * vsl + 128 * c:512 * vsl + 128 * c + 128],
                            p_sb[:, 512 * vsl + 128 * c:512 * vsl + 128 * c + 128],
                            smx[:, so + 12 + c:so + 13 + c], None, ALU.mult),
                            vc if c == 3 else None)
                    pnorm_v = vc.n

                    P.wait("tensor", vc, pnorm_v)
                    P.wait("tensor", vc, M.get(("Amul", rr - 1), 0))
                    for ch in range(2):
                        P.op("tensor", lambda t, ch=ch, vsl=vsl, ppb=ppb: t.matmul(
                            ppb[:, 256 * ch:256 * ch + 256], bch[:],
                            p_sb[:, 512 * vsl + 256 * ch:512 * vsl + 256 * ch + 256],
                            start=(ch == 0), stop=(ch == 1), skip_group_check=True),
                            tc if ch == 1 else None)
                    M[("pb", rr)] = tc.n

                    P.wait("vector", tc, M[("pb", rr)])
                    P.wait("vector", ac, M[("evV", u * 4 + rr)])
                    P.wait("vector", tc, M.get(("wo_mm_done", u * 4 + rr - 2), 0))
                    P.op("vector", lambda v, vsl=vsl, ppb=ppb: v.tensor_mul(
                        a_bf[:, 512 * vsl:512 * vsl + 512], ppb[:],
                        v_sb[:, 512 * vsl:512 * vsl + 512]), vc)
                    M[("Amul", rr)] = vc.n

                    for s2 in range(2):
                        j = 2 * rr + s2
                        jsl = j % 2
                        mvp, ltp, rsp, nmp = MV2P(jsl), LT2P(jsl), RS2P(jsl), NM2P(jsl)
                        P.wait("tensor", vc, M[("Amul", rr)])
                        for hf in range(2):
                            P.wait("tensor", ac, M.get(("aoc", 2 * rr + s2 - 1, hf), 0))
                            P.op("tensor", lambda t, hf=hf, pao=pao: t.matmul(
                                pao[:, 512 * hf:512 * hf + 512], onesb[:],
                                bo_rowb[:, 512 * hf:512 * hf + 512],
                                start=True, stop=False, skip_group_check=True), None)
                            for ch in range(2):
                                P.op("tensor", lambda t, hf=hf, ch=ch, s2=s2, vsl=vsl,
                                     gslot=gslot, pao=pao: t.matmul(
                                         pao[:, 512 * hf:512 * hf + 512],
                                         a_bf[:, 512 * vsl + 256 * ch + 128 * s2:
                                              512 * vsl + 256 * ch + 128 * s2 + 128],
                                         qkvo[:, gslot + 6144 + 1024 * ch + 512 * hf:
                                              gslot + 6144 + 1024 * ch + 512 * hf + 512],
                                         start=False, stop=(ch == 1),
                                         skip_group_check=True),
                                     tc if ch == 1 else None)
                            M[("wo_h", j, hf)] = tc.n
                        wo_t = tc.n
                        if s2 == 1:
                            M[("wo_mm_done", u * 4 + rr)] = wo_t

                        P.wait("scalar", vc, M.get(("aostats", j - 2), 0))
                        for hf in range(2):
                            P.wait("scalar", tc, M[("wo_h", j, hf)])
                            P.op("scalar", lambda s, jsl=jsl, hf=hf, pao=pao: s.activation(
                                ao_sb[:, D * jsl + 512 * hf:D * jsl + 512 * hf + 512],
                                pao[:, 512 * hf:512 * hf + 512], AF.Copy), ac)
                            M[("aoc", 2 * rr + s2, hf)] = ac.n
                        M[("aocopy", 2 * rr + s2)] = ac.n
                        P.wait("vector", ac, M[("aocopy", 2 * rr + s2)])
                        P.wait("vector", vc)
                        P.op("vector", lambda v, jsl=jsl: v.bn_stats(
                            bn_sb[:, 12 * jsl:12 * jsl + 6], ao_sb[:, D * jsl:D * jsl + 512]))
                        P.op("vector", lambda v, jsl=jsl: v.bn_stats(
                            bn_sb[:, 12 * jsl + 6:12 * jsl + 12],
                            ao_sb[:, D * jsl + 512:D * jsl + 1024]), vc)
                        M[("aostats", j)] = vc.n
                        P.wait("vector", vc)
                        P.wait("vector", ac, M.get(("lnexp2", j - 2), 0))
                        P.op("vector", lambda v, mvp=mvp, jsl=jsl: v.bn_aggr(
                            stats[:, mvp:mvp + 2], bn_sb[:, 12 * jsl:12 * jsl + 12]), vc)
                        P.wait("scalar", vc)
                        P.op("scalar", lambda s, mvp=mvp, ltp=ltp: s.activation(
                            stats[:, ltp:ltp + 1], stats[:, mvp + 1:mvp + 2],
                            AF.Ln, bias=eps_c[:, 0:1]), ac)
                        P.wait("scalar", ac)
                        P.op("scalar", lambda s, ltp=ltp, rsp=rsp: s.activation(
                            stats[:, rsp:rsp + 1], stats[:, ltp:ltp + 1], AF.Exp, scale=-0.5), ac)
                        M[("lnexp2", j)] = ac.n
                        P.wait("vector", ac)
                        P.wait("vector", ac, M.get(("ln2apply", j - 2), 0))
                        P.op("vector", lambda v, mvp=mvp, rsp=rsp, nmp=nmp: v.scalar_tensor_tensor(
                            stats[:, nmp:nmp + 1], stats[:, mvp:mvp + 1], -1.0,
                            stats[:, rsp:rsp + 1], ALU.mult, ALU.mult), vc)
                        P.wait("scalar", vc)
                        P.wait("scalar", vc, M.get(("resid2", j - 2), 0))
                        P.wait("scalar", vc, M.get(("r4t", jsl), 0))
                        P.op("scalar", lambda s, jsl=jsl, rsp=rsp, nmp=nmp: s.activation(
                            t_sb[:, D * jsl:D * jsl + D], ao_sb[:, D * jsl:D * jsl + D],
                            AF.Identity, bias=stats[:, nmp:nmp + 1],
                            scale=stats[:, rsp:rsp + 1]), ac)
                        M[("ln2apply", j)] = ac.n
                        emit_resid2(j)

                    if rr + 2 < 4:
                        load_wo_part(rr + 2)

            jobs = [(0, 0), (1, 0), (0, 1), (1, 1)] + \
                   [(hc, nt) for hc in range(2, HCN) for nt in range(2)]
            with nc.psum_tensor(f"ptr_{u}", [128, 2048], BF16) as ptrU2, \
                 nc.psum_tensor(f"pg{u}", [128, 3 * 512], F32) as pg, \
                 nc.psum_tensor(f"pu{u}", [128, 3 * 512], F32) as pu:
                loaded = set()
                for ji, (hc, nt) in enumerate(jobs):
                    slot = hc % 4
                    if hc not in loaded:
                        loaded.add(hc)
                        if ("rgu", u, hc - 4) in M:
                            P.wait("sync", tc, M[("rgu", u, hc - 4)])
                        elif u > 0 and hc < 4:
                            P.wait("sync", tc, M[("ring_gu", HCN - 4 + hc)])
                        if u == 0:
                            P.wait("sync", *M[("scrg", hc)])
                            P.wait("sync", *M[("scru", hc)])
                        P.op("sync", lambda sy, hc=hc, slot=slot: sy.dma_start(
                            out=ring_g[:, D * slot:D * slot + D], in_=scr_g[hc]), dsg[slot], 16)
                        P.op("sync", lambda sy, hc=hc, slot=slot: sy.dma_start(
                            out=ring_u[:, D * slot:D * slot + D], in_=scr_u[hc]), dsg[slot], 16)

                    P.wait("tensor", dsg[slot])
                    gsl = ji % 3
                    ssl = ji % 2
                    P.wait("tensor", vc, M.get(("xh3nt", nt), 0))
                    P.wait("tensor", vc, M.get(("hmulB", u, ji - 3), 0))
                    for kcc in range(KC):
                        P.op("tensor", lambda t, slot=slot, gsl=gsl, nt=nt, kcc=kcc, pg=pg: t.matmul(
                            pg[:, 512 * gsl:512 * gsl + 512],
                            ring_g[:, D * slot + 128 * kcc:D * slot + 128 * kcc + 128],
                            xhB[:, 1024 * kcc + 512 * nt:1024 * kcc + 512 * nt + 512],
                            start=(kcc == 0), stop=(kcc == KC - 1)), None)
                    for kcc in range(KC):
                        P.op("tensor", lambda t, slot=slot, gsl=gsl, nt=nt, kcc=kcc, pu=pu: t.matmul(
                            pu[:, 512 * gsl:512 * gsl + 512],
                            ring_u[:, D * slot + 128 * kcc:D * slot + 128 * kcc + 128],
                            xhB[:, 1024 * kcc + 512 * nt:1024 * kcc + 512 * nt + 512],
                            start=(kcc == 0), stop=(kcc == KC - 1)),
                            tc if kcc == KC - 1 else None)
                    if nt == 1:
                        M[("rgu", u, hc)] = tc.n
                    gu_t = tc.n
                    P.wait("scalar", tc, gu_t)
                    P.wait("scalar", vc, M.get(("hmulB", u, ji - 2), 0))
                    P.op("scalar", lambda s, gsl=gsl, ssl=ssl, pg=pg: s.activation(
                        silu_sb[:, 512 * ssl:512 * ssl + 512],
                        pg[:, 512 * gsl:512 * gsl + 512], AF.Silu, scale=SDQ), ac)
                    silu_a = ac.n
                    P.wait("vector", ac, silu_a)
                    P.op("vector", lambda v, gsl=gsl, ssl=ssl, hc=hc, nt=nt, pu=pu: v.tensor_mul(
                        h_sb[:, D * hc + 512 * nt:D * hc + 512 * nt + 512],
                        silu_sb[:, 512 * ssl:512 * ssl + 512],
                        pu[:, 512 * gsl:512 * gsl + 512]), vc)
                    M[("hmulB", u, ji)] = vc.n
                    if ji == 0:
                        for j in (4, 5):
                            emit_xhat_tile(j, ptrU2, M[("rst3b1", u)],
                                           M[("qkv_mm_done", u * 4 + j // 2)])
                    elif ji == 1:
                        for j in (6, 7):
                            emit_xhat_tile(j, ptrU2, M[("rst3b2", u)],
                                           M[("qkv_mm_done", u * 4 + j // 2)])
                        M[("xh3nt", 1)] = vc.n
                for back in range(4):
                    M[("ring_gu", HCN - 4 + back)] = M[("rgu", u, HCN - 4 + back)]
                M[("b1_done", u)] = tc.n
                h_ready_v = vc.n

            with contextlib.ExitStack() as pctx:
                pd = [pctx.enter_context(nc.psum_tensor(f"pd{u}_{i}", [128, D], F32))
                      for i in range(4)]
                for grp in range(2):
                    for kcc in range(HCN):
                        slot = kcc % 4
                        if (u, grp, kcc - 4) in M:
                            P.wait("sync", tc, M[(u, grp, kcc - 4)])
                        elif grp == 1 and kcc < 4:
                            P.wait("sync", tc, M[(u, 0, HCN - 4 + kcc)])
                        elif u > 0 and grp == 0 and kcc < 4:
                            P.wait("sync", tc, M[("ring_d", HCN - 4 + kcc)])
                        elif u == 0 and grp == 0 and kcc < 4:
                            P.wait("sync", dst2[0])
                            P.wait("sync", dst2[1])
                        P.op("sync", lambda sy, kcc=kcc, slot=slot: sy.dma_start(
                            out=ring_d[:, D * slot:D * slot + D], in_=scr_d[kcc]), drd[slot], 16)
                        P.wait("tensor", drd[slot])
                        if kcc == 0:
                            P.wait("tensor", vc, h_ready_v)
                        for ji in range(4):
                            j = 4 * grp + ji
                            if kcc == 0 and grp == 1:
                                P.wait("tensor", vc, M[("yevac", ji)])
                            for hf in range(2):
                                P.op("tensor", lambda t, ji=ji, j=j, hf=hf, kcc=kcc, slot=slot, pd=pd: t.matmul(
                                    pd[ji][:, 512 * hf:512 * hf + 512],
                                    h_sb[:, D * kcc + 128 * j:D * kcc + 128 * j + 128],
                                    ring_d[:, D * slot + 512 * hf:D * slot + 512 * hf + 512],
                                    start=(kcc == 0), stop=(kcc == HCN - 1)),
                                    tc if (ji == 3 and hf == 1) else None)
                        M[(u, grp, kcc)] = tc.n
                        if grp == 1 and kcc >= HCN - 4:
                            M[("ring_d", kcc)] = tc.n
                    dn_done_t = tc.n
                    for ji in range(4):
                        P.wait("vector", tc, dn_done_t)
                        P.wait("vector", ac,
                               M.get(("ln4apply", (4 * grp - 4 + ji) % 8), 0))
                        P.op("vector", lambda v, ji=ji, pd=pd: v.tensor_copy(
                            y_sb[:, D * ji:D * ji + D], pd[ji][:]), vc)
                        M[("yevac", ji)] = vc.n
                        if grp == 1 and ji == 0:
                            M[("yevacG1", u)] = vc.n
                    if grp == 1:
                        M[("b2yevac", u)] = vc.n
                    if grp == 0:
                        emit_grp_tail(u, 0)
                    else:
                        tail_pending[0] = lambda u=u: emit_grp_tail(u, 1)

        if tail_pending[0] is not None:
            tail_pending[0]()
            tail_pending[0] = None
        for dsem in dos:
            P.wait("gpsimd", dsem, dsem.n)
            P.wait("sync", dsem, dsem.n)

        with nc.Block() as block:
            @block.tensor
            def _(eng):
                for fn in P.streams["tensor"]:
                    fn(eng)

            @block.vector
            def _(eng):
                for fn in P.streams["vector"]:
                    fn(eng)

            @block.scalar
            def _(eng):
                for fn in P.streams["scalar"]:
                    fn(eng)

            @block.sync
            def _(eng):
                for fn in P.streams["sync"]:
                    fn(eng)

            @block.gpsimd
            def _(eng):
                for fn in P.streams["gpsimd"]:
                    fn(eng)

    return nc


def _host_inputs(inputs):
    perm = np.concatenate([
        np.arange(64 * (rr + 4 * hj), 64 * (rr + 4 * hj) + 64)
        for rr in range(4) for hj in range(4)])
    selh = np.zeros((128, 2), dtype=BFNP)
    selh[0:64, 0] = 1
    selh[64:128, 1] = 1
    bch = np.zeros((2, 128), dtype=BFNP)
    bch[0, 0:64] = 1
    bch[1, 64:128] = 1
    return {
        "wq": np.ascontiguousarray(inputs["L0_attn_Wq"][:, perm]),
        "wk": np.ascontiguousarray(inputs["L0_attn_Wk"][:, perm]),
        "wv": np.ascontiguousarray(inputs["L0_attn_Wv"][:, perm]),
        "wo": np.ascontiguousarray(inputs["L0_attn_Wo"][perm, :]),
        "bq": np.ascontiguousarray(inputs["L0_attn_bq"][perm]),
        "bk": np.ascontiguousarray(inputs["L0_attn_bk"][perm]),
        "bv": np.ascontiguousarray(inputs["L0_attn_bv"][perm]),
        "bo": np.ascontiguousarray(inputs["L0_attn_bo"]),
        "wg": np.ascontiguousarray(inputs["L0_ffn_w_gate"]),
        "wu": np.ascontiguousarray(inputs["L0_ffn_w_up"]),
        "wd": np.ascontiguousarray(inputs["L0_ffn_w_down"]),
        "c_identb": np.eye(128, dtype=np.float32).astype(BFNP),
        "c_ident8": np.eye(8, dtype=np.float32),
        "c_selh": selh,
        "c_bch": bch,
        "c_onesb": np.ones((1, 128), dtype=BFNP),
    }


def kernel(**inputs):
    x = np.asarray(inputs["L0_attn_x"])
    common = _host_inputs({k: np.asarray(v) for k, v in inputs.items()})
    nc = build_nc(UNITS)
    in_maps = []
    for c in range(NCORES):
        m = dict(common)
        m["x"] = np.ascontiguousarray(x[4 * c:4 * c + 4])
        in_maps.append(m)
    res = run_bass_kernel_spmd(nc, in_maps, core_ids=list(range(NCORES)))
    return np.concatenate([res.results[c]["out"] for c in range(NCORES)], axis=0)


# version 18
# speedup vs baseline: 1.0131x; 1.0131x over previous
import contextlib

import numpy as np
import ml_dtypes

import concourse.bass as bass
import concourse.mybir as mybir
from concourse.bass_utils import run_bass_kernel_spmd

F32 = mybir.dt.float32
BF16 = mybir.dt.bfloat16
I32 = mybir.dt.int32
AF = mybir.ActivationFunctionType
ALU = mybir.AluOpType
AX = mybir.AxisListType
BFNP = ml_dtypes.bfloat16

D = 1024
HID = 3072
KC = 8
HCN = 24
EPS = 1e-5
SDQ = 1.0 / 127.0
SCL = 0.125
NCORES = 8
UNITS = 8
ENGS = ("tensor", "vector", "scalar", "sync", "gpsimd")

MV = 0
RST = 16
NMR = 24
MV3 = 48
LT1 = 64
LT4 = 72
RS4 = 76
NM4 = 80
MV4 = 84


def MV2P(p):
    return 32 + 2 * p


def LT2P(p):
    return 36 + p


def RS2P(p):
    return 38 + p


def NM2P(p):
    return 42 + p


class Ctr:
    def __init__(self, name):
        self.name = name
        self.n = 0
        self.sem = None


class Rec:
    def __init__(self):
        self.streams = {e: [] for e in ENGS}
        self.lastw = {}

    def op(self, e, f, inc=None, k=1):
        if inc is not None:
            inc.n += k

        def fn(eng, f=f, inc=inc, k=k):
            ins = f(eng)
            if inc is not None:
                ins.then_inc(inc.sem, k)

        self.streams[e].append(fn)
        return inc.n if inc is not None else None

    def wait(self, e, ctr, n=None):
        t = ctr.n if n is None else n
        if t <= 0 or t <= self.lastw.get((e, ctr.name), 0):
            return
        self.lastw[(e, ctr.name)] = t
        self.streams[e].append(lambda eng, c=ctr, t=t: eng.wait_ge(c.sem, t))


def build_nc(n_units=UNITS):
    nc = bass.Bass("TRN2", target_bir_lowering=False)

    x_d = nc.dram_tensor("x", [4, 2048, D], F32, kind="ExternalInput").ap()
    out_d = nc.dram_tensor("out", [4, 2048, D], F32, kind="ExternalOutput").ap()
    wq_d = nc.dram_tensor("wq", [D, D], F32, kind="ExternalInput").ap()
    wk_d = nc.dram_tensor("wk", [D, D], F32, kind="ExternalInput").ap()
    wv_d = nc.dram_tensor("wv", [D, D], F32, kind="ExternalInput").ap()
    wo_d = nc.dram_tensor("wo", [D, D], F32, kind="ExternalInput").ap()
    bq_d = nc.dram_tensor("bq", [D], F32, kind="ExternalInput").ap()
    bk_d = nc.dram_tensor("bk", [D], F32, kind="ExternalInput").ap()
    bv_d = nc.dram_tensor("bv", [D], F32, kind="ExternalInput").ap()
    bo_d = nc.dram_tensor("bo", [D], F32, kind="ExternalInput").ap()
    wg_d = nc.dram_tensor("wg", [HID, D], I32, kind="ExternalInput").ap()
    wu_d = nc.dram_tensor("wu", [HID, D], I32, kind="ExternalInput").ap()
    wd_d = nc.dram_tensor("wd", [D, HID], I32, kind="ExternalInput").ap()
    c_identb = nc.dram_tensor("c_identb", [128, 128], BF16, kind="ExternalInput").ap()
    c_ident8 = nc.dram_tensor("c_ident8", [8, 8], F32, kind="ExternalInput").ap()
    c_selh = nc.dram_tensor("c_selh", [128, 2], BF16, kind="ExternalInput").ap()
    c_bch = nc.dram_tensor("c_bch", [2, 128], BF16, kind="ExternalInput").ap()
    c_onesb = nc.dram_tensor("c_onesb", [1, 128], BF16, kind="ExternalInput").ap()
    scr_g = nc.dram_tensor("scr_g", [HCN, 128, D], BF16, kind="Internal").ap()
    scr_u = nc.dram_tensor("scr_u", [HCN, 128, D], BF16, kind="Internal").ap()
    scr_d = nc.dram_tensor("scr_d", [HCN, 128, D], BF16, kind="Internal").ap()
    scr_q = nc.dram_tensor("scr_q", [4, 128, 8192], BF16, kind="Internal").ap()

    with contextlib.ExitStack() as ctx:
        E = ctx.enter_context

        x_tm = E(nc.sbuf_tensor("x_tm", [128, 8 * D], F32))
        xh_tm = E(nc.sbuf_tensor("xh_tm", [128, 2 * D], BF16))
        xhB = E(nc.sbuf_tensor("xhB", [128, 8 * D], BF16))
        qkvo = E(nc.sbuf_tensor("qkvo", [128, 2 * 8192], BF16))
        ring_g = E(nc.sbuf_tensor("ring_g", [128, 4 * D], BF16))
        ring_u = E(nc.sbuf_tensor("ring_u", [128, 4 * D], BF16))
        ring_d = E(nc.sbuf_tensor("ring_d", [128, 4 * D], BF16))
        h_sb = E(nc.sbuf_tensor("h_sb", [128, HCN * D], BF16))
        q_sb = E(nc.sbuf_tensor("q_sb", [128, 512], BF16))
        k_sb = E(nc.sbuf_tensor("k_sb", [128, 512], BF16))
        v_sb = E(nc.sbuf_tensor("v_sb", [128, 2 * 512], BF16))
        qk_bf = E(nc.sbuf_tensor("qk_bf", [128, 2 * 512], BF16))
        a_bf = E(nc.sbuf_tensor("a_bf", [128, 2 * 512], BF16))
        p_sb = E(nc.sbuf_tensor("p_sb", [2, 2 * 512], BF16))
        smx = E(nc.sbuf_tensor("smx", [2, 32], F32))
        ao_sb = E(nc.sbuf_tensor("ao_sb", [128, 2 * D], BF16))
        t_sb = E(nc.sbuf_tensor("t_sb", [128, 2 * D], F32))
        y_sb = E(nc.sbuf_tensor("y_sb", [128, 4 * D], BF16))
        silu_sb = E(nc.sbuf_tensor("silu_sb", [128, 2 * 512], BF16))
        ostg_off = nc.sbuf_base
        ostg = E(nc.sbuf_tensor("ostg", [128, 3 * D], F32))
        tfi = nc.alloc_sbuf_tensor_at("tfi32", [128, 2 * D], I32, offset=ostg_off)
        stats = E(nc.sbuf_tensor("stats", [128, 92], F32))
        bn_sb = E(nc.sbuf_tensor("bn_sb", [128, 48], F32))
        bn2_sb = E(nc.sbuf_tensor("bn2_sb", [128, 12], F32))
        bqT = E(nc.sbuf_tensor("bqT", [128, 8], F32))
        bkT = E(nc.sbuf_tensor("bkT", [128, 8], F32))
        bvT = E(nc.sbuf_tensor("bvT", [128, 8], F32))
        b8 = E(nc.sbuf_tensor("b8", [8, 128], F32))
        bo_rowb = E(nc.sbuf_tensor("bo_rowb", [1, D], BF16))
        identb = E(nc.sbuf_tensor("identb", [128, 128], BF16))
        ident8 = E(nc.sbuf_tensor("ident8", [8, 8], F32))
        selh = E(nc.sbuf_tensor("selh", [128, 2], BF16))
        bch = E(nc.sbuf_tensor("bch", [2, 128], BF16))
        onesb = E(nc.sbuf_tensor("onesb", [1, 128], BF16))
        eps_c = E(nc.sbuf_tensor("eps_c", [128, 1], F32))

        def mk(name):
            c = Ctr(name)
            c.sem = E(nc.semaphore("sem_" + name))
            return c

        dxs = [mk(f"dx{j}") for j in range(8)]
        dq = [mk(f"dq{i}") for i in range(2)]
        dsg = [mk(f"dsg{i}") for i in range(4)]
        drd = [mk(f"drd{i}") for i in range(4)]
        dos = [mk(f"do{i}") for i in range(3)]
        dw = mk("dw")
        dwq = [mk(f"dwq{i}") for i in range(2)]
        dtf = [mk(f"dtf{i}") for i in range(2)]
        dst2 = [mk(f"dst2_{i}") for i in range(2)]
        dstc = mk("dst")
        tc = mk("t")
        vc = mk("v")
        ac = mk("a")
        gc = mk("g")

        P = Rec()
        M = {}

        P.op("vector", lambda v: v.memset(eps_c[:], EPS), vc)

        def emit_xload(un, j, war=None):
            b_l, hf = un // 2, un % 2
            n = 2 * hf + (j % 2)
            rr = j // 2
            if war is not None:
                P.wait("gpsimd", war[0], war[1])
            P.op("gpsimd", lambda g, b=b_l, n=n, rr=rr, j=j: g.dma_start(
                out=x_tm[:, D * j:D * j + D],
                in_=x_d[b, n * 512 + rr:(n + 1) * 512:4, :]), dxs[j], 16)

        def emit_ln1_stats(j):
            P.wait("vector", dxs[j])
            P.wait("vector", vc)
            xj = x_tm[:, D * j:D * j + D]
            P.op("vector", lambda v, xj=xj: v.bn_stats(bn2_sb[:, 0:6], xj[:, 0:512]))
            P.op("vector", lambda v, xj=xj: v.bn_stats(bn2_sb[:, 6:12], xj[:, 512:1024]), vc)
            P.wait("vector", vc)
            P.op("vector", lambda v, j=j: v.bn_aggr(
                stats[:, MV + 2 * j:MV + 2 * j + 2], bn2_sb[:, 0:12]), vc)

        def emit_batch_rstd(src_off, lo, hi, lt0=LT1, rs0=RST, nm0=NMR):
            w = hi - lo
            P.wait("scalar", vc)
            P.op("scalar", lambda s, so=src_off, lo=lo, hi=hi, w=w, lt0=lt0: s.activation(
                stats[:, lt0 + lo:lt0 + lo + w],
                stats[:, so + 2 * lo + 1:so + 2 * hi:2],
                AF.Ln, bias=eps_c[:, 0:1]), ac)
            P.wait("scalar", ac)
            P.op("scalar", lambda s, lo=lo, w=w, lt0=lt0, rs0=rs0: s.activation(
                stats[:, rs0 + lo:rs0 + lo + w], stats[:, lt0 + lo:lt0 + lo + w],
                AF.Exp, scale=-0.5), ac)
            P.wait("vector", ac)
            P.op("vector", lambda v, so=src_off, lo=lo, hi=hi, w=w, rs0=rs0, nm0=nm0: v.scalar_tensor_tensor(
                stats[:, nm0 + lo:nm0 + lo + w], stats[:, so + 2 * lo:so + 2 * hi:2],
                -1.0, stats[:, rs0 + lo:rs0 + lo + w], ALU.mult, ALU.mult), vc)
            return vc.n

        def emit_xhat_tile(j, ptr_ps, rst_ready_v, war_t, war_v=0):
            sl = j % 2
            po = 1024 * sl
            P.wait("scalar", vc, rst_ready_v)
            P.wait("scalar", tc, M.get(("xh_slot", sl), 0))
            P.op("scalar", lambda s, j=j, sl=sl: s.activation(
                xh_tm[:, D * sl:D * sl + D], x_tm[:, D * j:D * j + D],
                AF.Identity, bias=stats[:, NMR + j:NMR + j + 1],
                scale=stats[:, RST + j:RST + j + 1]), ac)
            P.wait("tensor", ac)
            P.wait("tensor", vc, M.get(("ptr_evac", sl), 0))
            if war_v:
                P.wait("tensor", vc, war_v)
            for kcc in range(KC):
                P.op("tensor", lambda t, sl=sl, kcc=kcc, ps=ptr_ps, po=po: t.transpose(
                    ps[:, po + 128 * kcc:po + 128 * kcc + 128],
                    xh_tm[:, D * sl + 128 * kcc:D * sl + 128 * kcc + 128],
                    identb[:]), tc if kcc == KC - 1 else None)
            M[("xh_slot", sl)] = tc.n
            P.wait("vector", tc)
            P.wait("vector", tc, war_t)
            P.op("vector", lambda v, j=j, ps=ptr_ps, po=po: v.tensor_copy(
                xhB[:].rearrange("p (c x) -> p c x", x=1024)[:, :, 128 * j:128 * j + 128],
                ps[:, po:po + 1024].rearrange("p (c x) -> p c x", x=128)), vc)
            M[("ptr_evac", sl)] = vc.n
            M[("xhtile", j)] = vc.n

        def emit_store(u, j, js4):
            rr, nrel = j // 2, j % 2
            b_l, n = u // 2, 2 * (u % 2) + nrel
            P.wait("scalar", vc, M[("resid4", j)])
            P.op("scalar", lambda s, b=b_l, n=n, rr=rr, js4=js4: s.dma_start(
                out=out_d[b, n * 512 + rr:(n + 1) * 512:4, :],
                in_=ostg[:, D * js4:D * js4 + D]), dos[js4], 16)
            M[("ostore", u * 8 + j)] = dos[js4].n
            ost_war[js4] = (dos[js4], dos[js4].n)

        def emit_grp_tail(u, grp):
            nu = n_units
            for ji in range(4):
                P.wait("vector", vc, M[("yevac", ji)])
                P.op("vector", lambda v, ji=ji: v.bn_stats(
                    bn_sb[:, 12 * ji:12 * ji + 6], y_sb[:, D * ji:D * ji + 512]))
                P.op("vector", lambda v, ji=ji: v.bn_stats(
                    bn_sb[:, 12 * ji + 6:12 * ji + 12],
                    y_sb[:, D * ji + 512:D * ji + 1024]), vc)
            P.wait("vector", vc)
            P.wait("vector", ac, M.get(("ln4tail",), 0))
            for ji in range(4):
                P.op("vector", lambda v, ji=ji: v.bn_aggr(
                    stats[:, MV4 + 2 * ji:MV4 + 2 * ji + 2],
                    bn_sb[:, 12 * ji:12 * ji + 12]), vc)
            rv = emit_batch_rstd(MV4, 0, 4, lt0=LT4, rs0=RS4, nm0=NM4)
            for ji in range(4):
                j = 4 * grp + ji
                jsl = j % 2
                js4 = j % 3
                P.wait("scalar", vc, rv)
                P.wait("scalar", vc, M.get(("r4t", jsl), 0))
                P.wait("scalar", vc, M.get(("resid2", 6 + jsl), 0))
                P.op("scalar", lambda s, ji=ji, jsl=jsl: s.activation(
                    t_sb[:, D * jsl:D * jsl + D], y_sb[:, D * ji:D * ji + D],
                    AF.Identity, bias=stats[:, NM4 + ji:NM4 + ji + 1],
                    scale=stats[:, RS4 + ji:RS4 + ji + 1]), ac)
                M[("ln4apply", j)] = ac.n
                P.wait("vector", ac, M[("ln4apply", j)])
                P.wait("vector", vc, M[("resid2", j)])
                if js4 in ost_war:
                    P.wait("vector", ost_war[js4][0], ost_war[js4][1])
                P.op("vector", lambda v, j=j, jsl=jsl, js4=js4: v.tensor_tensor(
                    ostg[:, D * js4:D * js4 + D], t_sb[:, D * jsl:D * jsl + D],
                    x_tm[:, D * j:D * j + D], ALU.add), vc)
                M[("resid4", j)] = vc.n
                M[("r4t", jsl)] = vc.n
                if u + 1 < nu:
                    emit_xload(u + 1, j, (vc, M[("resid4", j)]))
                if ji > 0:
                    emit_store(u, j - 1, (j - 1) % 3)
            M[("ln4tail",)] = ac.n
            emit_store(u, 4 * grp + 3, (4 * grp + 3) % 3)
            if u + 1 < nu and grp == 0:
                for ji in range(4):
                    emit_ln1_stats(ji)
                M[("rst1a", u + 1)] = emit_batch_rstd(MV, 0, 4)

        P.op("gpsimd", lambda g: g.dma_start(out=identb[:], in_=c_identb[:, :]), dw, 16)
        P.op("gpsimd", lambda g: g.dma_start(out=ident8[:], in_=c_ident8[:, :]), dw, 16)
        P.op("gpsimd", lambda g: g.dma_start(out=selh[:], in_=c_selh[:, :]), dw, 16)
        P.op("gpsimd", lambda g: g.dma_start(out=bch[:], in_=c_bch[:, :]), dw, 16)
        P.op("gpsimd", lambda g: g.dma_start(out=onesb[:], in_=c_onesb[:, :]), dw, 16)
        P.op("gpsimd", lambda g: g.dma_start(
            out=bo_rowb[:], in_=bo_d.rearrange("(a d) -> a d", a=1)), dw, 16)
        for j in range(8):
            emit_xload(0, j)
            emit_ln1_stats(j)
        M[("rst1a", 0)] = emit_batch_rstd(MV, 0, 4)
        M[("rst1b", 0)] = emit_batch_rstd(MV, 4, 8)

        with nc.psum_tensor("p_b8", [128, 8], F32) as p_b8:
            for bias_src, bias_dst in ((bq_d, bqT), (bk_d, bkT), (bv_d, bvT)):
                P.op("gpsimd", lambda g, s=bias_src: g.dma_start(
                    out=b8[:], in_=s.rearrange("(g k) -> g k", k=128)), dw, 16)
                P.wait("tensor", dw)
                P.wait("tensor", vc)
                P.op("tensor", lambda t: t.transpose(p_b8[:], b8[:], ident8[:]), tc)
                P.wait("vector", tc)
                P.op("vector", lambda v, bdst=bias_dst: v.tensor_copy(bdst[:], p_b8[:]), vc)
                P.wait("gpsimd", tc)
            M[("tfE", -2)] = vc.n
            M[("tfE", -1)] = vc.n

        scrq_stores = []
        for rr in range(4):
            si = rr % 2
            slot = si * 8192
            if rr >= 2:
                P.wait("gpsimd", dstc, 32)
            for wi, wsrc in enumerate((wq_d, wk_d, wv_d)):
                P.op("gpsimd", lambda g, s=wsrc, rr=rr, slot=slot, wi=wi: g.dma_start(
                    out=qkvo[:, slot + 2048 * wi: slot + 2048 * wi + 2048]
                        .rearrange("p (c o) -> p c o", o=256),
                    in_=s.rearrange("(c p) o -> p c o", p=128)[:, :, 256 * rr:256 * rr + 256]),
                    dwq[si], 16)
            P.op("gpsimd", lambda g, rr=rr, slot=slot: g.dma_start(
                out=qkvo[:, slot + 6144: slot + 8192]
                    .rearrange("p (c o) -> p c o", o=1024),
                in_=wo_d.rearrange("(c p) o -> p c o", p=128)[:, 2 * rr:2 * rr + 2, :]),
                dwq[si], 16)
            def _sq(rr=rr, slot=slot, si=si, dn=None):
                P.wait("scalar", dwq[si], dn)
                P.op("scalar", lambda s, rr=rr, slot=slot: s.dma_start(
                    out=scr_q[rr], in_=qkvo[:, slot:slot + 8192]), dstc, 16)
            _sq.__defaults__ = (rr, slot, si, dwq[si].n)
            scrq_stores.append(_sq)
            M[("scrq", rr)] = 32 * ((rr // 2) + 1)

        with nc.psum_tensor("p_tr0", [128, D], BF16) as p_tr0, \
             nc.psum_tensor("p_tr1", [128, D], BF16) as p_tr1:
            p_tr = (p_tr0, p_tr1)
            for wsi, (w_src, w_scr, w_key) in enumerate(
                    ((wg_d, scr_g, "scrg"), (wu_d, scr_u, "scru"))):
                for hc in range(HCN):
                    gi = wsi * HCN + hc
                    sl2 = gi % 2
                    nat = xhB[:, sl2 * D:sl2 * D + D]
                    stg = xhB[:, 2048 + sl2 * D: 2048 + sl2 * D + D]
                    ps = p_tr[sl2]
                    P.wait("sync", vc, M.get(("tfc", gi - 2), 0))
                    P.op("sync", lambda sy, s=w_src, hc=hc, sl2=sl2: sy.dma_start(
                        out=tfi[:, D * sl2:D * sl2 + D],
                        in_=s[128 * hc:128 * hc + 128, :]), dtf[sl2], 16)
                    P.wait("vector", dtf[sl2])
                    P.wait("vector", tc, M.get(("tfT", gi - 2), 0))
                    P.op("vector", lambda v, nat=nat, sl2=sl2: v.tensor_copy(
                        nat, tfi[:, D * sl2:D * sl2 + D]), vc)
                    M[("tfc", gi)] = vc.n
                    P.wait("tensor", vc, M[("tfc", gi)])
                    P.wait("tensor", vc, M.get(("tfE", gi - 2), 0))
                    for kcc in range(KC):
                        P.op("tensor",
                             lambda t, ps=ps, kcc=kcc, nat=nat: t.transpose(
                                 ps[:, 128 * kcc:128 * kcc + 128],
                                 nat[:, 128 * kcc:128 * kcc + 128], identb[:]),
                             tc if kcc == KC - 1 else None)
                    M[("tfT", gi)] = tc.n
                    P.wait("vector", tc, M[("tfT", gi)])
                    P.wait("vector", dst2[sl2])
                    P.op("vector", lambda v, ps=ps, stg=stg: v.tensor_copy(stg, ps[:]), vc)
                    M[("tfE", gi)] = vc.n
                    P.wait("scalar", vc, M[("tfE", gi)])
                    P.op("scalar", lambda s, scr=w_scr, hc=hc, stg=stg: s.dma_start(
                        out=scr[hc], in_=stg), dst2[sl2], 16)
                    M[(w_key, hc)] = (dst2[sl2], dst2[sl2].n)
                scrq_stores.pop(0)()
                scrq_stores.pop(0)()
            for dc in range(KC):
                sl2 = dc % 2
                nat = h_sb[:, sl2 * HID:sl2 * HID + HID]
                stg = h_sb[:, 2 * HID + sl2 * HID: 2 * HID + sl2 * HID + HID]
                for r in range(3):
                    gi2 = 2 * HCN + 3 * dc + r
                    sl3 = gi2 % 2
                    ps = p_tr[r % 2]
                    P.wait("sync", vc, M.get(("tfc", gi2 - 2), 0))
                    P.op("sync", lambda sy, dc=dc, r=r, sl3=sl3: sy.dma_start(
                        out=tfi[:, D * sl3:D * sl3 + D],
                        in_=wd_d[128 * dc:128 * dc + 128, D * r:D * r + D]),
                        dtf[sl3], 16)
                    P.wait("vector", dtf[sl3])
                    P.wait("vector", tc, M.get(("wdT", dc - 2), 0))
                    P.op("vector", lambda v, nat=nat, r=r, sl3=sl3: v.tensor_copy(
                        nat[:, D * r:D * r + D], tfi[:, D * sl3:D * sl3 + D]), vc)
                    M[("tfc", gi2)] = vc.n
                    P.wait("tensor", vc, M[("tfc", gi2)])
                    P.wait("tensor", vc)
                    for i in range(8):
                        hc = 8 * r + i
                        P.op("tensor",
                             lambda t, ps=ps, i=i, hc=hc, nat=nat: t.transpose(
                                 ps[:, 128 * i:128 * i + 128],
                                 nat[:, 128 * hc:128 * hc + 128], identb[:]),
                             tc if i == 7 else None)
                    P.wait("vector", tc)
                    P.wait("vector", dst2[sl2])
                    P.op("vector", lambda v, ps=ps, stg=stg, r=r: v.tensor_copy(
                        stg[:, D * r:D * r + D], ps[:]), vc)
                M[("wdT", dc)] = tc.n
                P.wait("scalar", vc)
                P.op("scalar", lambda s, dc=dc, stg=stg: s.dma_start(
                    out=scr_d[:, :, 128 * dc:128 * dc + 128].rearrange("h p d -> p h d"),
                    in_=stg.rearrange("p (h d) -> p h d", d=128)), dst2[sl2], 16)

        startup_dst = dstc.n
        startup_v = vc.n
        ost_war = {}
        tail_pending = [None]
        M[("ptr_evac", 0)] = vc.n
        M[("ptr_evac", 1)] = vc.n

        for u in range(n_units):
            b_local, half = u // 2, u % 2
            nbase = 2 * half

            with nc.psum_tensor(f"ptr{u}", [128, 2048], BF16) as ptrU, \
                 contextlib.ExitStack() as pctx:
                pq = pctx.enter_context(nc.psum_tensor(f"pq{u}", [128, 512], F32))
                pkv = pctx.enter_context(nc.psum_tensor(f"pkv{u}", [128, 512], F32))
                psc = pctx.enter_context(nc.psum_tensor(f"psc{u}", [2, 512], F32))
                ppb = pctx.enter_context(nc.psum_tensor(f"ppb{u}", [128, 512], F32))
                pao = pctx.enter_context(nc.psum_tensor(f"pao{u}", [128, D], F32))

                def load_wqk(rr, u=u):
                    si = (u * 4 + rr) % 2
                    slot = si * 8192
                    if u * 4 + rr >= 2:
                        P.wait("sync", tc, M[("qkv_mm_done", u * 4 + rr - 2)])
                        if u == 0:
                            P.wait("sync", dstc, 64)
                    else:
                        P.wait("sync", dstc, 64)
                    P.op("sync", lambda sy, rr=rr, slot=slot: sy.dma_start(
                        out=qkvo[:, slot:slot + 4096], in_=scr_q[rr][:, 0:4096]),
                        dq[si], 16)

                def load_wv(rr, u=u):
                    si = (u * 4 + rr) % 2
                    slot = si * 8192
                    if u * 4 + rr >= 2:
                        P.wait("sync", tc, M[("vmm", u * 4 + rr - 2)])
                        if u == 0:
                            P.wait("sync", dstc, 64)
                    else:
                        P.wait("sync", dstc, 64)
                    P.op("sync", lambda sy, rr=rr, slot=slot: sy.dma_start(
                        out=qkvo[:, slot + 4096:slot + 6144], in_=scr_q[rr][:, 4096:6144]),
                        dq[si], 16)

                def load_wo_part(rr, u=u):
                    si = (u * 4 + rr) % 2
                    slot = si * 8192
                    if u * 4 + rr >= 2:
                        P.wait("sync", tc, M[("wo_mm_done", u * 4 + rr - 2)])
                        if u == 0:
                            P.wait("sync", dstc, 64)
                    else:
                        P.wait("sync", dstc, 64)
                    P.op("sync", lambda sy, rr=rr, slot=slot: sy.dma_start(
                        out=qkvo[:, slot + 6144:slot + 8192], in_=scr_q[rr][:, 6144:8192]),
                        dq[si], 16)

                def emit_one_proj(rr, wi, dst_ap, u=u):
                    si = (u * 4 + rr) % 2
                    gslot = si * 8192
                    for ch in range(2):
                        for kcc in range(KC):
                            P.op("tensor", lambda t, d2=dst_ap, wi=wi, ch=ch, kcc=kcc,
                                 gslot=gslot, rr=rr: t.matmul(
                                     d2[:, 256 * ch:256 * ch + 256],
                                     qkvo[:, gslot + 2048 * wi + 256 * kcc + 128 * ch:
                                          gslot + 2048 * wi + 256 * kcc + 128 * ch + 128],
                                     xhB[:, 1024 * kcc + 256 * rr:
                                          1024 * kcc + 256 * rr + 256],
                                     start=(ch == 0 and kcc == 0),
                                     stop=(ch == 1 and kcc == KC - 1),
                                     skip_group_check=True),
                                 tc if (ch == 1 and kcc == KC - 1) else None)
                    M[("mm", "qkv"[wi], rr)] = tc.n

                def emit_qk_mm(rr, u=u, pq=pq, pkv=pkv):
                    si = (u * 4 + rr) % 2
                    P.wait("tensor", vc, M[("xhtile", 2 * rr + 1)])
                    P.wait("tensor", dq[si])
                    P.wait("tensor", ac, M.get(("evQ", u * 4 + rr - 1), 0))
                    P.wait("tensor", ac, M.get(("evV", u * 4 + rr - 1), 0))
                    emit_one_proj(rr, 0, pq[:])
                    emit_one_proj(rr, 1, pkv[:])
                    M[("qkv_mm_done", u * 4 + rr)] = tc.n

                load_wqk(0)
                load_wv(0)
                load_wo_part(0)
                load_wqk(1)
                load_wv(1)
                load_wo_part(1)
                b2war = M.get(("yevacG1", u - 1), 0)
                if u == 0:
                    P.wait("vector", dst2[0])
                    P.wait("vector", dst2[1])
                for j in (0, 1):
                    emit_xhat_tile(j, ptrU, M[("rst1a", u)], M.get(("b1_done", u - 1), 0), b2war)
                emit_qk_mm(0)
                load_wqk(2)
                for j in (2, 3):
                    emit_xhat_tile(j, ptrU, M[("rst1a", u)], M.get(("b1_done", u - 1), 0), b2war)
                if tail_pending[0] is not None:
                    tail_pending[0]()
                    tail_pending[0] = None
                P.wait("scalar", tc, M[("mm", "q", 0)])
                P.wait("scalar", gc, M.get(("qkprod", -1), 0))
                for ch in range(2):
                    P.op("scalar", lambda s, ch=ch, pq=pq: s.activation(
                        q_sb[:, 256 * ch:256 * ch + 256], pq[:, 256 * ch:256 * ch + 256],
                        AF.Identity, bias=bqT[:, ch:ch + 1]), ac)
                M[("evQ", u * 4)] = ac.n
                P.wait("scalar", tc, M[("mm", "k", 0)])
                for ch in range(2):
                    P.op("scalar", lambda s, ch=ch, pkv=pkv: s.activation(
                        k_sb[:, 256 * ch:256 * ch + 256], pkv[:, 256 * ch:256 * ch + 256],
                        AF.Identity, bias=bkT[:, ch:ch + 1]), ac)
                M[("evK", u * 4)] = ac.n
                if u > 0:
                    for j in (4, 5, 6, 7):
                        emit_ln1_stats(j)
                    M[("rst1b", u)] = emit_batch_rstd(MV, 4, 8)

                def emit_resid2(j, u=u):
                    jsl = j % 2
                    P.wait("vector", ac, M[("ln2apply", j)])
                    P.op("vector", lambda v, j=j, jsl=jsl: v.tensor_tensor(
                        x_tm[:, D * j:D * j + D], x_tm[:, D * j:D * j + D],
                        t_sb[:, D * jsl:D * jsl + D], ALU.add), vc)
                    M[("resid2", j)] = vc.n
                    P.wait("vector", vc)
                    xj = x_tm[:, D * j:D * j + D]
                    P.op("vector", lambda v, xj=xj: v.bn_stats(bn2_sb[:, 0:6], xj[:, 0:512]))
                    P.op("vector", lambda v, xj=xj: v.bn_stats(bn2_sb[:, 6:12], xj[:, 512:1024]), vc)
                    P.wait("vector", vc)
                    P.op("vector", lambda v, j=j: v.bn_aggr(
                        stats[:, MV3 + 2 * j:MV3 + 2 * j + 2], bn2_sb[:, 0:12]), vc)
                    if j == 3:
                        M[("rst3a", u)] = emit_batch_rstd(MV3, 0, 4)
                    elif j == 5:
                        M[("rst3b1", u)] = emit_batch_rstd(MV3, 4, 6)
                    elif j == 7:
                        M[("rst3b2", u)] = emit_batch_rstd(MV3, 6, 8)

                for rr in range(4):
                    si = (u * 4 + rr) % 2
                    gslot = si * 8192
                    vsl = rr % 2
                    P.wait("tensor", ac, M[("evK", u * 4 + rr)])
                    P.wait("tensor", dq[si])
                    emit_one_proj(rr, 2, pkv[:])
                    M[("vmm", u * 4 + rr)] = tc.n
                    if rr + 2 < 4:
                        load_wv(rr + 2)
                    P.wait("scalar", tc, M[("mm", "v", rr)])
                    P.wait("scalar", vc, M.get(("Amul", rr - 2), 0))
                    for ch in range(2):
                        P.op("scalar", lambda s, ch=ch, rr=rr, vsl=vsl, pkv=pkv: s.activation(
                            v_sb[:, 512 * vsl + 256 * ch:512 * vsl + 256 * ch + 256],
                            pkv[:, 256 * ch:256 * ch + 256],
                            AF.Identity, bias=bvT[:, 2 * rr + ch:2 * rr + ch + 1]), ac)
                    M[("evV", u * 4 + rr)] = ac.n

                    P.wait("gpsimd", ac, M[("evK", u * 4 + rr)])
                    P.wait("gpsimd", tc, M.get(("sc", rr - 2), 0))
                    P.op("gpsimd", lambda g, vsl=vsl: g.tensor_mul(
                        qk_bf[:, 512 * vsl:512 * vsl + 512], q_sb[:], k_sb[:]), gc)
                    M[("qkprod", rr)] = gc.n

                    P.wait("tensor", gc, M[("qkprod", rr)])
                    P.wait("tensor", ac, M.get(("exps", rr - 1), 0))
                    for ch in range(2):
                        P.op("tensor", lambda t, ch=ch, vsl=vsl, psc=psc: t.matmul(
                            psc[:, 256 * ch:256 * ch + 256], selh[:],
                            qk_bf[:, 512 * vsl + 256 * ch:512 * vsl + 256 * ch + 256],
                            start=(ch == 0), stop=(ch == 1), skip_group_check=True),
                            tc if ch == 1 else None)
                    M[("sc", rr)] = tc.n

                    so = 16 * vsl
                    P.wait("vector", tc, M[("sc", rr)])
                    P.wait("vector", ac, M.get(("exps", rr - 2), 0))
                    P.op("vector", lambda v, so=so, psc=psc: v.reduce_max(
                        smx[:, so:so + 4], psc[:].rearrange("p (b s) -> p b s", s=128),
                        axis=AX.X), vc)
                    P.wait("vector", vc)
                    P.op("vector", lambda v, so=so: v.tensor_scalar(
                        smx[:, so + 4:so + 8], smx[:, so:so + 4], -SCL, None, ALU.mult), vc)
                    P.wait("scalar", vc)
                    P.wait("scalar", tc, M.get(("pb", rr - 2), 0))
                    for c in range(4):
                        P.op("scalar", lambda s, c=c, so=so, vsl=vsl, psc=psc: s.activation(
                            p_sb[:, 512 * vsl + 128 * c:512 * vsl + 128 * c + 128],
                            psc[:, 128 * c:128 * c + 128],
                            AF.Exp, scale=SCL, bias=smx[:, so + 4 + c:so + 5 + c],
                            accum_out=smx[:, so + 8 + c:so + 9 + c]), ac)
                    M[("exps", rr)] = ac.n
                    if rr + 1 < 4:
                        emit_qk_mm(rr + 1)
                        P.wait("scalar", tc, M[("mm", "q", rr + 1)])
                        P.wait("scalar", gc, M.get(("qkprod", rr), 0))
                        for ch in range(2):
                            P.op("scalar", lambda s, ch=ch, rr2=rr + 1, pq=pq: s.activation(
                                q_sb[:, 256 * ch:256 * ch + 256], pq[:, 256 * ch:256 * ch + 256],
                                AF.Identity, bias=bqT[:, 2 * rr2 + ch:2 * rr2 + ch + 1]), ac)
                        M[("evQ", u * 4 + rr + 1)] = ac.n
                        P.wait("scalar", tc, M[("mm", "k", rr + 1)])
                        for ch in range(2):
                            P.op("scalar", lambda s, ch=ch, rr2=rr + 1, pkv=pkv: s.activation(
                                k_sb[:, 256 * ch:256 * ch + 256], pkv[:, 256 * ch:256 * ch + 256],
                                AF.Identity, bias=bkT[:, 2 * rr2 + ch:2 * rr2 + ch + 1]), ac)
                        M[("evK", u * 4 + rr + 1)] = ac.n
                        if rr < 2:
                            b2war = M.get(("yevacG1", u - 1), 0)
                            for j in (4 + 2 * rr, 5 + 2 * rr):
                                emit_xhat_tile(j, ptrU, M[("rst1b", u)],
                                               M.get(("b1_done", u - 1), 0), b2war)
                        if rr + 3 < 4:
                            load_wqk(rr + 3)
                    if rr == 3:
                        for j3 in (0, 1, 2, 3):
                            emit_xhat_tile(j3, ptrU, M[("rst3a", u)],
                                           M[("qkv_mm_done", u * 4 + j3 // 2)])
                        M[("xh3nt", 0)] = vc.n

                    P.wait("vector", ac, M[("exps", rr)])
                    P.op("vector", lambda v, so=so: v.reciprocal(
                        smx[:, so + 12:so + 16], smx[:, so + 8:so + 12]), vc)
                    P.wait("vector", vc)
                    for c in range(4):
                        P.op("vector", lambda v, c=c, so=so, vsl=vsl: v.tensor_scalar(
                            p_sb[:, 512 * vsl + 128 * c:512 * vsl + 128 * c + 128],
                            p_sb[:, 512 * vsl + 128 * c:512 * vsl + 128 * c + 128],
                            smx[:, so + 12 + c:so + 13 + c], None, ALU.mult),
                            vc if c == 3 else None)
                    pnorm_v = vc.n

                    P.wait("tensor", vc, pnorm_v)
                    P.wait("tensor", vc, M.get(("Amul", rr - 1), 0))
                    for ch in range(2):
                        P.op("tensor", lambda t, ch=ch, vsl=vsl, ppb=ppb: t.matmul(
                            ppb[:, 256 * ch:256 * ch + 256], bch[:],
                            p_sb[:, 512 * vsl + 256 * ch:512 * vsl + 256 * ch + 256],
                            start=(ch == 0), stop=(ch == 1), skip_group_check=True),
                            tc if ch == 1 else None)
                    M[("pb", rr)] = tc.n

                    P.wait("vector", tc, M[("pb", rr)])
                    P.wait("vector", ac, M[("evV", u * 4 + rr)])
                    P.wait("vector", tc, M.get(("wo_mm_done", u * 4 + rr - 2), 0))
                    P.op("vector", lambda v, vsl=vsl, ppb=ppb: v.tensor_mul(
                        a_bf[:, 512 * vsl:512 * vsl + 512], ppb[:],
                        v_sb[:, 512 * vsl:512 * vsl + 512]), vc)
                    M[("Amul", rr)] = vc.n

                    for s2 in range(2):
                        j = 2 * rr + s2
                        jsl = j % 2
                        mvp, ltp, rsp, nmp = MV2P(jsl), LT2P(jsl), RS2P(jsl), NM2P(jsl)
                        P.wait("tensor", vc, M[("Amul", rr)])
                        for hf in range(2):
                            P.wait("tensor", ac, M.get(("aoc", 2 * rr + s2 - 1, hf), 0))
                            P.op("tensor", lambda t, hf=hf, pao=pao: t.matmul(
                                pao[:, 512 * hf:512 * hf + 512], onesb[:],
                                bo_rowb[:, 512 * hf:512 * hf + 512],
                                start=True, stop=False, skip_group_check=True), None)
                            for ch in range(2):
                                P.op("tensor", lambda t, hf=hf, ch=ch, s2=s2, vsl=vsl,
                                     gslot=gslot, pao=pao: t.matmul(
                                         pao[:, 512 * hf:512 * hf + 512],
                                         a_bf[:, 512 * vsl + 256 * ch + 128 * s2:
                                              512 * vsl + 256 * ch + 128 * s2 + 128],
                                         qkvo[:, gslot + 6144 + 1024 * ch + 512 * hf:
                                              gslot + 6144 + 1024 * ch + 512 * hf + 512],
                                         start=False, stop=(ch == 1),
                                         skip_group_check=True),
                                     tc if ch == 1 else None)
                            M[("wo_h", j, hf)] = tc.n
                        wo_t = tc.n
                        if s2 == 1:
                            M[("wo_mm_done", u * 4 + rr)] = wo_t

                        P.wait("scalar", vc, M.get(("aostats", j - 2), 0))
                        for hf in range(2):
                            P.wait("scalar", tc, M[("wo_h", j, hf)])
                            P.op("scalar", lambda s, jsl=jsl, hf=hf, pao=pao: s.activation(
                                ao_sb[:, D * jsl + 512 * hf:D * jsl + 512 * hf + 512],
                                pao[:, 512 * hf:512 * hf + 512], AF.Copy), ac)
                            M[("aoc", 2 * rr + s2, hf)] = ac.n
                        M[("aocopy", 2 * rr + s2)] = ac.n
                        P.wait("vector", ac, M[("aocopy", 2 * rr + s2)])
                        P.wait("vector", vc)
                        P.op("vector", lambda v, jsl=jsl: v.bn_stats(
                            bn_sb[:, 12 * jsl:12 * jsl + 6], ao_sb[:, D * jsl:D * jsl + 512]))
                        P.op("vector", lambda v, jsl=jsl: v.bn_stats(
                            bn_sb[:, 12 * jsl + 6:12 * jsl + 12],
                            ao_sb[:, D * jsl + 512:D * jsl + 1024]), vc)
                        M[("aostats", j)] = vc.n
                        P.wait("vector", vc)
                        P.wait("vector", ac, M.get(("lnexp2", j - 2), 0))
                        P.op("vector", lambda v, mvp=mvp, jsl=jsl: v.bn_aggr(
                            stats[:, mvp:mvp + 2], bn_sb[:, 12 * jsl:12 * jsl + 12]), vc)
                        P.wait("scalar", vc)
                        P.op("scalar", lambda s, mvp=mvp, ltp=ltp: s.activation(
                            stats[:, ltp:ltp + 1], stats[:, mvp + 1:mvp + 2],
                            AF.Ln, bias=eps_c[:, 0:1]), ac)
                        P.wait("scalar", ac)
                        P.op("scalar", lambda s, ltp=ltp, rsp=rsp: s.activation(
                            stats[:, rsp:rsp + 1], stats[:, ltp:ltp + 1], AF.Exp, scale=-0.5), ac)
                        M[("lnexp2", j)] = ac.n
                        P.wait("vector", ac)
                        P.wait("vector", ac, M.get(("ln2apply", j - 2), 0))
                        P.op("vector", lambda v, mvp=mvp, rsp=rsp, nmp=nmp: v.scalar_tensor_tensor(
                            stats[:, nmp:nmp + 1], stats[:, mvp:mvp + 1], -1.0,
                            stats[:, rsp:rsp + 1], ALU.mult, ALU.mult), vc)
                        P.wait("scalar", vc)
                        P.wait("scalar", vc, M.get(("resid2", j - 2), 0))
                        P.wait("scalar", vc, M.get(("r4t", jsl), 0))
                        P.op("scalar", lambda s, jsl=jsl, rsp=rsp, nmp=nmp: s.activation(
                            t_sb[:, D * jsl:D * jsl + D], ao_sb[:, D * jsl:D * jsl + D],
                            AF.Identity, bias=stats[:, nmp:nmp + 1],
                            scale=stats[:, rsp:rsp + 1]), ac)
                        M[("ln2apply", j)] = ac.n
                        emit_resid2(j)

                    if rr + 2 < 4:
                        load_wo_part(rr + 2)

            jobs = [(0, 0), (1, 0), (0, 1), (1, 1)] + \
                   [(hc, nt) for hc in range(2, HCN) for nt in range(2)]
            with nc.psum_tensor(f"ptr_{u}", [128, 2048], BF16) as ptrU2, \
                 nc.psum_tensor(f"pg{u}", [128, 3 * 512], F32) as pg, \
                 nc.psum_tensor(f"pu{u}", [128, 3 * 512], F32) as pu:
                loaded = set()
                for ji, (hc, nt) in enumerate(jobs):
                    slot = hc % 4
                    if hc not in loaded:
                        loaded.add(hc)
                        if ("rgu", u, hc - 4) in M:
                            P.wait("sync", tc, M[("rgu", u, hc - 4)])
                        elif u > 0 and hc < 4:
                            P.wait("sync", tc, M[("ring_gu", HCN - 4 + hc)])
                        if u == 0:
                            P.wait("sync", *M[("scrg", hc)])
                            P.wait("sync", *M[("scru", hc)])
                        P.op("sync", lambda sy, hc=hc, slot=slot: sy.dma_start(
                            out=ring_g[:, D * slot:D * slot + D], in_=scr_g[hc]), dsg[slot], 16)
                        P.op("sync", lambda sy, hc=hc, slot=slot: sy.dma_start(
                            out=ring_u[:, D * slot:D * slot + D], in_=scr_u[hc]), dsg[slot], 16)

                    P.wait("tensor", dsg[slot])
                    gsl = ji % 3
                    ssl = ji % 2
                    P.wait("tensor", vc, M.get(("xh3nt", nt), 0))
                    P.wait("tensor", vc, M.get(("hmulB", u, ji - 3), 0))
                    for kcc in range(KC):
                        P.op("tensor", lambda t, slot=slot, gsl=gsl, nt=nt, kcc=kcc, pg=pg: t.matmul(
                            pg[:, 512 * gsl:512 * gsl + 512],
                            ring_g[:, D * slot + 128 * kcc:D * slot + 128 * kcc + 128],
                            xhB[:, 1024 * kcc + 512 * nt:1024 * kcc + 512 * nt + 512],
                            start=(kcc == 0), stop=(kcc == KC - 1)), None)
                    for kcc in range(KC):
                        P.op("tensor", lambda t, slot=slot, gsl=gsl, nt=nt, kcc=kcc, pu=pu: t.matmul(
                            pu[:, 512 * gsl:512 * gsl + 512],
                            ring_u[:, D * slot + 128 * kcc:D * slot + 128 * kcc + 128],
                            xhB[:, 1024 * kcc + 512 * nt:1024 * kcc + 512 * nt + 512],
                            start=(kcc == 0), stop=(kcc == KC - 1)),
                            tc if kcc == KC - 1 else None)
                    if nt == 1:
                        M[("rgu", u, hc)] = tc.n
                    gu_t = tc.n
                    P.wait("scalar", tc, gu_t)
                    P.wait("scalar", vc, M.get(("hmulB", u, ji - 2), 0))
                    P.op("scalar", lambda s, gsl=gsl, ssl=ssl, pg=pg: s.activation(
                        silu_sb[:, 512 * ssl:512 * ssl + 512],
                        pg[:, 512 * gsl:512 * gsl + 512], AF.Silu, scale=SDQ), ac)
                    silu_a = ac.n
                    P.wait("vector", ac, silu_a)
                    P.op("vector", lambda v, gsl=gsl, ssl=ssl, hc=hc, nt=nt, pu=pu: v.tensor_mul(
                        h_sb[:, D * hc + 512 * nt:D * hc + 512 * nt + 512],
                        silu_sb[:, 512 * ssl:512 * ssl + 512],
                        pu[:, 512 * gsl:512 * gsl + 512]), vc)
                    M[("hmulB", u, ji)] = vc.n
                    if ji == 0:
                        for j in (4, 5):
                            emit_xhat_tile(j, ptrU2, M[("rst3b1", u)],
                                           M[("qkv_mm_done", u * 4 + j // 2)])
                    elif ji == 1:
                        for j in (6, 7):
                            emit_xhat_tile(j, ptrU2, M[("rst3b2", u)],
                                           M[("qkv_mm_done", u * 4 + j // 2)])
                        M[("xh3nt", 1)] = vc.n
                for back in range(4):
                    M[("ring_gu", HCN - 4 + back)] = M[("rgu", u, HCN - 4 + back)]
                M[("b1_done", u)] = tc.n
                h_ready_v = vc.n

            with contextlib.ExitStack() as pctx:
                pd = [pctx.enter_context(nc.psum_tensor(f"pd{u}_{i}", [128, D], F32))
                      for i in range(4)]
                for grp in range(2):
                    for kcc in range(HCN):
                        slot = kcc % 4
                        if (u, grp, kcc - 4) in M:
                            P.wait("sync", tc, M[(u, grp, kcc - 4)])
                        elif grp == 1 and kcc < 4:
                            P.wait("sync", tc, M[(u, 0, HCN - 4 + kcc)])
                        elif u > 0 and grp == 0 and kcc < 4:
                            P.wait("sync", tc, M[("ring_d", HCN - 4 + kcc)])
                        elif u == 0 and grp == 0 and kcc < 4:
                            P.wait("sync", dst2[0])
                            P.wait("sync", dst2[1])
                        P.op("sync", lambda sy, kcc=kcc, slot=slot: sy.dma_start(
                            out=ring_d[:, D * slot:D * slot + D], in_=scr_d[kcc]), drd[slot], 16)
                        P.wait("tensor", drd[slot])
                        if kcc == 0:
                            P.wait("tensor", vc, h_ready_v)
                        for ji in range(4):
                            j = 4 * grp + ji
                            if kcc == 0 and grp == 1:
                                P.wait("tensor", vc, M[("yevac", ji)])
                            for hf in range(2):
                                P.op("tensor", lambda t, ji=ji, j=j, hf=hf, kcc=kcc, slot=slot, pd=pd: t.matmul(
                                    pd[ji][:, 512 * hf:512 * hf + 512],
                                    h_sb[:, D * kcc + 128 * j:D * kcc + 128 * j + 128],
                                    ring_d[:, D * slot + 512 * hf:D * slot + 512 * hf + 512],
                                    start=(kcc == 0), stop=(kcc == HCN - 1)),
                                    tc if (ji == 3 and hf == 1) else None)
                        M[(u, grp, kcc)] = tc.n
                        if grp == 1 and kcc >= HCN - 4:
                            M[("ring_d", kcc)] = tc.n
                    dn_done_t = tc.n
                    for ji in range(4):
                        P.wait("vector", tc, dn_done_t)
                        P.wait("vector", ac,
                               M.get(("ln4apply", (4 * grp - 4 + ji) % 8), 0))
                        P.op("vector", lambda v, ji=ji, pd=pd: v.tensor_copy(
                            y_sb[:, D * ji:D * ji + D], pd[ji][:]), vc)
                        M[("yevac", ji)] = vc.n
                        if grp == 1 and ji == 0:
                            M[("yevacG1", u)] = vc.n
                    if grp == 1:
                        M[("b2yevac", u)] = vc.n
                    if grp == 0:
                        emit_grp_tail(u, 0)
                    else:
                        tail_pending[0] = lambda u=u: emit_grp_tail(u, 1)

        if tail_pending[0] is not None:
            tail_pending[0]()
            tail_pending[0] = None
        for dsem in dos:
            P.wait("gpsimd", dsem, dsem.n)
            P.wait("sync", dsem, dsem.n)

        with nc.Block() as block:
            @block.tensor
            def _(eng):
                for fn in P.streams["tensor"]:
                    fn(eng)

            @block.vector
            def _(eng):
                for fn in P.streams["vector"]:
                    fn(eng)

            @block.scalar
            def _(eng):
                for fn in P.streams["scalar"]:
                    fn(eng)

            @block.sync
            def _(eng):
                for fn in P.streams["sync"]:
                    fn(eng)

            @block.gpsimd
            def _(eng):
                for fn in P.streams["gpsimd"]:
                    fn(eng)

    return nc


def _host_inputs(inputs):
    perm = np.concatenate([
        np.arange(64 * (rr + 4 * hj), 64 * (rr + 4 * hj) + 64)
        for rr in range(4) for hj in range(4)])
    selh = np.zeros((128, 2), dtype=BFNP)
    selh[0:64, 0] = 1
    selh[64:128, 1] = 1
    bch = np.zeros((2, 128), dtype=BFNP)
    bch[0, 0:64] = 1
    bch[1, 64:128] = 1
    return {
        "wq": np.ascontiguousarray(inputs["L0_attn_Wq"][:, perm]),
        "wk": np.ascontiguousarray(inputs["L0_attn_Wk"][:, perm]),
        "wv": np.ascontiguousarray(inputs["L0_attn_Wv"][:, perm]),
        "wo": np.ascontiguousarray(inputs["L0_attn_Wo"][perm, :]),
        "bq": np.ascontiguousarray(inputs["L0_attn_bq"][perm]),
        "bk": np.ascontiguousarray(inputs["L0_attn_bk"][perm]),
        "bv": np.ascontiguousarray(inputs["L0_attn_bv"][perm]),
        "bo": np.ascontiguousarray(inputs["L0_attn_bo"]),
        "wg": np.ascontiguousarray(inputs["L0_ffn_w_gate"]),
        "wu": np.ascontiguousarray(inputs["L0_ffn_w_up"]),
        "wd": np.ascontiguousarray(inputs["L0_ffn_w_down"]),
        "c_identb": np.eye(128, dtype=np.float32).astype(BFNP),
        "c_ident8": np.eye(8, dtype=np.float32),
        "c_selh": selh,
        "c_bch": bch,
        "c_onesb": np.ones((1, 128), dtype=BFNP),
    }


def kernel(**inputs):
    x = np.asarray(inputs["L0_attn_x"])
    common = _host_inputs({k: np.asarray(v) for k, v in inputs.items()})
    nc = build_nc(UNITS)
    in_maps = []
    for c in range(NCORES):
        m = dict(common)
        m["x"] = np.ascontiguousarray(x[4 * c:4 * c + 4])
        in_maps.append(m)
    res = run_bass_kernel_spmd(nc, in_maps, core_ids=list(range(NCORES)))
    return np.concatenate([res.results[c]["out"] for c in range(NCORES)], axis=0)


# version 19
# speedup vs baseline: 1.0181x; 1.0049x over previous
import contextlib

import numpy as np
import ml_dtypes

import concourse.bass as bass
import concourse.mybir as mybir
from concourse.bass_utils import run_bass_kernel_spmd

F32 = mybir.dt.float32
BF16 = mybir.dt.bfloat16
I32 = mybir.dt.int32
AF = mybir.ActivationFunctionType
ALU = mybir.AluOpType
AX = mybir.AxisListType
BFNP = ml_dtypes.bfloat16

D = 1024
HID = 3072
KC = 8
HCN = 24
EPS = 1e-5
SDQ = 1.0 / 127.0
SCL = 0.125
NCORES = 8
UNITS = 8
ENGS = ("tensor", "vector", "scalar", "sync", "gpsimd")

MV = 0
RST = 16
NMR = 24
MV3 = 48
LT1 = 64
LT4 = 72
RS4 = 76
NM4 = 80
MV4 = 84


def MV2P(p):
    return 32 + 2 * p


def LT2P(p):
    return 36 + p


def RS2P(p):
    return 38 + p


def NM2P(p):
    return 42 + p


class Ctr:
    def __init__(self, name):
        self.name = name
        self.n = 0
        self.sem = None


class Rec:
    def __init__(self):
        self.streams = {e: [] for e in ENGS}
        self.lastw = {}

    def op(self, e, f, inc=None, k=1):
        if inc is not None:
            inc.n += k

        def fn(eng, f=f, inc=inc, k=k):
            ins = f(eng)
            if inc is not None:
                ins.then_inc(inc.sem, k)

        self.streams[e].append(fn)
        return inc.n if inc is not None else None

    def wait(self, e, ctr, n=None):
        t = ctr.n if n is None else n
        if t <= 0 or t <= self.lastw.get((e, ctr.name), 0):
            return
        self.lastw[(e, ctr.name)] = t
        self.streams[e].append(lambda eng, c=ctr, t=t: eng.wait_ge(c.sem, t))


def build_nc(n_units=UNITS):
    nc = bass.Bass("TRN2", target_bir_lowering=False)

    x_d = nc.dram_tensor("x", [4, 2048, D], F32, kind="ExternalInput").ap()
    out_d = nc.dram_tensor("out", [4, 2048, D], F32, kind="ExternalOutput").ap()
    wq_d = nc.dram_tensor("wq", [D, D], F32, kind="ExternalInput").ap()
    wk_d = nc.dram_tensor("wk", [D, D], F32, kind="ExternalInput").ap()
    wv_d = nc.dram_tensor("wv", [D, D], F32, kind="ExternalInput").ap()
    wo_d = nc.dram_tensor("wo", [D, D], F32, kind="ExternalInput").ap()
    bq_d = nc.dram_tensor("bq", [D], F32, kind="ExternalInput").ap()
    bk_d = nc.dram_tensor("bk", [D], F32, kind="ExternalInput").ap()
    bv_d = nc.dram_tensor("bv", [D], F32, kind="ExternalInput").ap()
    bo_d = nc.dram_tensor("bo", [D], F32, kind="ExternalInput").ap()
    wg_d = nc.dram_tensor("wg", [HID, D], I32, kind="ExternalInput").ap()
    wu_d = nc.dram_tensor("wu", [HID, D], I32, kind="ExternalInput").ap()
    wd_d = nc.dram_tensor("wd", [D, HID], I32, kind="ExternalInput").ap()
    c_identb = nc.dram_tensor("c_identb", [128, 128], BF16, kind="ExternalInput").ap()
    c_ident8 = nc.dram_tensor("c_ident8", [8, 8], F32, kind="ExternalInput").ap()
    c_selh = nc.dram_tensor("c_selh", [128, 2], BF16, kind="ExternalInput").ap()
    c_bch = nc.dram_tensor("c_bch", [2, 128], BF16, kind="ExternalInput").ap()
    c_onesb = nc.dram_tensor("c_onesb", [1, 128], BF16, kind="ExternalInput").ap()
    scr_g = nc.dram_tensor("scr_g", [HCN, 128, D], BF16, kind="Internal").ap()
    scr_u = nc.dram_tensor("scr_u", [HCN, 128, D], BF16, kind="Internal").ap()
    scr_d = nc.dram_tensor("scr_d", [HCN, 128, D], BF16, kind="Internal").ap()
    scr_q = nc.dram_tensor("scr_q", [4, 128, 8192], BF16, kind="Internal").ap()

    with contextlib.ExitStack() as ctx:
        E = ctx.enter_context

        x_tm = E(nc.sbuf_tensor("x_tm", [128, 8 * D], F32))
        xh_tm = E(nc.sbuf_tensor("xh_tm", [128, 2 * D], BF16))
        xhB = E(nc.sbuf_tensor("xhB", [128, 8 * D], BF16))
        qkvo = E(nc.sbuf_tensor("qkvo", [128, 2 * 8192], BF16))
        ring_g = E(nc.sbuf_tensor("ring_g", [128, 4 * D], BF16))
        ring_u = E(nc.sbuf_tensor("ring_u", [128, 4 * D], BF16))
        ring_d = E(nc.sbuf_tensor("ring_d", [128, 4 * D], BF16))
        h_sb = E(nc.sbuf_tensor("h_sb", [128, HCN * D], BF16))
        q_sb = E(nc.sbuf_tensor("q_sb", [128, 512], BF16))
        k_sb = E(nc.sbuf_tensor("k_sb", [128, 512], BF16))
        v_sb = E(nc.sbuf_tensor("v_sb", [128, 2 * 512], BF16))
        qk_bf = E(nc.sbuf_tensor("qk_bf", [128, 2 * 512], BF16))
        a_bf = E(nc.sbuf_tensor("a_bf", [128, 2 * 512], BF16))
        p_sb = E(nc.sbuf_tensor("p_sb", [2, 2 * 512], BF16))
        smx = E(nc.sbuf_tensor("smx", [2, 32], F32))
        ao_sb = E(nc.sbuf_tensor("ao_sb", [128, 2 * D], BF16))
        t_sb = E(nc.sbuf_tensor("t_sb", [128, 2 * D], F32))
        y_sb = E(nc.sbuf_tensor("y_sb", [128, 4 * D], BF16))
        silu_sb = E(nc.sbuf_tensor("silu_sb", [128, 2 * 512], BF16))
        ostg_off = nc.sbuf_base
        ostg = E(nc.sbuf_tensor("ostg", [128, 3 * D], F32))
        tfi = nc.alloc_sbuf_tensor_at("tfi32", [128, 3 * D], I32, offset=ostg_off)
        stats = E(nc.sbuf_tensor("stats", [128, 92], F32))
        bn_sb = E(nc.sbuf_tensor("bn_sb", [128, 48], F32))
        bn2_sb = E(nc.sbuf_tensor("bn2_sb", [128, 12], F32))
        bqT = E(nc.sbuf_tensor("bqT", [128, 8], F32))
        bkT = E(nc.sbuf_tensor("bkT", [128, 8], F32))
        bvT = E(nc.sbuf_tensor("bvT", [128, 8], F32))
        b8 = E(nc.sbuf_tensor("b8", [8, 128], F32))
        bo_rowb = E(nc.sbuf_tensor("bo_rowb", [1, D], BF16))
        identb = E(nc.sbuf_tensor("identb", [128, 128], BF16))
        ident8 = E(nc.sbuf_tensor("ident8", [8, 8], F32))
        selh = E(nc.sbuf_tensor("selh", [128, 2], BF16))
        bch = E(nc.sbuf_tensor("bch", [2, 128], BF16))
        onesb = E(nc.sbuf_tensor("onesb", [1, 128], BF16))
        eps_c = E(nc.sbuf_tensor("eps_c", [128, 1], F32))

        def mk(name):
            c = Ctr(name)
            c.sem = E(nc.semaphore("sem_" + name))
            return c

        dxs = [mk(f"dx{j}") for j in range(8)]
        dq = [mk(f"dq{i}") for i in range(2)]
        dsg = [mk(f"dsg{i}") for i in range(4)]
        drd = [mk(f"drd{i}") for i in range(4)]
        dos = [mk(f"do{i}") for i in range(3)]
        dw = mk("dw")
        dwq = [mk(f"dwq{i}") for i in range(2)]
        dtf = [mk(f"dtf{i}") for i in range(3)]
        dst2 = [mk(f"dst2_{i}") for i in range(2)]
        dstc = mk("dst")
        tc = mk("t")
        vc = mk("v")
        ac = mk("a")
        gc = mk("g")

        P = Rec()
        M = {}

        P.op("vector", lambda v: v.memset(eps_c[:], EPS), vc)

        def emit_xload(un, j, war=None):
            b_l, hf = un // 2, un % 2
            n = 2 * hf + (j % 2)
            rr = j // 2
            if war is not None:
                P.wait("gpsimd", war[0], war[1])
            P.op("gpsimd", lambda g, b=b_l, n=n, rr=rr, j=j: g.dma_start(
                out=x_tm[:, D * j:D * j + D],
                in_=x_d[b, n * 512 + rr:(n + 1) * 512:4, :]), dxs[j], 16)

        def emit_ln1_stats(j):
            P.wait("vector", dxs[j])
            P.wait("vector", vc)
            xj = x_tm[:, D * j:D * j + D]
            P.op("vector", lambda v, xj=xj: v.bn_stats(bn2_sb[:, 0:6], xj[:, 0:512]))
            P.op("vector", lambda v, xj=xj: v.bn_stats(bn2_sb[:, 6:12], xj[:, 512:1024]), vc)
            P.wait("vector", vc)
            P.op("vector", lambda v, j=j: v.bn_aggr(
                stats[:, MV + 2 * j:MV + 2 * j + 2], bn2_sb[:, 0:12]), vc)

        def emit_batch_rstd(src_off, lo, hi, lt0=LT1, rs0=RST, nm0=NMR):
            w = hi - lo
            P.wait("scalar", vc)
            P.op("scalar", lambda s, so=src_off, lo=lo, hi=hi, w=w, lt0=lt0: s.activation(
                stats[:, lt0 + lo:lt0 + lo + w],
                stats[:, so + 2 * lo + 1:so + 2 * hi:2],
                AF.Ln, bias=eps_c[:, 0:1]), ac)
            P.wait("scalar", ac)
            P.op("scalar", lambda s, lo=lo, w=w, lt0=lt0, rs0=rs0: s.activation(
                stats[:, rs0 + lo:rs0 + lo + w], stats[:, lt0 + lo:lt0 + lo + w],
                AF.Exp, scale=-0.5), ac)
            P.wait("vector", ac)
            P.op("vector", lambda v, so=src_off, lo=lo, hi=hi, w=w, rs0=rs0, nm0=nm0: v.scalar_tensor_tensor(
                stats[:, nm0 + lo:nm0 + lo + w], stats[:, so + 2 * lo:so + 2 * hi:2],
                -1.0, stats[:, rs0 + lo:rs0 + lo + w], ALU.mult, ALU.mult), vc)
            return vc.n

        def emit_xhat_tile(j, ptr_ps, rst_ready_v, war_t, war_v=0):
            sl = j % 2
            po = 1024 * sl
            P.wait("scalar", vc, rst_ready_v)
            P.wait("scalar", tc, M.get(("xh_slot", sl), 0))
            P.op("scalar", lambda s, j=j, sl=sl: s.activation(
                xh_tm[:, D * sl:D * sl + D], x_tm[:, D * j:D * j + D],
                AF.Identity, bias=stats[:, NMR + j:NMR + j + 1],
                scale=stats[:, RST + j:RST + j + 1]), ac)
            P.wait("tensor", ac)
            P.wait("tensor", vc, M.get(("ptr_evac", sl), 0))
            if war_v:
                P.wait("tensor", vc, war_v)
            for kcc in range(KC):
                P.op("tensor", lambda t, sl=sl, kcc=kcc, ps=ptr_ps, po=po: t.transpose(
                    ps[:, po + 128 * kcc:po + 128 * kcc + 128],
                    xh_tm[:, D * sl + 128 * kcc:D * sl + 128 * kcc + 128],
                    identb[:]), tc if kcc == KC - 1 else None)
            M[("xh_slot", sl)] = tc.n
            P.wait("vector", tc)
            P.wait("vector", tc, war_t)
            P.op("vector", lambda v, j=j, ps=ptr_ps, po=po: v.tensor_copy(
                xhB[:].rearrange("p (c x) -> p c x", x=1024)[:, :, 128 * j:128 * j + 128],
                ps[:, po:po + 1024].rearrange("p (c x) -> p c x", x=128)), vc)
            M[("ptr_evac", sl)] = vc.n
            M[("xhtile", j)] = vc.n

        def emit_store(u, j, js4):
            rr, nrel = j // 2, j % 2
            b_l, n = u // 2, 2 * (u % 2) + nrel
            P.wait("scalar", vc, M[("resid4", j)])
            P.op("scalar", lambda s, b=b_l, n=n, rr=rr, js4=js4: s.dma_start(
                out=out_d[b, n * 512 + rr:(n + 1) * 512:4, :],
                in_=ostg[:, D * js4:D * js4 + D]), dos[js4], 16)
            M[("ostore", u * 8 + j)] = dos[js4].n
            ost_war[js4] = (dos[js4], dos[js4].n)

        def emit_grp_tail(u, grp):
            nu = n_units
            for ji in range(4):
                P.wait("vector", vc, M[("yevac", ji)])
                P.op("vector", lambda v, ji=ji: v.bn_stats(
                    bn_sb[:, 12 * ji:12 * ji + 6], y_sb[:, D * ji:D * ji + 512]))
                P.op("vector", lambda v, ji=ji: v.bn_stats(
                    bn_sb[:, 12 * ji + 6:12 * ji + 12],
                    y_sb[:, D * ji + 512:D * ji + 1024]), vc)
            P.wait("vector", vc)
            P.wait("vector", ac, M.get(("ln4tail",), 0))
            for ji in range(4):
                P.op("vector", lambda v, ji=ji: v.bn_aggr(
                    stats[:, MV4 + 2 * ji:MV4 + 2 * ji + 2],
                    bn_sb[:, 12 * ji:12 * ji + 12]), vc)
            rv = emit_batch_rstd(MV4, 0, 4, lt0=LT4, rs0=RS4, nm0=NM4)
            for ji in range(4):
                j = 4 * grp + ji
                jsl = j % 2
                js4 = j % 3
                P.wait("scalar", vc, rv)
                P.wait("scalar", vc, M.get(("r4t", jsl), 0))
                P.wait("scalar", vc, M.get(("resid2", 6 + jsl), 0))
                P.op("scalar", lambda s, ji=ji, jsl=jsl: s.activation(
                    t_sb[:, D * jsl:D * jsl + D], y_sb[:, D * ji:D * ji + D],
                    AF.Identity, bias=stats[:, NM4 + ji:NM4 + ji + 1],
                    scale=stats[:, RS4 + ji:RS4 + ji + 1]), ac)
                M[("ln4apply", j)] = ac.n
                P.wait("vector", ac, M[("ln4apply", j)])
                P.wait("vector", vc, M[("resid2", j)])
                if js4 in ost_war:
                    P.wait("vector", ost_war[js4][0], ost_war[js4][1])
                P.op("vector", lambda v, j=j, jsl=jsl, js4=js4: v.tensor_tensor(
                    ostg[:, D * js4:D * js4 + D], t_sb[:, D * jsl:D * jsl + D],
                    x_tm[:, D * j:D * j + D], ALU.add), vc)
                M[("resid4", j)] = vc.n
                M[("r4t", jsl)] = vc.n
                if u + 1 < nu:
                    emit_xload(u + 1, j, (vc, M[("resid4", j)]))
                if ji > 0:
                    emit_store(u, j - 1, (j - 1) % 3)
            M[("ln4tail",)] = ac.n
            emit_store(u, 4 * grp + 3, (4 * grp + 3) % 3)
            if u + 1 < nu and grp == 0:
                for ji in range(4):
                    emit_ln1_stats(ji)
                M[("rst1a", u + 1)] = emit_batch_rstd(MV, 0, 4)

        P.op("gpsimd", lambda g: g.dma_start(out=identb[:], in_=c_identb[:, :]), dw, 16)
        P.op("gpsimd", lambda g: g.dma_start(out=ident8[:], in_=c_ident8[:, :]), dw, 16)
        P.op("gpsimd", lambda g: g.dma_start(out=selh[:], in_=c_selh[:, :]), dw, 16)
        P.op("gpsimd", lambda g: g.dma_start(out=bch[:], in_=c_bch[:, :]), dw, 16)
        P.op("gpsimd", lambda g: g.dma_start(out=onesb[:], in_=c_onesb[:, :]), dw, 16)
        P.op("gpsimd", lambda g: g.dma_start(
            out=bo_rowb[:], in_=bo_d.rearrange("(a d) -> a d", a=1)), dw, 16)
        for j in range(8):
            emit_xload(0, j)
            emit_ln1_stats(j)
        M[("rst1a", 0)] = emit_batch_rstd(MV, 0, 4)
        M[("rst1b", 0)] = emit_batch_rstd(MV, 4, 8)

        with nc.psum_tensor("p_b8", [128, 8], F32) as p_b8:
            for bias_src, bias_dst in ((bq_d, bqT), (bk_d, bkT), (bv_d, bvT)):
                P.op("gpsimd", lambda g, s=bias_src: g.dma_start(
                    out=b8[:], in_=s.rearrange("(g k) -> g k", k=128)), dw, 16)
                P.wait("tensor", dw)
                P.wait("tensor", vc)
                P.op("tensor", lambda t: t.transpose(p_b8[:], b8[:], ident8[:]), tc)
                P.wait("vector", tc)
                P.op("vector", lambda v, bdst=bias_dst: v.tensor_copy(bdst[:], p_b8[:]), vc)
                P.wait("gpsimd", tc)
            M[("tfE", -2)] = vc.n
            M[("tfE", -1)] = vc.n

        scrq_stores = []
        for rr in range(4):
            si = rr % 2
            slot = si * 8192
            if rr >= 2:
                P.wait("gpsimd", dstc, 32)
            for wi, wsrc in enumerate((wq_d, wk_d, wv_d)):
                P.op("gpsimd", lambda g, s=wsrc, rr=rr, slot=slot, wi=wi: g.dma_start(
                    out=qkvo[:, slot + 2048 * wi: slot + 2048 * wi + 2048]
                        .rearrange("p (c o) -> p c o", o=256),
                    in_=s.rearrange("(c p) o -> p c o", p=128)[:, :, 256 * rr:256 * rr + 256]),
                    dwq[si], 16)
            P.op("gpsimd", lambda g, rr=rr, slot=slot: g.dma_start(
                out=qkvo[:, slot + 6144: slot + 8192]
                    .rearrange("p (c o) -> p c o", o=1024),
                in_=wo_d.rearrange("(c p) o -> p c o", p=128)[:, 2 * rr:2 * rr + 2, :]),
                dwq[si], 16)
            def _sq(rr=rr, slot=slot, si=si, dn=None):
                P.wait("scalar", dwq[si], dn)
                P.op("scalar", lambda s, rr=rr, slot=slot: s.dma_start(
                    out=scr_q[rr], in_=qkvo[:, slot:slot + 8192]), dstc, 16)
            _sq.__defaults__ = (rr, slot, si, dwq[si].n)
            scrq_stores.append(_sq)
            M[("scrq", rr)] = 32 * ((rr // 2) + 1)

        with nc.psum_tensor("p_tr0", [128, D], BF16) as p_tr0, \
             nc.psum_tensor("p_tr1", [128, D], BF16) as p_tr1:
            p_tr = (p_tr0, p_tr1)
            for wsi, (w_src, w_scr, w_key) in enumerate(
                    ((wg_d, scr_g, "scrg"), (wu_d, scr_u, "scru"))):
                for hc in range(HCN):
                    gi = wsi * HCN + hc
                    sl2 = gi % 2
                    sl3i = gi % 3
                    nat = xhB[:, sl2 * D:sl2 * D + D]
                    stg = xhB[:, 2048 + sl2 * D: 2048 + sl2 * D + D]
                    ps = p_tr[sl2]
                    P.wait("sync", vc, M.get(("tfc", gi - 3), 0))
                    P.op("sync", lambda sy, s=w_src, hc=hc, sl3i=sl3i: sy.dma_start(
                        out=tfi[:, D * sl3i:D * sl3i + D],
                        in_=s[128 * hc:128 * hc + 128, :]), dtf[sl3i], 16)
                    P.wait("vector", dtf[sl3i])
                    P.wait("vector", tc, M.get(("tfT", gi - 2), 0))
                    P.op("vector", lambda v, nat=nat, sl3i=sl3i: v.tensor_copy(
                        nat, tfi[:, D * sl3i:D * sl3i + D]), vc)
                    M[("tfc", gi)] = vc.n
                    P.wait("tensor", vc, M[("tfc", gi)])
                    P.wait("tensor", vc, M.get(("tfE", gi - 2), 0))
                    for kcc in range(KC):
                        P.op("tensor",
                             lambda t, ps=ps, kcc=kcc, nat=nat: t.transpose(
                                 ps[:, 128 * kcc:128 * kcc + 128],
                                 nat[:, 128 * kcc:128 * kcc + 128], identb[:]),
                             tc if kcc == KC - 1 else None)
                    M[("tfT", gi)] = tc.n
                    P.wait("vector", tc, M[("tfT", gi)])
                    P.wait("vector", dst2[sl2])
                    P.op("vector", lambda v, ps=ps, stg=stg: v.tensor_copy(stg, ps[:]), vc)
                    M[("tfE", gi)] = vc.n
                    P.wait("scalar", vc, M[("tfE", gi)])
                    P.op("scalar", lambda s, scr=w_scr, hc=hc, stg=stg: s.dma_start(
                        out=scr[hc], in_=stg), dst2[sl2], 16)
                    M[(w_key, hc)] = (dst2[sl2], dst2[sl2].n)
                scrq_stores.pop(0)()
                scrq_stores.pop(0)()
            for dc in range(KC):
                sl2 = dc % 2
                nat = h_sb[:, sl2 * HID:sl2 * HID + HID]
                stg = h_sb[:, 2 * HID + sl2 * HID: 2 * HID + sl2 * HID + HID]
                for r in range(3):
                    gi2 = 2 * HCN + 3 * dc + r
                    sl3 = gi2 % 3
                    ps = p_tr[r % 2]
                    P.wait("sync", vc, M.get(("tfc", gi2 - 3), 0))
                    P.op("sync", lambda sy, dc=dc, r=r, sl3=sl3: sy.dma_start(
                        out=tfi[:, D * sl3:D * sl3 + D],
                        in_=wd_d[128 * dc:128 * dc + 128, D * r:D * r + D]),
                        dtf[sl3], 16)
                    P.wait("vector", dtf[sl3])
                    P.wait("vector", tc, M.get(("wdT", dc - 2), 0))
                    P.op("vector", lambda v, nat=nat, r=r, sl3=sl3: v.tensor_copy(
                        nat[:, D * r:D * r + D], tfi[:, D * sl3:D * sl3 + D]), vc)
                    M[("tfc", gi2)] = vc.n
                    P.wait("tensor", vc, M[("tfc", gi2)])
                    P.wait("tensor", vc)
                    for i in range(8):
                        hc = 8 * r + i
                        P.op("tensor",
                             lambda t, ps=ps, i=i, hc=hc, nat=nat: t.transpose(
                                 ps[:, 128 * i:128 * i + 128],
                                 nat[:, 128 * hc:128 * hc + 128], identb[:]),
                             tc if i == 7 else None)
                    P.wait("vector", tc)
                    P.wait("vector", dst2[sl2])
                    P.op("vector", lambda v, ps=ps, stg=stg, r=r: v.tensor_copy(
                        stg[:, D * r:D * r + D], ps[:]), vc)
                M[("wdT", dc)] = tc.n
                P.wait("scalar", vc)
                P.op("scalar", lambda s, dc=dc, stg=stg: s.dma_start(
                    out=scr_d[:, :, 128 * dc:128 * dc + 128].rearrange("h p d -> p h d"),
                    in_=stg.rearrange("p (h d) -> p h d", d=128)), dst2[sl2], 16)

        startup_dst = dstc.n
        startup_v = vc.n
        ost_war = {}
        tail_pending = [None]
        M[("ptr_evac", 0)] = vc.n
        M[("ptr_evac", 1)] = vc.n

        for u in range(n_units):
            b_local, half = u // 2, u % 2
            nbase = 2 * half

            with nc.psum_tensor(f"ptr{u}", [128, 2048], BF16) as ptrU, \
                 contextlib.ExitStack() as pctx:
                pq = pctx.enter_context(nc.psum_tensor(f"pq{u}", [128, 512], F32))
                pkv = pctx.enter_context(nc.psum_tensor(f"pkv{u}", [128, 512], F32))
                psc = pctx.enter_context(nc.psum_tensor(f"psc{u}", [2, 512], F32))
                ppb = pctx.enter_context(nc.psum_tensor(f"ppb{u}", [128, 512], F32))
                pao = pctx.enter_context(nc.psum_tensor(f"pao{u}", [128, D], F32))

                def load_wqk(rr, u=u):
                    si = (u * 4 + rr) % 2
                    slot = si * 8192
                    if u * 4 + rr >= 2:
                        P.wait("sync", tc, M[("qkv_mm_done", u * 4 + rr - 2)])
                        if u == 0:
                            P.wait("sync", dstc, 64)
                    else:
                        P.wait("sync", dstc, 64)
                    P.op("sync", lambda sy, rr=rr, slot=slot: sy.dma_start(
                        out=qkvo[:, slot:slot + 4096], in_=scr_q[rr][:, 0:4096]),
                        dq[si], 16)

                def load_wv(rr, u=u):
                    si = (u * 4 + rr) % 2
                    slot = si * 8192
                    if u * 4 + rr >= 2:
                        P.wait("sync", tc, M[("vmm", u * 4 + rr - 2)])
                        if u == 0:
                            P.wait("sync", dstc, 64)
                    else:
                        P.wait("sync", dstc, 64)
                    P.op("sync", lambda sy, rr=rr, slot=slot: sy.dma_start(
                        out=qkvo[:, slot + 4096:slot + 6144], in_=scr_q[rr][:, 4096:6144]),
                        dq[si], 16)

                def load_wo_part(rr, u=u):
                    si = (u * 4 + rr) % 2
                    slot = si * 8192
                    if u * 4 + rr >= 2:
                        P.wait("sync", tc, M[("wo_mm_done", u * 4 + rr - 2)])
                        if u == 0:
                            P.wait("sync", dstc, 64)
                    else:
                        P.wait("sync", dstc, 64)
                    P.op("sync", lambda sy, rr=rr, slot=slot: sy.dma_start(
                        out=qkvo[:, slot + 6144:slot + 8192], in_=scr_q[rr][:, 6144:8192]),
                        dq[si], 16)

                def emit_one_proj(rr, wi, dst_ap, u=u):
                    si = (u * 4 + rr) % 2
                    gslot = si * 8192
                    for ch in range(2):
                        for kcc in range(KC):
                            P.op("tensor", lambda t, d2=dst_ap, wi=wi, ch=ch, kcc=kcc,
                                 gslot=gslot, rr=rr: t.matmul(
                                     d2[:, 256 * ch:256 * ch + 256],
                                     qkvo[:, gslot + 2048 * wi + 256 * kcc + 128 * ch:
                                          gslot + 2048 * wi + 256 * kcc + 128 * ch + 128],
                                     xhB[:, 1024 * kcc + 256 * rr:
                                          1024 * kcc + 256 * rr + 256],
                                     start=(ch == 0 and kcc == 0),
                                     stop=(ch == 1 and kcc == KC - 1),
                                     skip_group_check=True),
                                 tc if (ch == 1 and kcc == KC - 1) else None)
                    M[("mm", "qkv"[wi], rr)] = tc.n

                def emit_qk_mm(rr, u=u, pq=pq, pkv=pkv):
                    si = (u * 4 + rr) % 2
                    P.wait("tensor", vc, M[("xhtile", 2 * rr + 1)])
                    P.wait("tensor", dq[si])
                    P.wait("tensor", ac, M.get(("evQ", u * 4 + rr - 1), 0))
                    P.wait("tensor", ac, M.get(("evV", u * 4 + rr - 1), 0))
                    emit_one_proj(rr, 0, pq[:])
                    emit_one_proj(rr, 1, pkv[:])
                    M[("qkv_mm_done", u * 4 + rr)] = tc.n

                load_wqk(0)
                load_wv(0)
                load_wo_part(0)
                load_wqk(1)
                load_wv(1)
                load_wo_part(1)
                b2war = M.get(("yevacG1", u - 1), 0)
                if u == 0:
                    P.wait("vector", dst2[0])
                    P.wait("vector", dst2[1])
                for j in (0, 1):
                    emit_xhat_tile(j, ptrU, M[("rst1a", u)], M.get(("b1_done", u - 1), 0), b2war)
                emit_qk_mm(0)
                load_wqk(2)
                for j in (2, 3):
                    emit_xhat_tile(j, ptrU, M[("rst1a", u)], M.get(("b1_done", u - 1), 0), b2war)
                if tail_pending[0] is not None:
                    tail_pending[0]()
                    tail_pending[0] = None
                P.wait("scalar", tc, M[("mm", "q", 0)])
                P.wait("scalar", gc, M.get(("qkprod", -1), 0))
                for ch in range(2):
                    P.op("scalar", lambda s, ch=ch, pq=pq: s.activation(
                        q_sb[:, 256 * ch:256 * ch + 256], pq[:, 256 * ch:256 * ch + 256],
                        AF.Identity, bias=bqT[:, ch:ch + 1]), ac)
                M[("evQ", u * 4)] = ac.n
                P.wait("scalar", tc, M[("mm", "k", 0)])
                for ch in range(2):
                    P.op("scalar", lambda s, ch=ch, pkv=pkv: s.activation(
                        k_sb[:, 256 * ch:256 * ch + 256], pkv[:, 256 * ch:256 * ch + 256],
                        AF.Identity, bias=bkT[:, ch:ch + 1]), ac)
                M[("evK", u * 4)] = ac.n
                if u > 0:
                    for j in (4, 5, 6, 7):
                        emit_ln1_stats(j)
                    M[("rst1b", u)] = emit_batch_rstd(MV, 4, 8)

                def emit_resid2(j, u=u):
                    jsl = j % 2
                    P.wait("vector", ac, M[("ln2apply", j)])
                    P.op("vector", lambda v, j=j, jsl=jsl: v.tensor_tensor(
                        x_tm[:, D * j:D * j + D], x_tm[:, D * j:D * j + D],
                        t_sb[:, D * jsl:D * jsl + D], ALU.add), vc)
                    M[("resid2", j)] = vc.n
                    P.wait("vector", vc)
                    xj = x_tm[:, D * j:D * j + D]
                    P.op("vector", lambda v, xj=xj: v.bn_stats(bn2_sb[:, 0:6], xj[:, 0:512]))
                    P.op("vector", lambda v, xj=xj: v.bn_stats(bn2_sb[:, 6:12], xj[:, 512:1024]), vc)
                    P.wait("vector", vc)
                    P.op("vector", lambda v, j=j: v.bn_aggr(
                        stats[:, MV3 + 2 * j:MV3 + 2 * j + 2], bn2_sb[:, 0:12]), vc)
                    if j == 3:
                        M[("rst3a", u)] = emit_batch_rstd(MV3, 0, 4)
                    elif j == 5:
                        M[("rst3b1", u)] = emit_batch_rstd(MV3, 4, 6)
                    elif j == 7:
                        M[("rst3b2", u)] = emit_batch_rstd(MV3, 6, 8)

                for rr in range(4):
                    si = (u * 4 + rr) % 2
                    gslot = si * 8192
                    vsl = rr % 2
                    P.wait("tensor", ac, M[("evK", u * 4 + rr)])
                    P.wait("tensor", dq[si])
                    emit_one_proj(rr, 2, pkv[:])
                    M[("vmm", u * 4 + rr)] = tc.n
                    if rr + 2 < 4:
                        load_wv(rr + 2)
                    P.wait("scalar", tc, M[("mm", "v", rr)])
                    P.wait("scalar", vc, M.get(("Amul", rr - 2), 0))
                    for ch in range(2):
                        P.op("scalar", lambda s, ch=ch, rr=rr, vsl=vsl, pkv=pkv: s.activation(
                            v_sb[:, 512 * vsl + 256 * ch:512 * vsl + 256 * ch + 256],
                            pkv[:, 256 * ch:256 * ch + 256],
                            AF.Identity, bias=bvT[:, 2 * rr + ch:2 * rr + ch + 1]), ac)
                    M[("evV", u * 4 + rr)] = ac.n

                    P.wait("gpsimd", ac, M[("evK", u * 4 + rr)])
                    P.wait("gpsimd", tc, M.get(("sc", rr - 2), 0))
                    P.op("gpsimd", lambda g, vsl=vsl: g.tensor_mul(
                        qk_bf[:, 512 * vsl:512 * vsl + 512], q_sb[:], k_sb[:]), gc)
                    M[("qkprod", rr)] = gc.n

                    P.wait("tensor", gc, M[("qkprod", rr)])
                    P.wait("tensor", ac, M.get(("exps", rr - 1), 0))
                    for ch in range(2):
                        P.op("tensor", lambda t, ch=ch, vsl=vsl, psc=psc: t.matmul(
                            psc[:, 256 * ch:256 * ch + 256], selh[:],
                            qk_bf[:, 512 * vsl + 256 * ch:512 * vsl + 256 * ch + 256],
                            start=(ch == 0), stop=(ch == 1), skip_group_check=True),
                            tc if ch == 1 else None)
                    M[("sc", rr)] = tc.n

                    so = 16 * vsl
                    P.wait("vector", tc, M[("sc", rr)])
                    P.wait("vector", ac, M.get(("exps", rr - 2), 0))
                    P.op("vector", lambda v, so=so, psc=psc: v.reduce_max(
                        smx[:, so:so + 4], psc[:].rearrange("p (b s) -> p b s", s=128),
                        axis=AX.X), vc)
                    P.wait("vector", vc)
                    P.op("vector", lambda v, so=so: v.tensor_scalar(
                        smx[:, so + 4:so + 8], smx[:, so:so + 4], -SCL, None, ALU.mult), vc)
                    P.wait("scalar", vc)
                    P.wait("scalar", tc, M.get(("pb", rr - 2), 0))
                    for c in range(4):
                        P.op("scalar", lambda s, c=c, so=so, vsl=vsl, psc=psc: s.activation(
                            p_sb[:, 512 * vsl + 128 * c:512 * vsl + 128 * c + 128],
                            psc[:, 128 * c:128 * c + 128],
                            AF.Exp, scale=SCL, bias=smx[:, so + 4 + c:so + 5 + c],
                            accum_out=smx[:, so + 8 + c:so + 9 + c]), ac)
                    M[("exps", rr)] = ac.n
                    if rr + 1 < 4:
                        emit_qk_mm(rr + 1)
                        P.wait("scalar", tc, M[("mm", "q", rr + 1)])
                        P.wait("scalar", gc, M.get(("qkprod", rr), 0))
                        for ch in range(2):
                            P.op("scalar", lambda s, ch=ch, rr2=rr + 1, pq=pq: s.activation(
                                q_sb[:, 256 * ch:256 * ch + 256], pq[:, 256 * ch:256 * ch + 256],
                                AF.Identity, bias=bqT[:, 2 * rr2 + ch:2 * rr2 + ch + 1]), ac)
                        M[("evQ", u * 4 + rr + 1)] = ac.n
                        P.wait("scalar", tc, M[("mm", "k", rr + 1)])
                        for ch in range(2):
                            P.op("scalar", lambda s, ch=ch, rr2=rr + 1, pkv=pkv: s.activation(
                                k_sb[:, 256 * ch:256 * ch + 256], pkv[:, 256 * ch:256 * ch + 256],
                                AF.Identity, bias=bkT[:, 2 * rr2 + ch:2 * rr2 + ch + 1]), ac)
                        M[("evK", u * 4 + rr + 1)] = ac.n
                        if rr < 2:
                            b2war = M.get(("yevacG1", u - 1), 0)
                            for j in (4 + 2 * rr, 5 + 2 * rr):
                                emit_xhat_tile(j, ptrU, M[("rst1b", u)],
                                               M.get(("b1_done", u - 1), 0), b2war)
                        if rr + 3 < 4:
                            load_wqk(rr + 3)
                    if rr == 3:
                        for j3 in (0, 1, 2, 3):
                            emit_xhat_tile(j3, ptrU, M[("rst3a", u)],
                                           M[("qkv_mm_done", u * 4 + j3 // 2)])
                        M[("xh3nt", 0)] = vc.n

                    P.wait("vector", ac, M[("exps", rr)])
                    P.op("vector", lambda v, so=so: v.reciprocal(
                        smx[:, so + 12:so + 16], smx[:, so + 8:so + 12]), vc)
                    P.wait("vector", vc)
                    for c in range(4):
                        P.op("vector", lambda v, c=c, so=so, vsl=vsl: v.tensor_scalar(
                            p_sb[:, 512 * vsl + 128 * c:512 * vsl + 128 * c + 128],
                            p_sb[:, 512 * vsl + 128 * c:512 * vsl + 128 * c + 128],
                            smx[:, so + 12 + c:so + 13 + c], None, ALU.mult),
                            vc if c == 3 else None)
                    pnorm_v = vc.n

                    P.wait("tensor", vc, pnorm_v)
                    P.wait("tensor", vc, M.get(("Amul", rr - 1), 0))
                    for ch in range(2):
                        P.op("tensor", lambda t, ch=ch, vsl=vsl, ppb=ppb: t.matmul(
                            ppb[:, 256 * ch:256 * ch + 256], bch[:],
                            p_sb[:, 512 * vsl + 256 * ch:512 * vsl + 256 * ch + 256],
                            start=(ch == 0), stop=(ch == 1), skip_group_check=True),
                            tc if ch == 1 else None)
                    M[("pb", rr)] = tc.n

                    P.wait("vector", tc, M[("pb", rr)])
                    P.wait("vector", ac, M[("evV", u * 4 + rr)])
                    P.wait("vector", tc, M.get(("wo_mm_done", u * 4 + rr - 2), 0))
                    P.op("vector", lambda v, vsl=vsl, ppb=ppb: v.tensor_mul(
                        a_bf[:, 512 * vsl:512 * vsl + 512], ppb[:],
                        v_sb[:, 512 * vsl:512 * vsl + 512]), vc)
                    M[("Amul", rr)] = vc.n

                    for s2 in range(2):
                        j = 2 * rr + s2
                        jsl = j % 2
                        mvp, ltp, rsp, nmp = MV2P(jsl), LT2P(jsl), RS2P(jsl), NM2P(jsl)
                        P.wait("tensor", vc, M[("Amul", rr)])
                        for hf in range(2):
                            P.wait("tensor", ac, M.get(("aoc", 2 * rr + s2 - 1, hf), 0))
                            P.op("tensor", lambda t, hf=hf, pao=pao: t.matmul(
                                pao[:, 512 * hf:512 * hf + 512], onesb[:],
                                bo_rowb[:, 512 * hf:512 * hf + 512],
                                start=True, stop=False, skip_group_check=True), None)
                            for ch in range(2):
                                P.op("tensor", lambda t, hf=hf, ch=ch, s2=s2, vsl=vsl,
                                     gslot=gslot, pao=pao: t.matmul(
                                         pao[:, 512 * hf:512 * hf + 512],
                                         a_bf[:, 512 * vsl + 256 * ch + 128 * s2:
                                              512 * vsl + 256 * ch + 128 * s2 + 128],
                                         qkvo[:, gslot + 6144 + 1024 * ch + 512 * hf:
                                              gslot + 6144 + 1024 * ch + 512 * hf + 512],
                                         start=False, stop=(ch == 1),
                                         skip_group_check=True),
                                     tc if ch == 1 else None)
                            M[("wo_h", j, hf)] = tc.n
                        wo_t = tc.n
                        if s2 == 1:
                            M[("wo_mm_done", u * 4 + rr)] = wo_t

                        P.wait("scalar", vc, M.get(("aostats", j - 2), 0))
                        for hf in range(2):
                            P.wait("scalar", tc, M[("wo_h", j, hf)])
                            P.op("scalar", lambda s, jsl=jsl, hf=hf, pao=pao: s.activation(
                                ao_sb[:, D * jsl + 512 * hf:D * jsl + 512 * hf + 512],
                                pao[:, 512 * hf:512 * hf + 512], AF.Copy), ac)
                            M[("aoc", 2 * rr + s2, hf)] = ac.n
                        M[("aocopy", 2 * rr + s2)] = ac.n
                        P.wait("vector", ac, M[("aocopy", 2 * rr + s2)])
                        P.wait("vector", vc)
                        P.op("vector", lambda v, jsl=jsl: v.bn_stats(
                            bn_sb[:, 12 * jsl:12 * jsl + 6], ao_sb[:, D * jsl:D * jsl + 512]))
                        P.op("vector", lambda v, jsl=jsl: v.bn_stats(
                            bn_sb[:, 12 * jsl + 6:12 * jsl + 12],
                            ao_sb[:, D * jsl + 512:D * jsl + 1024]), vc)
                        M[("aostats", j)] = vc.n
                        P.wait("vector", vc)
                        P.wait("vector", ac, M.get(("lnexp2", j - 2), 0))
                        P.op("vector", lambda v, mvp=mvp, jsl=jsl: v.bn_aggr(
                            stats[:, mvp:mvp + 2], bn_sb[:, 12 * jsl:12 * jsl + 12]), vc)
                        P.wait("scalar", vc)
                        P.op("scalar", lambda s, mvp=mvp, ltp=ltp: s.activation(
                            stats[:, ltp:ltp + 1], stats[:, mvp + 1:mvp + 2],
                            AF.Ln, bias=eps_c[:, 0:1]), ac)
                        P.wait("scalar", ac)
                        P.op("scalar", lambda s, ltp=ltp, rsp=rsp: s.activation(
                            stats[:, rsp:rsp + 1], stats[:, ltp:ltp + 1], AF.Exp, scale=-0.5), ac)
                        M[("lnexp2", j)] = ac.n
                        P.wait("vector", ac)
                        P.wait("vector", ac, M.get(("ln2apply", j - 2), 0))
                        P.op("vector", lambda v, mvp=mvp, rsp=rsp, nmp=nmp: v.scalar_tensor_tensor(
                            stats[:, nmp:nmp + 1], stats[:, mvp:mvp + 1], -1.0,
                            stats[:, rsp:rsp + 1], ALU.mult, ALU.mult), vc)
                        P.wait("scalar", vc)
                        P.wait("scalar", vc, M.get(("resid2", j - 2), 0))
                        P.wait("scalar", vc, M.get(("r4t", jsl), 0))
                        P.op("scalar", lambda s, jsl=jsl, rsp=rsp, nmp=nmp: s.activation(
                            t_sb[:, D * jsl:D * jsl + D], ao_sb[:, D * jsl:D * jsl + D],
                            AF.Identity, bias=stats[:, nmp:nmp + 1],
                            scale=stats[:, rsp:rsp + 1]), ac)
                        M[("ln2apply", j)] = ac.n
                        emit_resid2(j)

                    if rr + 2 < 4:
                        load_wo_part(rr + 2)

            jobs = [(0, 0), (1, 0), (0, 1), (1, 1)] + \
                   [(hc, nt) for hc in range(2, HCN) for nt in range(2)]
            with nc.psum_tensor(f"ptr_{u}", [128, 2048], BF16) as ptrU2, \
                 nc.psum_tensor(f"pg{u}", [128, 3 * 512], F32) as pg, \
                 nc.psum_tensor(f"pu{u}", [128, 3 * 512], F32) as pu:
                loaded = set()
                for ji, (hc, nt) in enumerate(jobs):
                    slot = hc % 4
                    if hc not in loaded:
                        loaded.add(hc)
                        if ("rgu", u, hc - 4) in M:
                            P.wait("sync", tc, M[("rgu", u, hc - 4)])
                        elif u > 0 and hc < 4:
                            P.wait("sync", tc, M[("ring_gu", HCN - 4 + hc)])
                        if u == 0:
                            P.wait("sync", *M[("scrg", hc)])
                            P.wait("sync", *M[("scru", hc)])
                        P.op("sync", lambda sy, hc=hc, slot=slot: sy.dma_start(
                            out=ring_g[:, D * slot:D * slot + D], in_=scr_g[hc]), dsg[slot], 16)
                        P.op("sync", lambda sy, hc=hc, slot=slot: sy.dma_start(
                            out=ring_u[:, D * slot:D * slot + D], in_=scr_u[hc]), dsg[slot], 16)

                    P.wait("tensor", dsg[slot])
                    gsl = ji % 3
                    ssl = ji % 2
                    P.wait("tensor", vc, M.get(("xh3nt", nt), 0))
                    P.wait("tensor", vc, M.get(("hmulB", u, ji - 3), 0))
                    for kcc in range(KC):
                        P.op("tensor", lambda t, slot=slot, gsl=gsl, nt=nt, kcc=kcc, pg=pg: t.matmul(
                            pg[:, 512 * gsl:512 * gsl + 512],
                            ring_g[:, D * slot + 128 * kcc:D * slot + 128 * kcc + 128],
                            xhB[:, 1024 * kcc + 512 * nt:1024 * kcc + 512 * nt + 512],
                            start=(kcc == 0), stop=(kcc == KC - 1)), None)
                    for kcc in range(KC):
                        P.op("tensor", lambda t, slot=slot, gsl=gsl, nt=nt, kcc=kcc, pu=pu: t.matmul(
                            pu[:, 512 * gsl:512 * gsl + 512],
                            ring_u[:, D * slot + 128 * kcc:D * slot + 128 * kcc + 128],
                            xhB[:, 1024 * kcc + 512 * nt:1024 * kcc + 512 * nt + 512],
                            start=(kcc == 0), stop=(kcc == KC - 1)),
                            tc if kcc == KC - 1 else None)
                    if nt == 1:
                        M[("rgu", u, hc)] = tc.n
                    gu_t = tc.n
                    P.wait("scalar", tc, gu_t)
                    P.wait("scalar", vc, M.get(("hmulB", u, ji - 2), 0))
                    P.op("scalar", lambda s, gsl=gsl, ssl=ssl, pg=pg: s.activation(
                        silu_sb[:, 512 * ssl:512 * ssl + 512],
                        pg[:, 512 * gsl:512 * gsl + 512], AF.Silu, scale=SDQ), ac)
                    silu_a = ac.n
                    P.wait("vector", ac, silu_a)
                    P.op("vector", lambda v, gsl=gsl, ssl=ssl, hc=hc, nt=nt, pu=pu: v.tensor_mul(
                        h_sb[:, D * hc + 512 * nt:D * hc + 512 * nt + 512],
                        silu_sb[:, 512 * ssl:512 * ssl + 512],
                        pu[:, 512 * gsl:512 * gsl + 512]), vc)
                    M[("hmulB", u, ji)] = vc.n
                    if ji == 0:
                        for j in (4, 5):
                            emit_xhat_tile(j, ptrU2, M[("rst3b1", u)],
                                           M[("qkv_mm_done", u * 4 + j // 2)])
                    elif ji == 1:
                        for j in (6, 7):
                            emit_xhat_tile(j, ptrU2, M[("rst3b2", u)],
                                           M[("qkv_mm_done", u * 4 + j // 2)])
                        M[("xh3nt", 1)] = vc.n
                for back in range(4):
                    M[("ring_gu", HCN - 4 + back)] = M[("rgu", u, HCN - 4 + back)]
                M[("b1_done", u)] = tc.n
                h_ready_v = vc.n

            with contextlib.ExitStack() as pctx:
                pd = [pctx.enter_context(nc.psum_tensor(f"pd{u}_{i}", [128, D], F32))
                      for i in range(4)]
                for grp in range(2):
                    for kcc in range(HCN):
                        slot = kcc % 4
                        if (u, grp, kcc - 4) in M:
                            P.wait("sync", tc, M[(u, grp, kcc - 4)])
                        elif grp == 1 and kcc < 4:
                            P.wait("sync", tc, M[(u, 0, HCN - 4 + kcc)])
                        elif u > 0 and grp == 0 and kcc < 4:
                            P.wait("sync", tc, M[("ring_d", HCN - 4 + kcc)])
                        elif u == 0 and grp == 0 and kcc < 4:
                            P.wait("sync", dst2[0])
                            P.wait("sync", dst2[1])
                        P.op("sync", lambda sy, kcc=kcc, slot=slot: sy.dma_start(
                            out=ring_d[:, D * slot:D * slot + D], in_=scr_d[kcc]), drd[slot], 16)
                        P.wait("tensor", drd[slot])
                        if kcc == 0:
                            P.wait("tensor", vc, h_ready_v)
                        for ji in range(4):
                            j = 4 * grp + ji
                            if kcc == 0 and grp == 1:
                                P.wait("tensor", vc, M[("yevac", ji)])
                            for hf in range(2):
                                P.op("tensor", lambda t, ji=ji, j=j, hf=hf, kcc=kcc, slot=slot, pd=pd: t.matmul(
                                    pd[ji][:, 512 * hf:512 * hf + 512],
                                    h_sb[:, D * kcc + 128 * j:D * kcc + 128 * j + 128],
                                    ring_d[:, D * slot + 512 * hf:D * slot + 512 * hf + 512],
                                    start=(kcc == 0), stop=(kcc == HCN - 1)),
                                    tc if (ji == 3 and hf == 1) else None)
                        M[(u, grp, kcc)] = tc.n
                        if grp == 1 and kcc >= HCN - 4:
                            M[("ring_d", kcc)] = tc.n
                    dn_done_t = tc.n
                    for ji in range(4):
                        P.wait("vector", tc, dn_done_t)
                        P.wait("vector", ac,
                               M.get(("ln4apply", (4 * grp - 4 + ji) % 8), 0))
                        P.op("vector", lambda v, ji=ji, pd=pd: v.tensor_copy(
                            y_sb[:, D * ji:D * ji + D], pd[ji][:]), vc)
                        M[("yevac", ji)] = vc.n
                        if grp == 1 and ji == 0:
                            M[("yevacG1", u)] = vc.n
                    if grp == 1:
                        M[("b2yevac", u)] = vc.n
                    if grp == 0:
                        emit_grp_tail(u, 0)
                    else:
                        tail_pending[0] = lambda u=u: emit_grp_tail(u, 1)

        if tail_pending[0] is not None:
            tail_pending[0]()
            tail_pending[0] = None
        for dsem in dos:
            P.wait("gpsimd", dsem, dsem.n)
            P.wait("sync", dsem, dsem.n)

        with nc.Block() as block:
            @block.tensor
            def _(eng):
                for fn in P.streams["tensor"]:
                    fn(eng)

            @block.vector
            def _(eng):
                for fn in P.streams["vector"]:
                    fn(eng)

            @block.scalar
            def _(eng):
                for fn in P.streams["scalar"]:
                    fn(eng)

            @block.sync
            def _(eng):
                for fn in P.streams["sync"]:
                    fn(eng)

            @block.gpsimd
            def _(eng):
                for fn in P.streams["gpsimd"]:
                    fn(eng)

    return nc


def _host_inputs(inputs):
    perm = np.concatenate([
        np.arange(64 * (rr + 4 * hj), 64 * (rr + 4 * hj) + 64)
        for rr in range(4) for hj in range(4)])
    selh = np.zeros((128, 2), dtype=BFNP)
    selh[0:64, 0] = 1
    selh[64:128, 1] = 1
    bch = np.zeros((2, 128), dtype=BFNP)
    bch[0, 0:64] = 1
    bch[1, 64:128] = 1
    return {
        "wq": np.ascontiguousarray(inputs["L0_attn_Wq"][:, perm]),
        "wk": np.ascontiguousarray(inputs["L0_attn_Wk"][:, perm]),
        "wv": np.ascontiguousarray(inputs["L0_attn_Wv"][:, perm]),
        "wo": np.ascontiguousarray(inputs["L0_attn_Wo"][perm, :]),
        "bq": np.ascontiguousarray(inputs["L0_attn_bq"][perm]),
        "bk": np.ascontiguousarray(inputs["L0_attn_bk"][perm]),
        "bv": np.ascontiguousarray(inputs["L0_attn_bv"][perm]),
        "bo": np.ascontiguousarray(inputs["L0_attn_bo"]),
        "wg": np.ascontiguousarray(inputs["L0_ffn_w_gate"]),
        "wu": np.ascontiguousarray(inputs["L0_ffn_w_up"]),
        "wd": np.ascontiguousarray(inputs["L0_ffn_w_down"]),
        "c_identb": np.eye(128, dtype=np.float32).astype(BFNP),
        "c_ident8": np.eye(8, dtype=np.float32),
        "c_selh": selh,
        "c_bch": bch,
        "c_onesb": np.ones((1, 128), dtype=BFNP),
    }


def kernel(**inputs):
    x = np.asarray(inputs["L0_attn_x"])
    common = _host_inputs({k: np.asarray(v) for k, v in inputs.items()})
    nc = build_nc(UNITS)
    in_maps = []
    for c in range(NCORES):
        m = dict(common)
        m["x"] = np.ascontiguousarray(x[4 * c:4 * c + 4])
        in_maps.append(m)
    res = run_bass_kernel_spmd(nc, in_maps, core_ids=list(range(NCORES)))
    return np.concatenate([res.results[c]["out"] for c in range(NCORES)], axis=0)


# version 20
# speedup vs baseline: 1.0223x; 1.0041x over previous
import contextlib

import numpy as np
import ml_dtypes

import concourse.bass as bass
import concourse.mybir as mybir
from concourse.bass_utils import run_bass_kernel_spmd

F32 = mybir.dt.float32
BF16 = mybir.dt.bfloat16
I32 = mybir.dt.int32
AF = mybir.ActivationFunctionType
ALU = mybir.AluOpType
AX = mybir.AxisListType
BFNP = ml_dtypes.bfloat16

D = 1024
HID = 3072
KC = 8
HCN = 24
EPS = 1e-5
SDQ = 1.0 / 127.0
SCL = 0.125
NCORES = 8
UNITS = 8
ENGS = ("tensor", "vector", "scalar", "sync", "gpsimd")

MV = 0
RST = 16
NMR = 24
MV3 = 48
LT1 = 64
LT4 = 72
RS4 = 76
NM4 = 80
MV4 = 84


def MV2P(p):
    return 32 + 2 * p


def LT2P(p):
    return 36 + p


def RS2P(p):
    return 38 + p


def NM2P(p):
    return 42 + p


class Ctr:
    def __init__(self, name):
        self.name = name
        self.n = 0
        self.sem = None


class Rec:
    def __init__(self):
        self.streams = {e: [] for e in ENGS}
        self.lastw = {}

    def op(self, e, f, inc=None, k=1):
        if inc is not None:
            inc.n += k

        def fn(eng, f=f, inc=inc, k=k):
            ins = f(eng)
            if inc is not None:
                ins.then_inc(inc.sem, k)

        self.streams[e].append(fn)
        return inc.n if inc is not None else None

    def wait(self, e, ctr, n=None):
        t = ctr.n if n is None else n
        if t <= 0 or t <= self.lastw.get((e, ctr.name), 0):
            return
        self.lastw[(e, ctr.name)] = t
        self.streams[e].append(lambda eng, c=ctr, t=t: eng.wait_ge(c.sem, t))


def build_nc(n_units=UNITS):
    nc = bass.Bass("TRN2", target_bir_lowering=False)

    x_d = nc.dram_tensor("x", [4, 2048, D], F32, kind="ExternalInput").ap()
    out_d = nc.dram_tensor("out", [4, 2048, D], F32, kind="ExternalOutput").ap()
    wq_d = nc.dram_tensor("wq", [D, D], F32, kind="ExternalInput").ap()
    wk_d = nc.dram_tensor("wk", [D, D], F32, kind="ExternalInput").ap()
    wv_d = nc.dram_tensor("wv", [D, D], F32, kind="ExternalInput").ap()
    wo_d = nc.dram_tensor("wo", [D, D], F32, kind="ExternalInput").ap()
    bq_d = nc.dram_tensor("bq", [D], F32, kind="ExternalInput").ap()
    bk_d = nc.dram_tensor("bk", [D], F32, kind="ExternalInput").ap()
    bv_d = nc.dram_tensor("bv", [D], F32, kind="ExternalInput").ap()
    bo_d = nc.dram_tensor("bo", [D], F32, kind="ExternalInput").ap()
    wg_d = nc.dram_tensor("wg", [HID, D], I32, kind="ExternalInput").ap()
    wu_d = nc.dram_tensor("wu", [HID, D], I32, kind="ExternalInput").ap()
    wd_d = nc.dram_tensor("wd", [D, HID], I32, kind="ExternalInput").ap()
    c_identb = nc.dram_tensor("c_identb", [128, 128], BF16, kind="ExternalInput").ap()
    c_ident8 = nc.dram_tensor("c_ident8", [8, 8], F32, kind="ExternalInput").ap()
    c_selh = nc.dram_tensor("c_selh", [128, 2], BF16, kind="ExternalInput").ap()
    c_bch = nc.dram_tensor("c_bch", [2, 128], BF16, kind="ExternalInput").ap()
    c_onesb = nc.dram_tensor("c_onesb", [1, 128], BF16, kind="ExternalInput").ap()
    scr_g = nc.dram_tensor("scr_g", [HCN, 128, D], BF16, kind="Internal").ap()
    scr_u = nc.dram_tensor("scr_u", [HCN, 128, D], BF16, kind="Internal").ap()
    scr_d = nc.dram_tensor("scr_d", [HCN, 128, D], BF16, kind="Internal").ap()
    scr_q = nc.dram_tensor("scr_q", [4, 128, 8192], BF16, kind="Internal").ap()

    with contextlib.ExitStack() as ctx:
        E = ctx.enter_context

        x_tm = E(nc.sbuf_tensor("x_tm", [128, 8 * D], F32))
        xh_tm = E(nc.sbuf_tensor("xh_tm", [128, 2 * D], BF16))
        xhB = E(nc.sbuf_tensor("xhB", [128, 8 * D], BF16))
        qkvo = E(nc.sbuf_tensor("qkvo", [128, 2 * 8192], BF16))
        ring_g = E(nc.sbuf_tensor("ring_g", [128, 4 * D], BF16))
        ring_u = E(nc.sbuf_tensor("ring_u", [128, 4 * D], BF16))
        ring_d = E(nc.sbuf_tensor("ring_d", [128, 4 * D], BF16))
        h_sb = E(nc.sbuf_tensor("h_sb", [128, HCN * D], BF16))
        q_sb = E(nc.sbuf_tensor("q_sb", [128, 512], BF16))
        k_sb = E(nc.sbuf_tensor("k_sb", [128, 512], BF16))
        v_sb = E(nc.sbuf_tensor("v_sb", [128, 2 * 512], BF16))
        qk_bf = E(nc.sbuf_tensor("qk_bf", [128, 2 * 512], BF16))
        a_bf = E(nc.sbuf_tensor("a_bf", [128, 2 * 512], BF16))
        p_sb = E(nc.sbuf_tensor("p_sb", [2, 2 * 512], BF16))
        smx = E(nc.sbuf_tensor("smx", [2, 32], F32))
        ao_sb = E(nc.sbuf_tensor("ao_sb", [128, 2 * D], BF16))
        t_sb = E(nc.sbuf_tensor("t_sb", [128, 2 * D], F32))
        y_sb = E(nc.sbuf_tensor("y_sb", [128, 4 * D], BF16))
        silu_sb = E(nc.sbuf_tensor("silu_sb", [128, 2 * 512], BF16))
        ostg_off = nc.sbuf_base
        ostg = E(nc.sbuf_tensor("ostg", [128, 3 * D], F32))
        tfi = nc.alloc_sbuf_tensor_at("tfi32", [128, 3 * D], I32, offset=ostg_off)
        stats = E(nc.sbuf_tensor("stats", [128, 92], F32))
        bn_sb = E(nc.sbuf_tensor("bn_sb", [128, 48], F32))
        bn2_sb = E(nc.sbuf_tensor("bn2_sb", [128, 12], F32))
        bqT = E(nc.sbuf_tensor("bqT", [128, 8], F32))
        bkT = E(nc.sbuf_tensor("bkT", [128, 8], F32))
        bvT = E(nc.sbuf_tensor("bvT", [128, 8], F32))
        b8 = E(nc.sbuf_tensor("b8", [8, 128], F32))
        bo_rowb = E(nc.sbuf_tensor("bo_rowb", [1, D], BF16))
        identb = E(nc.sbuf_tensor("identb", [128, 128], BF16))
        ident8 = E(nc.sbuf_tensor("ident8", [8, 8], F32))
        selh = E(nc.sbuf_tensor("selh", [128, 2], BF16))
        bch = E(nc.sbuf_tensor("bch", [2, 128], BF16))
        onesb = E(nc.sbuf_tensor("onesb", [1, 128], BF16))
        eps_c = E(nc.sbuf_tensor("eps_c", [128, 1], F32))

        def mk(name):
            c = Ctr(name)
            c.sem = E(nc.semaphore("sem_" + name))
            return c

        dxs = [mk(f"dx{j}") for j in range(8)]
        dq = [mk(f"dq{i}") for i in range(2)]
        dsg = [mk(f"dsg{i}") for i in range(4)]
        drd = [mk(f"drd{i}") for i in range(4)]
        dos = [mk(f"do{i}") for i in range(3)]
        dw = mk("dw")
        dwq = [mk(f"dwq{i}") for i in range(2)]
        dtf = [mk(f"dtf{i}") for i in range(3)]
        dst2 = [mk(f"dst2_{i}") for i in range(2)]
        dstc = mk("dst")
        tc = mk("t")
        vc = mk("v")
        ac = mk("a")
        gc = mk("g")

        P = Rec()
        M = {}

        P.op("vector", lambda v: v.memset(eps_c[:], EPS), vc)

        def emit_xload(un, j, war=None):
            b_l, hf = un // 2, un % 2
            n = 2 * hf + (j % 2)
            rr = j // 2
            if war is not None:
                P.wait("scalar", war[0], war[1])
            P.op("scalar", lambda s, b=b_l, n=n, rr=rr, j=j: s.dma_start(
                out=x_tm[:, D * j:D * j + D],
                in_=x_d[b, n * 512 + rr:(n + 1) * 512:4, :]), dxs[j], 16)

        def emit_ln1_stats(j):
            P.wait("vector", dxs[j])
            P.wait("vector", vc)
            xj = x_tm[:, D * j:D * j + D]
            P.op("vector", lambda v, xj=xj: v.bn_stats(bn2_sb[:, 0:6], xj[:, 0:512]))
            P.op("vector", lambda v, xj=xj: v.bn_stats(bn2_sb[:, 6:12], xj[:, 512:1024]), vc)
            P.wait("vector", vc)
            P.op("vector", lambda v, j=j: v.bn_aggr(
                stats[:, MV + 2 * j:MV + 2 * j + 2], bn2_sb[:, 0:12]), vc)

        def emit_batch_rstd(src_off, lo, hi, lt0=LT1, rs0=RST, nm0=NMR):
            w = hi - lo
            P.wait("scalar", vc)
            P.op("scalar", lambda s, so=src_off, lo=lo, hi=hi, w=w, lt0=lt0: s.activation(
                stats[:, lt0 + lo:lt0 + lo + w],
                stats[:, so + 2 * lo + 1:so + 2 * hi:2],
                AF.Ln, bias=eps_c[:, 0:1]), ac)
            P.wait("scalar", ac)
            P.op("scalar", lambda s, lo=lo, w=w, lt0=lt0, rs0=rs0: s.activation(
                stats[:, rs0 + lo:rs0 + lo + w], stats[:, lt0 + lo:lt0 + lo + w],
                AF.Exp, scale=-0.5), ac)
            P.wait("vector", ac)
            P.op("vector", lambda v, so=src_off, lo=lo, hi=hi, w=w, rs0=rs0, nm0=nm0: v.scalar_tensor_tensor(
                stats[:, nm0 + lo:nm0 + lo + w], stats[:, so + 2 * lo:so + 2 * hi:2],
                -1.0, stats[:, rs0 + lo:rs0 + lo + w], ALU.mult, ALU.mult), vc)
            return vc.n

        def emit_xhat_tile(j, ptr_ps, rst_ready_v, war_t, war_v=0):
            sl = j % 2
            po = 1024 * sl
            P.wait("scalar", vc, rst_ready_v)
            P.wait("scalar", tc, M.get(("xh_slot", sl), 0))
            P.op("scalar", lambda s, j=j, sl=sl: s.activation(
                xh_tm[:, D * sl:D * sl + D], x_tm[:, D * j:D * j + D],
                AF.Identity, bias=stats[:, NMR + j:NMR + j + 1],
                scale=stats[:, RST + j:RST + j + 1]), ac)
            P.wait("tensor", ac)
            P.wait("tensor", vc, M.get(("ptr_evac", sl), 0))
            if war_v:
                P.wait("tensor", vc, war_v)
            for kcc in range(KC):
                P.op("tensor", lambda t, sl=sl, kcc=kcc, ps=ptr_ps, po=po: t.transpose(
                    ps[:, po + 128 * kcc:po + 128 * kcc + 128],
                    xh_tm[:, D * sl + 128 * kcc:D * sl + 128 * kcc + 128],
                    identb[:]), tc if kcc == KC - 1 else None)
            M[("xh_slot", sl)] = tc.n
            P.wait("vector", tc)
            P.wait("vector", tc, war_t)
            P.op("vector", lambda v, j=j, ps=ptr_ps, po=po: v.tensor_copy(
                xhB[:].rearrange("p (c x) -> p c x", x=1024)[:, :, 128 * j:128 * j + 128],
                ps[:, po:po + 1024].rearrange("p (c x) -> p c x", x=128)), vc)
            M[("ptr_evac", sl)] = vc.n
            M[("xhtile", j)] = vc.n

        def emit_store(u, j, js4):
            rr, nrel = j // 2, j % 2
            b_l, n = u // 2, 2 * (u % 2) + nrel
            P.wait("scalar", vc, M[("resid4", j)])
            P.op("scalar", lambda s, b=b_l, n=n, rr=rr, js4=js4: s.dma_start(
                out=out_d[b, n * 512 + rr:(n + 1) * 512:4, :],
                in_=ostg[:, D * js4:D * js4 + D]), dos[js4], 16)
            M[("ostore", u * 8 + j)] = dos[js4].n
            ost_war[js4] = (dos[js4], dos[js4].n)

        def emit_grp_tail(u, grp):
            nu = n_units
            for ji in range(4):
                P.wait("vector", vc, M[("yevac", ji)])
                P.op("vector", lambda v, ji=ji: v.bn_stats(
                    bn_sb[:, 12 * ji:12 * ji + 6], y_sb[:, D * ji:D * ji + 512]))
                P.op("vector", lambda v, ji=ji: v.bn_stats(
                    bn_sb[:, 12 * ji + 6:12 * ji + 12],
                    y_sb[:, D * ji + 512:D * ji + 1024]), vc)
            P.wait("vector", vc)
            P.wait("vector", ac, M.get(("ln4tail",), 0))
            for ji in range(4):
                P.op("vector", lambda v, ji=ji: v.bn_aggr(
                    stats[:, MV4 + 2 * ji:MV4 + 2 * ji + 2],
                    bn_sb[:, 12 * ji:12 * ji + 12]), vc)
            rv = emit_batch_rstd(MV4, 0, 4, lt0=LT4, rs0=RS4, nm0=NM4)
            for ji in range(4):
                j = 4 * grp + ji
                jsl = j % 2
                js4 = j % 3
                P.wait("scalar", vc, rv)
                P.wait("scalar", vc, M.get(("r4t", jsl), 0))
                P.wait("scalar", vc, M.get(("resid2", 6 + jsl), 0))
                P.op("scalar", lambda s, ji=ji, jsl=jsl: s.activation(
                    t_sb[:, D * jsl:D * jsl + D], y_sb[:, D * ji:D * ji + D],
                    AF.Identity, bias=stats[:, NM4 + ji:NM4 + ji + 1],
                    scale=stats[:, RS4 + ji:RS4 + ji + 1]), ac)
                M[("ln4apply", j)] = ac.n
                P.wait("vector", ac, M[("ln4apply", j)])
                P.wait("vector", vc, M[("resid2", j)])
                if js4 in ost_war:
                    P.wait("vector", ost_war[js4][0], ost_war[js4][1])
                P.op("vector", lambda v, j=j, jsl=jsl, js4=js4: v.tensor_tensor(
                    ostg[:, D * js4:D * js4 + D], t_sb[:, D * jsl:D * jsl + D],
                    x_tm[:, D * j:D * j + D], ALU.add), vc)
                M[("resid4", j)] = vc.n
                M[("r4t", jsl)] = vc.n
                if ji > 0:
                    emit_store(u, j - 1, (j - 1) % 3)
                    if u + 1 < nu:
                        emit_xload(u + 1, j - 1, (vc, M[("resid4", j - 1)]))
            M[("ln4tail",)] = ac.n
            emit_store(u, 4 * grp + 3, (4 * grp + 3) % 3)
            if u + 1 < nu:
                emit_xload(u + 1, 4 * grp + 3,
                           (vc, M[("resid4", 4 * grp + 3)]))
            if u + 1 < nu and grp == 0:
                for ji in range(4):
                    emit_ln1_stats(ji)
                M[("rst1a", u + 1)] = emit_batch_rstd(MV, 0, 4)

        P.op("gpsimd", lambda g: g.dma_start(out=identb[:], in_=c_identb[:, :]), dw, 16)
        P.op("gpsimd", lambda g: g.dma_start(out=ident8[:], in_=c_ident8[:, :]), dw, 16)
        P.op("gpsimd", lambda g: g.dma_start(out=selh[:], in_=c_selh[:, :]), dw, 16)
        P.op("gpsimd", lambda g: g.dma_start(out=bch[:], in_=c_bch[:, :]), dw, 16)
        P.op("gpsimd", lambda g: g.dma_start(out=onesb[:], in_=c_onesb[:, :]), dw, 16)
        P.op("gpsimd", lambda g: g.dma_start(
            out=bo_rowb[:], in_=bo_d.rearrange("(a d) -> a d", a=1)), dw, 16)
        for j in range(8):
            emit_xload(0, j)
            emit_ln1_stats(j)
        M[("rst1a", 0)] = emit_batch_rstd(MV, 0, 4)
        M[("rst1b", 0)] = emit_batch_rstd(MV, 4, 8)

        with nc.psum_tensor("p_b8", [128, 8], F32) as p_b8:
            for bias_src, bias_dst in ((bq_d, bqT), (bk_d, bkT), (bv_d, bvT)):
                P.op("gpsimd", lambda g, s=bias_src: g.dma_start(
                    out=b8[:], in_=s.rearrange("(g k) -> g k", k=128)), dw, 16)
                P.wait("tensor", dw)
                P.wait("tensor", vc)
                P.op("tensor", lambda t: t.transpose(p_b8[:], b8[:], ident8[:]), tc)
                P.wait("vector", tc)
                P.op("vector", lambda v, bdst=bias_dst: v.tensor_copy(bdst[:], p_b8[:]), vc)
                P.wait("gpsimd", tc)
            M[("tfE", -2)] = vc.n
            M[("tfE", -1)] = vc.n

        scrq_stores = []
        for rr in range(4):
            si = rr % 2
            slot = si * 8192
            if rr >= 2:
                P.wait("gpsimd", dstc, 32)
            for wi, wsrc in enumerate((wq_d, wk_d, wv_d)):
                P.op("gpsimd", lambda g, s=wsrc, rr=rr, slot=slot, wi=wi: g.dma_start(
                    out=qkvo[:, slot + 2048 * wi: slot + 2048 * wi + 2048]
                        .rearrange("p (c o) -> p c o", o=256),
                    in_=s.rearrange("(c p) o -> p c o", p=128)[:, :, 256 * rr:256 * rr + 256]),
                    dwq[si], 16)
            P.op("gpsimd", lambda g, rr=rr, slot=slot: g.dma_start(
                out=qkvo[:, slot + 6144: slot + 8192]
                    .rearrange("p (c o) -> p c o", o=1024),
                in_=wo_d.rearrange("(c p) o -> p c o", p=128)[:, 2 * rr:2 * rr + 2, :]),
                dwq[si], 16)
            def _sq(rr=rr, slot=slot, si=si, dn=None):
                P.wait("scalar", dwq[si], dn)
                P.op("scalar", lambda s, rr=rr, slot=slot: s.dma_start(
                    out=scr_q[rr], in_=qkvo[:, slot:slot + 8192]), dstc, 16)
            _sq.__defaults__ = (rr, slot, si, dwq[si].n)
            scrq_stores.append(_sq)
            M[("scrq", rr)] = 32 * ((rr // 2) + 1)

        with nc.psum_tensor("p_tr0", [128, D], BF16) as p_tr0, \
             nc.psum_tensor("p_tr1", [128, D], BF16) as p_tr1:
            p_tr = (p_tr0, p_tr1)
            for wsi, (w_src, w_scr, w_key) in enumerate(
                    ((wg_d, scr_g, "scrg"), (wu_d, scr_u, "scru"))):
                for hc in range(HCN):
                    gi = wsi * HCN + hc
                    sl2 = gi % 2
                    sl3i = gi % 3
                    nat = xhB[:, sl2 * D:sl2 * D + D]
                    stg = xhB[:, 2048 + sl2 * D: 2048 + sl2 * D + D]
                    ps = p_tr[sl2]
                    P.wait("sync", vc, M.get(("tfc", gi - 3), 0))
                    P.op("sync", lambda sy, s=w_src, hc=hc, sl3i=sl3i: sy.dma_start(
                        out=tfi[:, D * sl3i:D * sl3i + D],
                        in_=s[128 * hc:128 * hc + 128, :]), dtf[sl3i], 16)
                    P.wait("vector", dtf[sl3i])
                    P.wait("vector", tc, M.get(("tfT", gi - 2), 0))
                    P.op("vector", lambda v, nat=nat, sl3i=sl3i: v.tensor_copy(
                        nat, tfi[:, D * sl3i:D * sl3i + D]), vc)
                    M[("tfc", gi)] = vc.n
                    P.wait("tensor", vc, M[("tfc", gi)])
                    P.wait("tensor", vc, M.get(("tfE", gi - 2), 0))
                    for kcc in range(KC):
                        P.op("tensor",
                             lambda t, ps=ps, kcc=kcc, nat=nat: t.transpose(
                                 ps[:, 128 * kcc:128 * kcc + 128],
                                 nat[:, 128 * kcc:128 * kcc + 128], identb[:]),
                             tc if kcc == KC - 1 else None)
                    M[("tfT", gi)] = tc.n
                    P.wait("vector", tc, M[("tfT", gi)])
                    P.wait("vector", dst2[sl2])
                    P.op("vector", lambda v, ps=ps, stg=stg: v.tensor_copy(stg, ps[:]), vc)
                    M[("tfE", gi)] = vc.n
                    P.wait("scalar", vc, M[("tfE", gi)])
                    P.op("scalar", lambda s, scr=w_scr, hc=hc, stg=stg: s.dma_start(
                        out=scr[hc], in_=stg), dst2[sl2], 16)
                    M[(w_key, hc)] = (dst2[sl2], dst2[sl2].n)
                scrq_stores.pop(0)()
                scrq_stores.pop(0)()
            for dc in range(KC):
                sl2 = dc % 2
                nat = h_sb[:, sl2 * HID:sl2 * HID + HID]
                stg = h_sb[:, 2 * HID + sl2 * HID: 2 * HID + sl2 * HID + HID]
                for r in range(3):
                    gi2 = 2 * HCN + 3 * dc + r
                    sl3 = gi2 % 3
                    ps = p_tr[r % 2]
                    P.wait("sync", vc, M.get(("tfc", gi2 - 3), 0))
                    P.op("sync", lambda sy, dc=dc, r=r, sl3=sl3: sy.dma_start(
                        out=tfi[:, D * sl3:D * sl3 + D],
                        in_=wd_d[128 * dc:128 * dc + 128, D * r:D * r + D]),
                        dtf[sl3], 16)
                    P.wait("vector", dtf[sl3])
                    P.wait("vector", tc, M.get(("wdT", dc - 2), 0))
                    P.op("vector", lambda v, nat=nat, r=r, sl3=sl3: v.tensor_copy(
                        nat[:, D * r:D * r + D], tfi[:, D * sl3:D * sl3 + D]), vc)
                    M[("tfc", gi2)] = vc.n
                    P.wait("tensor", vc, M[("tfc", gi2)])
                    P.wait("tensor", vc)
                    for i in range(8):
                        hc = 8 * r + i
                        P.op("tensor",
                             lambda t, ps=ps, i=i, hc=hc, nat=nat: t.transpose(
                                 ps[:, 128 * i:128 * i + 128],
                                 nat[:, 128 * hc:128 * hc + 128], identb[:]),
                             tc if i == 7 else None)
                    P.wait("vector", tc)
                    P.wait("vector", dst2[sl2])
                    P.op("vector", lambda v, ps=ps, stg=stg, r=r: v.tensor_copy(
                        stg[:, D * r:D * r + D], ps[:]), vc)
                M[("wdT", dc)] = tc.n
                P.wait("scalar", vc)
                P.op("scalar", lambda s, dc=dc, stg=stg: s.dma_start(
                    out=scr_d[:, :, 128 * dc:128 * dc + 128].rearrange("h p d -> p h d"),
                    in_=stg.rearrange("p (h d) -> p h d", d=128)), dst2[sl2], 16)

        startup_dst = dstc.n
        startup_v = vc.n
        ost_war = {}
        tail_pending = [None]
        M[("ptr_evac", 0)] = vc.n
        M[("ptr_evac", 1)] = vc.n

        for u in range(n_units):
            b_local, half = u // 2, u % 2
            nbase = 2 * half

            with nc.psum_tensor(f"ptr{u}", [128, 2048], BF16) as ptrU, \
                 contextlib.ExitStack() as pctx:
                pq = pctx.enter_context(nc.psum_tensor(f"pq{u}", [128, 512], F32))
                pkv = pctx.enter_context(nc.psum_tensor(f"pkv{u}", [128, 512], F32))
                psc = pctx.enter_context(nc.psum_tensor(f"psc{u}", [2, 512], F32))
                ppb = pctx.enter_context(nc.psum_tensor(f"ppb{u}", [128, 512], F32))
                pao = pctx.enter_context(nc.psum_tensor(f"pao{u}", [128, D], F32))

                def load_wqk(rr, u=u):
                    si = (u * 4 + rr) % 2
                    slot = si * 8192
                    if u * 4 + rr >= 2:
                        P.wait("sync", tc, M[("qkv_mm_done", u * 4 + rr - 2)])
                        if u == 0:
                            P.wait("sync", dstc, 64)
                    else:
                        P.wait("sync", dstc, 64)
                    P.op("sync", lambda sy, rr=rr, slot=slot: sy.dma_start(
                        out=qkvo[:, slot:slot + 4096], in_=scr_q[rr][:, 0:4096]),
                        dq[si], 16)

                def load_wv(rr, u=u):
                    si = (u * 4 + rr) % 2
                    slot = si * 8192
                    if u * 4 + rr >= 2:
                        P.wait("sync", tc, M[("vmm", u * 4 + rr - 2)])
                        if u == 0:
                            P.wait("sync", dstc, 64)
                    else:
                        P.wait("sync", dstc, 64)
                    P.op("sync", lambda sy, rr=rr, slot=slot: sy.dma_start(
                        out=qkvo[:, slot + 4096:slot + 6144], in_=scr_q[rr][:, 4096:6144]),
                        dq[si], 16)

                def load_wo_part(rr, u=u):
                    si = (u * 4 + rr) % 2
                    slot = si * 8192
                    if u * 4 + rr >= 2:
                        P.wait("sync", tc, M[("wo_mm_done", u * 4 + rr - 2)])
                        if u == 0:
                            P.wait("sync", dstc, 64)
                    else:
                        P.wait("sync", dstc, 64)
                    P.op("sync", lambda sy, rr=rr, slot=slot: sy.dma_start(
                        out=qkvo[:, slot + 6144:slot + 8192], in_=scr_q[rr][:, 6144:8192]),
                        dq[si], 16)

                def emit_one_proj(rr, wi, dst_ap, u=u):
                    si = (u * 4 + rr) % 2
                    gslot = si * 8192
                    for ch in range(2):
                        for kcc in range(KC):
                            P.op("tensor", lambda t, d2=dst_ap, wi=wi, ch=ch, kcc=kcc,
                                 gslot=gslot, rr=rr: t.matmul(
                                     d2[:, 256 * ch:256 * ch + 256],
                                     qkvo[:, gslot + 2048 * wi + 256 * kcc + 128 * ch:
                                          gslot + 2048 * wi + 256 * kcc + 128 * ch + 128],
                                     xhB[:, 1024 * kcc + 256 * rr:
                                          1024 * kcc + 256 * rr + 256],
                                     start=(ch == 0 and kcc == 0),
                                     stop=(ch == 1 and kcc == KC - 1),
                                     skip_group_check=True),
                                 tc if (ch == 1 and kcc == KC - 1) else None)
                    M[("mm", "qkv"[wi], rr)] = tc.n

                def emit_qk_mm(rr, u=u, pq=pq, pkv=pkv):
                    si = (u * 4 + rr) % 2
                    P.wait("tensor", vc, M[("xhtile", 2 * rr + 1)])
                    P.wait("tensor", dq[si])
                    P.wait("tensor", ac, M.get(("evQ", u * 4 + rr - 1), 0))
                    P.wait("tensor", ac, M.get(("evV", u * 4 + rr - 1), 0))
                    emit_one_proj(rr, 0, pq[:])
                    emit_one_proj(rr, 1, pkv[:])
                    M[("qkv_mm_done", u * 4 + rr)] = tc.n

                load_wqk(0)
                load_wv(0)
                load_wo_part(0)
                load_wqk(1)
                load_wv(1)
                load_wo_part(1)
                b2war = M.get(("yevacG1", u - 1), 0)
                if u == 0:
                    P.wait("vector", dst2[0])
                    P.wait("vector", dst2[1])
                for j in (0, 1):
                    emit_xhat_tile(j, ptrU, M[("rst1a", u)], M.get(("b1_done", u - 1), 0), b2war)
                emit_qk_mm(0)
                load_wqk(2)
                for j in (2, 3):
                    emit_xhat_tile(j, ptrU, M[("rst1a", u)], M.get(("b1_done", u - 1), 0), b2war)
                if tail_pending[0] is not None:
                    tail_pending[0]()
                    tail_pending[0] = None
                P.wait("scalar", tc, M[("mm", "q", 0)])
                P.wait("scalar", gc, M.get(("qkprod", -1), 0))
                for ch in range(2):
                    P.op("scalar", lambda s, ch=ch, pq=pq: s.activation(
                        q_sb[:, 256 * ch:256 * ch + 256], pq[:, 256 * ch:256 * ch + 256],
                        AF.Identity, bias=bqT[:, ch:ch + 1]), ac)
                M[("evQ", u * 4)] = ac.n
                P.wait("scalar", tc, M[("mm", "k", 0)])
                for ch in range(2):
                    P.op("scalar", lambda s, ch=ch, pkv=pkv: s.activation(
                        k_sb[:, 256 * ch:256 * ch + 256], pkv[:, 256 * ch:256 * ch + 256],
                        AF.Identity, bias=bkT[:, ch:ch + 1]), ac)
                M[("evK", u * 4)] = ac.n
                if u > 0:
                    for j in (4, 5, 6, 7):
                        emit_ln1_stats(j)
                    M[("rst1b", u)] = emit_batch_rstd(MV, 4, 8)

                def emit_resid2(j, u=u):
                    jsl = j % 2
                    P.wait("vector", ac, M[("ln2apply", j)])
                    P.op("vector", lambda v, j=j, jsl=jsl: v.tensor_tensor(
                        x_tm[:, D * j:D * j + D], x_tm[:, D * j:D * j + D],
                        t_sb[:, D * jsl:D * jsl + D], ALU.add), vc)
                    M[("resid2", j)] = vc.n
                    P.wait("vector", vc)
                    xj = x_tm[:, D * j:D * j + D]
                    P.op("vector", lambda v, xj=xj: v.bn_stats(bn2_sb[:, 0:6], xj[:, 0:512]))
                    P.op("vector", lambda v, xj=xj: v.bn_stats(bn2_sb[:, 6:12], xj[:, 512:1024]), vc)
                    P.wait("vector", vc)
                    P.op("vector", lambda v, j=j: v.bn_aggr(
                        stats[:, MV3 + 2 * j:MV3 + 2 * j + 2], bn2_sb[:, 0:12]), vc)
                    if j == 3:
                        M[("rst3a", u)] = emit_batch_rstd(MV3, 0, 4)
                    elif j == 5:
                        M[("rst3b1", u)] = emit_batch_rstd(MV3, 4, 6)
                    elif j == 7:
                        M[("rst3b2", u)] = emit_batch_rstd(MV3, 6, 8)

                for rr in range(4):
                    si = (u * 4 + rr) % 2
                    gslot = si * 8192
                    vsl = rr % 2
                    P.wait("tensor", ac, M[("evK", u * 4 + rr)])
                    P.wait("tensor", dq[si])
                    emit_one_proj(rr, 2, pkv[:])
                    M[("vmm", u * 4 + rr)] = tc.n
                    if rr + 2 < 4:
                        load_wv(rr + 2)
                    P.wait("scalar", tc, M[("mm", "v", rr)])
                    P.wait("scalar", vc, M.get(("Amul", rr - 2), 0))
                    for ch in range(2):
                        P.op("scalar", lambda s, ch=ch, rr=rr, vsl=vsl, pkv=pkv: s.activation(
                            v_sb[:, 512 * vsl + 256 * ch:512 * vsl + 256 * ch + 256],
                            pkv[:, 256 * ch:256 * ch + 256],
                            AF.Identity, bias=bvT[:, 2 * rr + ch:2 * rr + ch + 1]), ac)
                    M[("evV", u * 4 + rr)] = ac.n

                    P.wait("gpsimd", ac, M[("evK", u * 4 + rr)])
                    P.wait("gpsimd", tc, M.get(("sc", rr - 2), 0))
                    P.op("gpsimd", lambda g, vsl=vsl: g.tensor_mul(
                        qk_bf[:, 512 * vsl:512 * vsl + 512], q_sb[:], k_sb[:]), gc)
                    M[("qkprod", rr)] = gc.n

                    P.wait("tensor", gc, M[("qkprod", rr)])
                    P.wait("tensor", ac, M.get(("exps", rr - 1), 0))
                    for ch in range(2):
                        P.op("tensor", lambda t, ch=ch, vsl=vsl, psc=psc: t.matmul(
                            psc[:, 256 * ch:256 * ch + 256], selh[:],
                            qk_bf[:, 512 * vsl + 256 * ch:512 * vsl + 256 * ch + 256],
                            start=(ch == 0), stop=(ch == 1), skip_group_check=True),
                            tc if ch == 1 else None)
                    M[("sc", rr)] = tc.n

                    so = 16 * vsl
                    P.wait("vector", tc, M[("sc", rr)])
                    P.wait("vector", ac, M.get(("exps", rr - 2), 0))
                    P.op("vector", lambda v, so=so, psc=psc: v.reduce_max(
                        smx[:, so:so + 4], psc[:].rearrange("p (b s) -> p b s", s=128),
                        axis=AX.X), vc)
                    P.wait("vector", vc)
                    P.op("vector", lambda v, so=so: v.tensor_scalar(
                        smx[:, so + 4:so + 8], smx[:, so:so + 4], -SCL, None, ALU.mult), vc)
                    P.wait("scalar", vc)
                    P.wait("scalar", tc, M.get(("pb", rr - 2), 0))
                    for c in range(4):
                        P.op("scalar", lambda s, c=c, so=so, vsl=vsl, psc=psc: s.activation(
                            p_sb[:, 512 * vsl + 128 * c:512 * vsl + 128 * c + 128],
                            psc[:, 128 * c:128 * c + 128],
                            AF.Exp, scale=SCL, bias=smx[:, so + 4 + c:so + 5 + c],
                            accum_out=smx[:, so + 8 + c:so + 9 + c]), ac)
                    M[("exps", rr)] = ac.n
                    if rr + 1 < 4:
                        emit_qk_mm(rr + 1)
                        P.wait("scalar", tc, M[("mm", "q", rr + 1)])
                        P.wait("scalar", gc, M.get(("qkprod", rr), 0))
                        for ch in range(2):
                            P.op("scalar", lambda s, ch=ch, rr2=rr + 1, pq=pq: s.activation(
                                q_sb[:, 256 * ch:256 * ch + 256], pq[:, 256 * ch:256 * ch + 256],
                                AF.Identity, bias=bqT[:, 2 * rr2 + ch:2 * rr2 + ch + 1]), ac)
                        M[("evQ", u * 4 + rr + 1)] = ac.n
                        P.wait("scalar", tc, M[("mm", "k", rr + 1)])
                        for ch in range(2):
                            P.op("scalar", lambda s, ch=ch, rr2=rr + 1, pkv=pkv: s.activation(
                                k_sb[:, 256 * ch:256 * ch + 256], pkv[:, 256 * ch:256 * ch + 256],
                                AF.Identity, bias=bkT[:, 2 * rr2 + ch:2 * rr2 + ch + 1]), ac)
                        M[("evK", u * 4 + rr + 1)] = ac.n
                        if rr < 2:
                            b2war = M.get(("yevacG1", u - 1), 0)
                            for j in (4 + 2 * rr, 5 + 2 * rr):
                                emit_xhat_tile(j, ptrU, M[("rst1b", u)],
                                               M.get(("b1_done", u - 1), 0), b2war)
                        if rr + 3 < 4:
                            load_wqk(rr + 3)
                    if rr == 3:
                        for j3 in (0, 1, 2, 3):
                            emit_xhat_tile(j3, ptrU, M[("rst3a", u)],
                                           M[("qkv_mm_done", u * 4 + j3 // 2)])
                        M[("xh3nt", 0)] = vc.n

                    P.wait("vector", ac, M[("exps", rr)])
                    P.op("vector", lambda v, so=so: v.reciprocal(
                        smx[:, so + 12:so + 16], smx[:, so + 8:so + 12]), vc)
                    P.wait("vector", vc)
                    for c in range(4):
                        P.op("vector", lambda v, c=c, so=so, vsl=vsl: v.tensor_scalar(
                            p_sb[:, 512 * vsl + 128 * c:512 * vsl + 128 * c + 128],
                            p_sb[:, 512 * vsl + 128 * c:512 * vsl + 128 * c + 128],
                            smx[:, so + 12 + c:so + 13 + c], None, ALU.mult),
                            vc if c == 3 else None)
                    pnorm_v = vc.n

                    P.wait("tensor", vc, pnorm_v)
                    P.wait("tensor", vc, M.get(("Amul", rr - 1), 0))
                    for ch in range(2):
                        P.op("tensor", lambda t, ch=ch, vsl=vsl, ppb=ppb: t.matmul(
                            ppb[:, 256 * ch:256 * ch + 256], bch[:],
                            p_sb[:, 512 * vsl + 256 * ch:512 * vsl + 256 * ch + 256],
                            start=(ch == 0), stop=(ch == 1), skip_group_check=True),
                            tc if ch == 1 else None)
                    M[("pb", rr)] = tc.n

                    P.wait("vector", tc, M[("pb", rr)])
                    P.wait("vector", ac, M[("evV", u * 4 + rr)])
                    P.wait("vector", tc, M.get(("wo_mm_done", u * 4 + rr - 2), 0))
                    P.op("vector", lambda v, vsl=vsl, ppb=ppb: v.tensor_mul(
                        a_bf[:, 512 * vsl:512 * vsl + 512], ppb[:],
                        v_sb[:, 512 * vsl:512 * vsl + 512]), vc)
                    M[("Amul", rr)] = vc.n

                    for s2 in range(2):
                        j = 2 * rr + s2
                        jsl = j % 2
                        mvp, ltp, rsp, nmp = MV2P(jsl), LT2P(jsl), RS2P(jsl), NM2P(jsl)
                        P.wait("tensor", vc, M[("Amul", rr)])
                        for hf in range(2):
                            P.wait("tensor", ac, M.get(("aoc", 2 * rr + s2 - 1, hf), 0))
                            P.op("tensor", lambda t, hf=hf, pao=pao: t.matmul(
                                pao[:, 512 * hf:512 * hf + 512], onesb[:],
                                bo_rowb[:, 512 * hf:512 * hf + 512],
                                start=True, stop=False, skip_group_check=True), None)
                            for ch in range(2):
                                P.op("tensor", lambda t, hf=hf, ch=ch, s2=s2, vsl=vsl,
                                     gslot=gslot, pao=pao: t.matmul(
                                         pao[:, 512 * hf:512 * hf + 512],
                                         a_bf[:, 512 * vsl + 256 * ch + 128 * s2:
                                              512 * vsl + 256 * ch + 128 * s2 + 128],
                                         qkvo[:, gslot + 6144 + 1024 * ch + 512 * hf:
                                              gslot + 6144 + 1024 * ch + 512 * hf + 512],
                                         start=False, stop=(ch == 1),
                                         skip_group_check=True),
                                     tc if ch == 1 else None)
                            M[("wo_h", j, hf)] = tc.n
                        wo_t = tc.n
                        if s2 == 1:
                            M[("wo_mm_done", u * 4 + rr)] = wo_t

                        P.wait("scalar", vc, M.get(("aostats", j - 2), 0))
                        for hf in range(2):
                            P.wait("scalar", tc, M[("wo_h", j, hf)])
                            P.op("scalar", lambda s, jsl=jsl, hf=hf, pao=pao: s.activation(
                                ao_sb[:, D * jsl + 512 * hf:D * jsl + 512 * hf + 512],
                                pao[:, 512 * hf:512 * hf + 512], AF.Copy), ac)
                            M[("aoc", 2 * rr + s2, hf)] = ac.n
                        M[("aocopy", 2 * rr + s2)] = ac.n
                        P.wait("vector", ac, M[("aocopy", 2 * rr + s2)])
                        P.wait("vector", vc)
                        P.op("vector", lambda v, jsl=jsl: v.bn_stats(
                            bn_sb[:, 12 * jsl:12 * jsl + 6], ao_sb[:, D * jsl:D * jsl + 512]))
                        P.op("vector", lambda v, jsl=jsl: v.bn_stats(
                            bn_sb[:, 12 * jsl + 6:12 * jsl + 12],
                            ao_sb[:, D * jsl + 512:D * jsl + 1024]), vc)
                        M[("aostats", j)] = vc.n
                        P.wait("vector", vc)
                        P.wait("vector", ac, M.get(("lnexp2", j - 2), 0))
                        P.op("vector", lambda v, mvp=mvp, jsl=jsl: v.bn_aggr(
                            stats[:, mvp:mvp + 2], bn_sb[:, 12 * jsl:12 * jsl + 12]), vc)
                        P.wait("scalar", vc)
                        P.op("scalar", lambda s, mvp=mvp, ltp=ltp: s.activation(
                            stats[:, ltp:ltp + 1], stats[:, mvp + 1:mvp + 2],
                            AF.Ln, bias=eps_c[:, 0:1]), ac)
                        P.wait("scalar", ac)
                        P.op("scalar", lambda s, ltp=ltp, rsp=rsp: s.activation(
                            stats[:, rsp:rsp + 1], stats[:, ltp:ltp + 1], AF.Exp, scale=-0.5), ac)
                        M[("lnexp2", j)] = ac.n
                        P.wait("vector", ac)
                        P.wait("vector", ac, M.get(("ln2apply", j - 2), 0))
                        P.op("vector", lambda v, mvp=mvp, rsp=rsp, nmp=nmp: v.scalar_tensor_tensor(
                            stats[:, nmp:nmp + 1], stats[:, mvp:mvp + 1], -1.0,
                            stats[:, rsp:rsp + 1], ALU.mult, ALU.mult), vc)
                        P.wait("scalar", vc)
                        P.wait("scalar", vc, M.get(("resid2", j - 2), 0))
                        P.wait("scalar", vc, M.get(("r4t", jsl), 0))
                        P.op("scalar", lambda s, jsl=jsl, rsp=rsp, nmp=nmp: s.activation(
                            t_sb[:, D * jsl:D * jsl + D], ao_sb[:, D * jsl:D * jsl + D],
                            AF.Identity, bias=stats[:, nmp:nmp + 1],
                            scale=stats[:, rsp:rsp + 1]), ac)
                        M[("ln2apply", j)] = ac.n
                        emit_resid2(j)

                    if rr + 2 < 4:
                        load_wo_part(rr + 2)

            jobs = [(0, 0), (1, 0), (0, 1), (1, 1)] + \
                   [(hc, nt) for hc in range(2, HCN) for nt in range(2)]
            with nc.psum_tensor(f"ptr_{u}", [128, 2048], BF16) as ptrU2, \
                 nc.psum_tensor(f"pg{u}", [128, 3 * 512], F32) as pg, \
                 nc.psum_tensor(f"pu{u}", [128, 3 * 512], F32) as pu:
                loaded = set()
                for ji, (hc, nt) in enumerate(jobs):
                    slot = hc % 4
                    if hc not in loaded:
                        loaded.add(hc)
                        if ("rgu", u, hc - 4) in M:
                            P.wait("sync", tc, M[("rgu", u, hc - 4)])
                        elif u > 0 and hc < 4:
                            P.wait("sync", tc, M[("ring_gu", HCN - 4 + hc)])
                        if u == 0:
                            P.wait("sync", *M[("scrg", hc)])
                            P.wait("sync", *M[("scru", hc)])
                        P.op("sync", lambda sy, hc=hc, slot=slot: sy.dma_start(
                            out=ring_g[:, D * slot:D * slot + D], in_=scr_g[hc]), dsg[slot], 16)
                        P.op("sync", lambda sy, hc=hc, slot=slot: sy.dma_start(
                            out=ring_u[:, D * slot:D * slot + D], in_=scr_u[hc]), dsg[slot], 16)

                    P.wait("tensor", dsg[slot])
                    gsl = ji % 3
                    ssl = ji % 2
                    P.wait("tensor", vc, M.get(("xh3nt", nt), 0))
                    P.wait("tensor", vc, M.get(("hmulB", u, ji - 3), 0))
                    for kcc in range(KC):
                        P.op("tensor", lambda t, slot=slot, gsl=gsl, nt=nt, kcc=kcc, pg=pg: t.matmul(
                            pg[:, 512 * gsl:512 * gsl + 512],
                            ring_g[:, D * slot + 128 * kcc:D * slot + 128 * kcc + 128],
                            xhB[:, 1024 * kcc + 512 * nt:1024 * kcc + 512 * nt + 512],
                            start=(kcc == 0), stop=(kcc == KC - 1)), None)
                    for kcc in range(KC):
                        P.op("tensor", lambda t, slot=slot, gsl=gsl, nt=nt, kcc=kcc, pu=pu: t.matmul(
                            pu[:, 512 * gsl:512 * gsl + 512],
                            ring_u[:, D * slot + 128 * kcc:D * slot + 128 * kcc + 128],
                            xhB[:, 1024 * kcc + 512 * nt:1024 * kcc + 512 * nt + 512],
                            start=(kcc == 0), stop=(kcc == KC - 1)),
                            tc if kcc == KC - 1 else None)
                    if nt == 1:
                        M[("rgu", u, hc)] = tc.n
                    gu_t = tc.n
                    P.wait("scalar", tc, gu_t)
                    P.wait("scalar", vc, M.get(("hmulB", u, ji - 2), 0))
                    P.op("scalar", lambda s, gsl=gsl, ssl=ssl, pg=pg: s.activation(
                        silu_sb[:, 512 * ssl:512 * ssl + 512],
                        pg[:, 512 * gsl:512 * gsl + 512], AF.Silu, scale=SDQ), ac)
                    silu_a = ac.n
                    P.wait("vector", ac, silu_a)
                    P.op("vector", lambda v, gsl=gsl, ssl=ssl, hc=hc, nt=nt, pu=pu: v.tensor_mul(
                        h_sb[:, D * hc + 512 * nt:D * hc + 512 * nt + 512],
                        silu_sb[:, 512 * ssl:512 * ssl + 512],
                        pu[:, 512 * gsl:512 * gsl + 512]), vc)
                    M[("hmulB", u, ji)] = vc.n
                    if ji == 0:
                        for j in (4, 5):
                            emit_xhat_tile(j, ptrU2, M[("rst3b1", u)],
                                           M[("qkv_mm_done", u * 4 + j // 2)])
                    elif ji == 1:
                        for j in (6, 7):
                            emit_xhat_tile(j, ptrU2, M[("rst3b2", u)],
                                           M[("qkv_mm_done", u * 4 + j // 2)])
                        M[("xh3nt", 1)] = vc.n
                for back in range(4):
                    M[("ring_gu", HCN - 4 + back)] = M[("rgu", u, HCN - 4 + back)]
                M[("b1_done", u)] = tc.n
                h_ready_v = vc.n

            with contextlib.ExitStack() as pctx:
                pd = [pctx.enter_context(nc.psum_tensor(f"pd{u}_{i}", [128, D], F32))
                      for i in range(4)]
                for grp in range(2):
                    for kcc in range(HCN):
                        slot = kcc % 4
                        if (u, grp, kcc - 4) in M:
                            P.wait("sync", tc, M[(u, grp, kcc - 4)])
                        elif grp == 1 and kcc < 4:
                            P.wait("sync", tc, M[(u, 0, HCN - 4 + kcc)])
                        elif u > 0 and grp == 0 and kcc < 4:
                            P.wait("sync", tc, M[("ring_d", HCN - 4 + kcc)])
                        elif u == 0 and grp == 0 and kcc < 4:
                            P.wait("sync", dst2[0])
                            P.wait("sync", dst2[1])
                        P.op("sync", lambda sy, kcc=kcc, slot=slot: sy.dma_start(
                            out=ring_d[:, D * slot:D * slot + D], in_=scr_d[kcc]), drd[slot], 16)
                        P.wait("tensor", drd[slot])
                        if kcc == 0:
                            P.wait("tensor", vc, h_ready_v)
                        for ji in range(4):
                            j = 4 * grp + ji
                            if kcc == 0 and grp == 1:
                                P.wait("tensor", vc, M[("yevac", ji)])
                            for hf in range(2):
                                P.op("tensor", lambda t, ji=ji, j=j, hf=hf, kcc=kcc, slot=slot, pd=pd: t.matmul(
                                    pd[ji][:, 512 * hf:512 * hf + 512],
                                    h_sb[:, D * kcc + 128 * j:D * kcc + 128 * j + 128],
                                    ring_d[:, D * slot + 512 * hf:D * slot + 512 * hf + 512],
                                    start=(kcc == 0), stop=(kcc == HCN - 1)),
                                    tc if (ji == 3 and hf == 1) else None)
                        M[(u, grp, kcc)] = tc.n
                        if grp == 1 and kcc >= HCN - 4:
                            M[("ring_d", kcc)] = tc.n
                    dn_done_t = tc.n
                    for ji in range(4):
                        P.wait("vector", tc, dn_done_t)
                        P.wait("vector", ac,
                               M.get(("ln4apply", (4 * grp - 4 + ji) % 8), 0))
                        P.op("vector", lambda v, ji=ji, pd=pd: v.tensor_copy(
                            y_sb[:, D * ji:D * ji + D], pd[ji][:]), vc)
                        M[("yevac", ji)] = vc.n
                        if grp == 1 and ji == 0:
                            M[("yevacG1", u)] = vc.n
                    if grp == 1:
                        M[("b2yevac", u)] = vc.n
                    if grp == 0:
                        emit_grp_tail(u, 0)
                    else:
                        tail_pending[0] = lambda u=u: emit_grp_tail(u, 1)

        if tail_pending[0] is not None:
            tail_pending[0]()
            tail_pending[0] = None
        for dsem in dos:
            P.wait("gpsimd", dsem, dsem.n)
            P.wait("sync", dsem, dsem.n)

        with nc.Block() as block:
            @block.tensor
            def _(eng):
                for fn in P.streams["tensor"]:
                    fn(eng)

            @block.vector
            def _(eng):
                for fn in P.streams["vector"]:
                    fn(eng)

            @block.scalar
            def _(eng):
                for fn in P.streams["scalar"]:
                    fn(eng)

            @block.sync
            def _(eng):
                for fn in P.streams["sync"]:
                    fn(eng)

            @block.gpsimd
            def _(eng):
                for fn in P.streams["gpsimd"]:
                    fn(eng)

    return nc


def _host_inputs(inputs):
    perm = np.concatenate([
        np.arange(64 * (rr + 4 * hj), 64 * (rr + 4 * hj) + 64)
        for rr in range(4) for hj in range(4)])
    selh = np.zeros((128, 2), dtype=BFNP)
    selh[0:64, 0] = 1
    selh[64:128, 1] = 1
    bch = np.zeros((2, 128), dtype=BFNP)
    bch[0, 0:64] = 1
    bch[1, 64:128] = 1
    return {
        "wq": np.ascontiguousarray(inputs["L0_attn_Wq"][:, perm]),
        "wk": np.ascontiguousarray(inputs["L0_attn_Wk"][:, perm]),
        "wv": np.ascontiguousarray(inputs["L0_attn_Wv"][:, perm]),
        "wo": np.ascontiguousarray(inputs["L0_attn_Wo"][perm, :]),
        "bq": np.ascontiguousarray(inputs["L0_attn_bq"][perm]),
        "bk": np.ascontiguousarray(inputs["L0_attn_bk"][perm]),
        "bv": np.ascontiguousarray(inputs["L0_attn_bv"][perm]),
        "bo": np.ascontiguousarray(inputs["L0_attn_bo"]),
        "wg": np.ascontiguousarray(inputs["L0_ffn_w_gate"]),
        "wu": np.ascontiguousarray(inputs["L0_ffn_w_up"]),
        "wd": np.ascontiguousarray(inputs["L0_ffn_w_down"]),
        "c_identb": np.eye(128, dtype=np.float32).astype(BFNP),
        "c_ident8": np.eye(8, dtype=np.float32),
        "c_selh": selh,
        "c_bch": bch,
        "c_onesb": np.ones((1, 128), dtype=BFNP),
    }


def kernel(**inputs):
    x = np.asarray(inputs["L0_attn_x"])
    common = _host_inputs({k: np.asarray(v) for k, v in inputs.items()})
    nc = build_nc(UNITS)
    in_maps = []
    for c in range(NCORES):
        m = dict(common)
        m["x"] = np.ascontiguousarray(x[4 * c:4 * c + 4])
        in_maps.append(m)
    res = run_bass_kernel_spmd(nc, in_maps, core_ids=list(range(NCORES)))
    return np.concatenate([res.results[c]["out"] for c in range(NCORES)], axis=0)


# version 21
# speedup vs baseline: 1.0226x; 1.0003x over previous
import contextlib

import numpy as np
import ml_dtypes

import concourse.bass as bass
import concourse.mybir as mybir
from concourse.bass_utils import run_bass_kernel_spmd

F32 = mybir.dt.float32
BF16 = mybir.dt.bfloat16
I32 = mybir.dt.int32
AF = mybir.ActivationFunctionType
ALU = mybir.AluOpType
AX = mybir.AxisListType
BFNP = ml_dtypes.bfloat16

D = 1024
HID = 3072
KC = 8
HCN = 24
EPS = 1e-5
SDQ = 1.0 / 127.0
SCL = 0.125
NCORES = 8
UNITS = 8
ENGS = ("tensor", "vector", "scalar", "sync", "gpsimd")

MV = 0
RST = 16
NMR = 24
MV3 = 48
LT1 = 64
LT4 = 72
RS4 = 76
NM4 = 80
MV4 = 84


def MV2P(p):
    return 32 + 2 * p


def LT2P(p):
    return 36 + p


def RS2P(p):
    return 38 + p


def NM2P(p):
    return 42 + p


class Ctr:
    def __init__(self, name):
        self.name = name
        self.n = 0
        self.sem = None


class Rec:
    def __init__(self):
        self.streams = {e: [] for e in ENGS}
        self.lastw = {}

    def op(self, e, f, inc=None, k=1):
        if inc is not None:
            inc.n += k

        def fn(eng, f=f, inc=inc, k=k):
            ins = f(eng)
            if inc is not None:
                ins.then_inc(inc.sem, k)

        self.streams[e].append(fn)
        return inc.n if inc is not None else None

    def wait(self, e, ctr, n=None):
        t = ctr.n if n is None else n
        if t <= 0 or t <= self.lastw.get((e, ctr.name), 0):
            return
        self.lastw[(e, ctr.name)] = t
        self.streams[e].append(lambda eng, c=ctr, t=t: eng.wait_ge(c.sem, t))


def build_nc(n_units=UNITS):
    nc = bass.Bass("TRN2", target_bir_lowering=False)

    x_d = nc.dram_tensor("x", [4, 2048, D], F32, kind="ExternalInput").ap()
    out_d = nc.dram_tensor("out", [4, 2048, D], F32, kind="ExternalOutput").ap()
    wq_d = nc.dram_tensor("wq", [D, D], F32, kind="ExternalInput").ap()
    wk_d = nc.dram_tensor("wk", [D, D], F32, kind="ExternalInput").ap()
    wv_d = nc.dram_tensor("wv", [D, D], F32, kind="ExternalInput").ap()
    wo_d = nc.dram_tensor("wo", [D, D], F32, kind="ExternalInput").ap()
    bq_d = nc.dram_tensor("bq", [D], F32, kind="ExternalInput").ap()
    bk_d = nc.dram_tensor("bk", [D], F32, kind="ExternalInput").ap()
    bv_d = nc.dram_tensor("bv", [D], F32, kind="ExternalInput").ap()
    bo_d = nc.dram_tensor("bo", [D], F32, kind="ExternalInput").ap()
    wg_d = nc.dram_tensor("wg", [HID, D], I32, kind="ExternalInput").ap()
    wu_d = nc.dram_tensor("wu", [HID, D], I32, kind="ExternalInput").ap()
    wd_d = nc.dram_tensor("wd", [D, HID], I32, kind="ExternalInput").ap()
    c_identb = nc.dram_tensor("c_identb", [128, 128], BF16, kind="ExternalInput").ap()
    c_ident8 = nc.dram_tensor("c_ident8", [8, 8], F32, kind="ExternalInput").ap()
    c_selh = nc.dram_tensor("c_selh", [128, 2], BF16, kind="ExternalInput").ap()
    c_bch = nc.dram_tensor("c_bch", [2, 128], BF16, kind="ExternalInput").ap()
    c_onesb = nc.dram_tensor("c_onesb", [1, 128], BF16, kind="ExternalInput").ap()
    scr_g = nc.dram_tensor("scr_g", [HCN, 128, D], BF16, kind="Internal").ap()
    scr_u = nc.dram_tensor("scr_u", [HCN, 128, D], BF16, kind="Internal").ap()
    scr_d = nc.dram_tensor("scr_d", [HCN, 128, D], BF16, kind="Internal").ap()
    scr_q = nc.dram_tensor("scr_q", [4, 128, 8192], BF16, kind="Internal").ap()

    with contextlib.ExitStack() as ctx:
        E = ctx.enter_context

        x_tm = E(nc.sbuf_tensor("x_tm", [128, 8 * D], F32))
        xh_tm = E(nc.sbuf_tensor("xh_tm", [128, 2 * D], BF16))
        xhB = E(nc.sbuf_tensor("xhB", [128, 8 * D], BF16))
        qkvo = E(nc.sbuf_tensor("qkvo", [128, 2 * 8192], BF16))
        ring_g = E(nc.sbuf_tensor("ring_g", [128, 4 * D], BF16))
        ring_u = E(nc.sbuf_tensor("ring_u", [128, 4 * D], BF16))
        ring_d = E(nc.sbuf_tensor("ring_d", [128, 4 * D], BF16))
        h_sb = E(nc.sbuf_tensor("h_sb", [128, HCN * D], BF16))
        q_sb = E(nc.sbuf_tensor("q_sb", [128, 512], BF16))
        k_sb = E(nc.sbuf_tensor("k_sb", [128, 512], BF16))
        v_sb = E(nc.sbuf_tensor("v_sb", [128, 2 * 512], BF16))
        qk_bf = E(nc.sbuf_tensor("qk_bf", [128, 2 * 512], BF16))
        a_bf = E(nc.sbuf_tensor("a_bf", [128, 2 * 512], BF16))
        p_sb = E(nc.sbuf_tensor("p_sb", [2, 2 * 512], BF16))
        smx = E(nc.sbuf_tensor("smx", [2, 32], F32))
        ao_sb = E(nc.sbuf_tensor("ao_sb", [128, 2 * D], BF16))
        t_sb = E(nc.sbuf_tensor("t_sb", [128, 2 * D], F32))
        y_off = nc.sbuf_base
        y_sb = E(nc.sbuf_tensor("y_sb", [128, 4 * D], BF16))
        silu_sb = E(nc.sbuf_tensor("silu_sb", [128, 2 * 512], BF16))
        ostg_off = nc.sbuf_base
        ostg = E(nc.sbuf_tensor("ostg", [128, 4 * D], F32))
        tfi = nc.alloc_sbuf_tensor_at("tfi32", [128, 3 * D], I32, offset=ostg_off)
        stats = E(nc.sbuf_tensor("stats", [128, 92], F32))
        bn_sb = E(nc.sbuf_tensor("bn_sb", [128, 48], F32))
        bn2_sb = E(nc.sbuf_tensor("bn2_sb", [128, 12], F32))
        bqT = E(nc.sbuf_tensor("bqT", [128, 8], F32))
        bkT = E(nc.sbuf_tensor("bkT", [128, 8], F32))
        bvT = E(nc.sbuf_tensor("bvT", [128, 8], F32))
        b8 = nc.alloc_sbuf_tensor_at("b8", [8, 128], F32, offset=y_off)
        bo_rowb = E(nc.sbuf_tensor("bo_rowb", [1, D], BF16))
        identb = E(nc.sbuf_tensor("identb", [128, 128], BF16))
        ident8 = E(nc.sbuf_tensor("ident8", [8, 8], F32))
        selh = E(nc.sbuf_tensor("selh", [128, 2], BF16))
        bch = E(nc.sbuf_tensor("bch", [2, 128], BF16))
        onesb = E(nc.sbuf_tensor("onesb", [1, 128], BF16))
        eps_c = E(nc.sbuf_tensor("eps_c", [128, 1], F32))

        def mk(name):
            c = Ctr(name)
            c.sem = E(nc.semaphore("sem_" + name))
            return c

        dxs = [mk(f"dx{j}") for j in range(8)]
        dq = [mk(f"dq{i}") for i in range(2)]
        dsg = [mk(f"dsg{i}") for i in range(4)]
        drd = [mk(f"drd{i}") for i in range(4)]
        dos = [mk(f"do{i}") for i in range(4)]
        dw = mk("dw")
        dwq = [mk(f"dwq{i}") for i in range(2)]
        dtf = [mk(f"dtf{i}") for i in range(3)]
        dst2 = [mk(f"dst2_{i}") for i in range(4)]
        dstc = mk("dst")
        tc = mk("t")
        vc = mk("v")
        ac = mk("a")
        gc = mk("g")

        P = Rec()
        M = {}

        P.op("vector", lambda v: v.memset(eps_c[:], EPS), vc)

        def emit_xload(un, j, war=None):
            b_l, hf = un // 2, un % 2
            n = 2 * hf + (j % 2)
            rr = j // 2
            if war is not None:
                P.wait("scalar", war[0], war[1])
            P.op("scalar", lambda s, b=b_l, n=n, rr=rr, j=j: s.dma_start(
                out=x_tm[:, D * j:D * j + D],
                in_=x_d[b, n * 512 + rr:(n + 1) * 512:4, :]), dxs[j], 16)

        def emit_ln1_stats(j):
            P.wait("vector", dxs[j])
            P.wait("vector", vc)
            xj = x_tm[:, D * j:D * j + D]
            P.op("vector", lambda v, xj=xj: v.bn_stats(bn2_sb[:, 0:6], xj[:, 0:512]))
            P.op("vector", lambda v, xj=xj: v.bn_stats(bn2_sb[:, 6:12], xj[:, 512:1024]), vc)
            P.wait("vector", vc)
            P.op("vector", lambda v, j=j: v.bn_aggr(
                stats[:, MV + 2 * j:MV + 2 * j + 2], bn2_sb[:, 0:12]), vc)

        def emit_batch_rstd(src_off, lo, hi, lt0=LT1, rs0=RST, nm0=NMR):
            w = hi - lo
            P.wait("scalar", vc)
            P.op("scalar", lambda s, so=src_off, lo=lo, hi=hi, w=w, lt0=lt0: s.activation(
                stats[:, lt0 + lo:lt0 + lo + w],
                stats[:, so + 2 * lo + 1:so + 2 * hi:2],
                AF.Ln, bias=eps_c[:, 0:1]), ac)
            P.wait("scalar", ac)
            P.op("scalar", lambda s, lo=lo, w=w, lt0=lt0, rs0=rs0: s.activation(
                stats[:, rs0 + lo:rs0 + lo + w], stats[:, lt0 + lo:lt0 + lo + w],
                AF.Exp, scale=-0.5), ac)
            P.wait("vector", ac)
            P.op("vector", lambda v, so=src_off, lo=lo, hi=hi, w=w, rs0=rs0, nm0=nm0: v.scalar_tensor_tensor(
                stats[:, nm0 + lo:nm0 + lo + w], stats[:, so + 2 * lo:so + 2 * hi:2],
                -1.0, stats[:, rs0 + lo:rs0 + lo + w], ALU.mult, ALU.mult), vc)
            return vc.n

        def emit_xhat_tile(j, ptr_ps, rst_ready_v, war_t, war_v=0):
            sl = j % 2
            po = 1024 * sl
            P.wait("scalar", vc, rst_ready_v)
            P.wait("scalar", tc, M.get(("xh_slot", sl), 0))
            P.op("scalar", lambda s, j=j, sl=sl: s.activation(
                xh_tm[:, D * sl:D * sl + D], x_tm[:, D * j:D * j + D],
                AF.Identity, bias=stats[:, NMR + j:NMR + j + 1],
                scale=stats[:, RST + j:RST + j + 1]), ac)
            P.wait("tensor", ac)
            P.wait("tensor", vc, M.get(("ptr_evac", sl), 0))
            if war_v:
                P.wait("tensor", vc, war_v)
            for kcc in range(KC):
                P.op("tensor", lambda t, sl=sl, kcc=kcc, ps=ptr_ps, po=po: t.transpose(
                    ps[:, po + 128 * kcc:po + 128 * kcc + 128],
                    xh_tm[:, D * sl + 128 * kcc:D * sl + 128 * kcc + 128],
                    identb[:]), tc if kcc == KC - 1 else None)
            M[("xh_slot", sl)] = tc.n
            P.wait("vector", tc)
            P.wait("vector", tc, war_t)
            P.op("vector", lambda v, j=j, ps=ptr_ps, po=po: v.tensor_copy(
                xhB[:].rearrange("p (c x) -> p c x", x=1024)[:, :, 128 * j:128 * j + 128],
                ps[:, po:po + 1024].rearrange("p (c x) -> p c x", x=128)), vc)
            M[("ptr_evac", sl)] = vc.n
            M[("xhtile", j)] = vc.n

        def emit_store(u, j, js4):
            rr, nrel = j // 2, j % 2
            b_l, n = u // 2, 2 * (u % 2) + nrel
            P.wait("scalar", vc, M[("resid4", j)])
            P.op("scalar", lambda s, b=b_l, n=n, rr=rr, js4=js4: s.dma_start(
                out=out_d[b, n * 512 + rr:(n + 1) * 512:4, :],
                in_=ostg[:, D * js4:D * js4 + D]), dos[js4], 16)
            M[("ostore", u * 8 + j)] = dos[js4].n
            ost_war[js4] = (dos[js4], dos[js4].n)

        def emit_grp_tail(u, grp):
            nu = n_units
            for ji in range(4):
                P.wait("vector", vc, M[("yevac", ji)])
                P.op("vector", lambda v, ji=ji: v.bn_stats(
                    bn_sb[:, 12 * ji:12 * ji + 6], y_sb[:, D * ji:D * ji + 512]))
                P.op("vector", lambda v, ji=ji: v.bn_stats(
                    bn_sb[:, 12 * ji + 6:12 * ji + 12],
                    y_sb[:, D * ji + 512:D * ji + 1024]), vc)
            P.wait("vector", vc)
            P.wait("vector", ac, M.get(("ln4tail",), 0))
            for ji in range(4):
                P.op("vector", lambda v, ji=ji: v.bn_aggr(
                    stats[:, MV4 + 2 * ji:MV4 + 2 * ji + 2],
                    bn_sb[:, 12 * ji:12 * ji + 12]), vc)
            rv = emit_batch_rstd(MV4, 0, 4, lt0=LT4, rs0=RS4, nm0=NM4)
            for ji in range(4):
                j = 4 * grp + ji
                jsl = j % 2
                js4 = j % 4
                P.wait("scalar", vc, rv)
                P.wait("scalar", vc, M.get(("r4t", jsl), 0))
                P.wait("scalar", vc, M.get(("resid2", 6 + jsl), 0))
                P.op("scalar", lambda s, ji=ji, jsl=jsl: s.activation(
                    t_sb[:, D * jsl:D * jsl + D], y_sb[:, D * ji:D * ji + D],
                    AF.Identity, bias=stats[:, NM4 + ji:NM4 + ji + 1],
                    scale=stats[:, RS4 + ji:RS4 + ji + 1]), ac)
                M[("ln4apply", j)] = ac.n
                P.wait("vector", ac, M[("ln4apply", j)])
                P.wait("vector", vc, M[("resid2", j)])
                if js4 in ost_war:
                    P.wait("vector", ost_war[js4][0], ost_war[js4][1])
                P.op("vector", lambda v, j=j, jsl=jsl, js4=js4: v.tensor_tensor(
                    ostg[:, D * js4:D * js4 + D], t_sb[:, D * jsl:D * jsl + D],
                    x_tm[:, D * j:D * j + D], ALU.add), vc)
                M[("resid4", j)] = vc.n
                M[("r4t", jsl)] = vc.n
                if ji > 0:
                    emit_store(u, j - 1, (j - 1) % 4)
                    if u + 1 < nu:
                        emit_xload(u + 1, j - 1, (vc, M[("resid4", j - 1)]))
            M[("ln4tail",)] = ac.n
            emit_store(u, 4 * grp + 3, (4 * grp + 3) % 4)
            if u + 1 < nu:
                emit_xload(u + 1, 4 * grp + 3,
                           (vc, M[("resid4", 4 * grp + 3)]))
            if u + 1 < nu and grp == 0:
                for ji in range(4):
                    emit_ln1_stats(ji)
                M[("rst1a", u + 1)] = emit_batch_rstd(MV, 0, 4)

        P.op("gpsimd", lambda g: g.dma_start(out=identb[:], in_=c_identb[:, :]), dw, 16)
        P.op("gpsimd", lambda g: g.dma_start(out=ident8[:], in_=c_ident8[:, :]), dw, 16)
        P.op("gpsimd", lambda g: g.dma_start(out=selh[:], in_=c_selh[:, :]), dw, 16)
        P.op("gpsimd", lambda g: g.dma_start(out=bch[:], in_=c_bch[:, :]), dw, 16)
        P.op("gpsimd", lambda g: g.dma_start(out=onesb[:], in_=c_onesb[:, :]), dw, 16)
        P.op("gpsimd", lambda g: g.dma_start(
            out=bo_rowb[:], in_=bo_d.rearrange("(a d) -> a d", a=1)), dw, 16)
        for j in range(8):
            emit_xload(0, j)
            emit_ln1_stats(j)
        M[("rst1a", 0)] = emit_batch_rstd(MV, 0, 4)
        M[("rst1b", 0)] = emit_batch_rstd(MV, 4, 8)

        with nc.psum_tensor("p_b8", [128, 8], F32) as p_b8:
            for bias_src, bias_dst in ((bq_d, bqT), (bk_d, bkT), (bv_d, bvT)):
                P.op("gpsimd", lambda g, s=bias_src: g.dma_start(
                    out=b8[:], in_=s.rearrange("(g k) -> g k", k=128)), dw, 16)
                P.wait("tensor", dw)
                P.wait("tensor", vc)
                P.op("tensor", lambda t: t.transpose(p_b8[:], b8[:], ident8[:]), tc)
                P.wait("vector", tc)
                P.op("vector", lambda v, bdst=bias_dst: v.tensor_copy(bdst[:], p_b8[:]), vc)
                P.wait("gpsimd", tc)
            for sseed in range(-4, 0):
                M[("tfE", sseed)] = vc.n

        scrq_stores = []
        for rr in range(4):
            si = rr % 2
            slot = si * 8192
            if rr >= 2:
                P.wait("gpsimd", dstc, 32)
            for wi, wsrc in enumerate((wq_d, wk_d, wv_d)):
                P.op("gpsimd", lambda g, s=wsrc, rr=rr, slot=slot, wi=wi: g.dma_start(
                    out=qkvo[:, slot + 2048 * wi: slot + 2048 * wi + 2048]
                        .rearrange("p (c o) -> p c o", o=256),
                    in_=s.rearrange("(c p) o -> p c o", p=128)[:, :, 256 * rr:256 * rr + 256]),
                    dwq[si], 16)
            P.op("gpsimd", lambda g, rr=rr, slot=slot: g.dma_start(
                out=qkvo[:, slot + 6144: slot + 8192]
                    .rearrange("p (c o) -> p c o", o=1024),
                in_=wo_d.rearrange("(c p) o -> p c o", p=128)[:, 2 * rr:2 * rr + 2, :]),
                dwq[si], 16)
            def _sq(rr=rr, slot=slot, si=si, dn=None):
                P.wait("scalar", dwq[si], dn)
                P.op("scalar", lambda s, rr=rr, slot=slot: s.dma_start(
                    out=scr_q[rr], in_=qkvo[:, slot:slot + 8192]), dstc, 16)
            _sq.__defaults__ = (rr, slot, si, dwq[si].n)
            scrq_stores.append(_sq)
            M[("scrq", rr)] = 32 * ((rr // 2) + 1)

        with nc.psum_tensor("p_tr0", [128, D], BF16) as p_tr0, \
             nc.psum_tensor("p_tr1", [128, D], BF16) as p_tr1, \
             nc.psum_tensor("p_tr2", [128, D], BF16) as p_tr2, \
             nc.psum_tensor("p_tr3", [128, D], BF16) as p_tr3:
            p_tr = (p_tr0, p_tr1, p_tr2, p_tr3)
            for wsi, (w_src, w_scr, w_key) in enumerate(
                    ((wg_d, scr_g, "scrg"), (wu_d, scr_u, "scru"))):
                for hc in range(HCN):
                    gi = wsi * HCN + hc
                    sl4 = gi % 4
                    sl3i = gi % 3
                    nat = xhB[:, sl4 * D:sl4 * D + D]
                    stg = xhB[:, 4096 + sl4 * D: 4096 + sl4 * D + D]
                    ps = p_tr[sl4]
                    P.wait("sync", vc, M.get(("tfc", gi - 3), 0))
                    P.op("sync", lambda e, s=w_src, hc=hc, sl3i=sl3i: e.dma_start(
                        out=tfi[:, D * sl3i:D * sl3i + D],
                        in_=s[128 * hc:128 * hc + 128, :]), dtf[sl3i], 16)
                    P.wait("vector", dtf[sl3i])
                    P.wait("vector", tc, M.get(("tfT", gi - 4), 0))
                    P.op("vector", lambda v, nat=nat, sl3i=sl3i: v.tensor_copy(
                        nat, tfi[:, D * sl3i:D * sl3i + D]), vc)
                    M[("tfc", gi)] = vc.n
                    P.wait("tensor", vc, M[("tfc", gi)])
                    P.wait("tensor", vc, M.get(("tfE", gi - 4), 0))
                    for kcc in range(KC):
                        P.op("tensor",
                             lambda t, ps=ps, kcc=kcc, nat=nat: t.transpose(
                                 ps[:, 128 * kcc:128 * kcc + 128],
                                 nat[:, 128 * kcc:128 * kcc + 128], identb[:]),
                             tc if kcc == KC - 1 else None)
                    M[("tfT", gi)] = tc.n
                    P.wait("vector", tc, M[("tfT", gi)])
                    P.wait("vector", dst2[sl4])
                    P.op("vector", lambda v, ps=ps, stg=stg: v.tensor_copy(stg, ps[:]), vc)
                    M[("tfE", gi)] = vc.n
                    P.wait("scalar", vc, M[("tfE", gi)])
                    P.op("scalar", lambda s, scr=w_scr, hc=hc, stg=stg: s.dma_start(
                        out=scr[hc], in_=stg), dst2[sl4], 16)
                    M[(w_key, hc)] = (dst2[sl4], dst2[sl4].n)
                scrq_stores.pop(0)()
                scrq_stores.pop(0)()
            for dc in range(KC):
                sl2 = dc % 2
                nat = h_sb[:, sl2 * HID:sl2 * HID + HID]
                stg = h_sb[:, 2 * HID + sl2 * HID: 2 * HID + sl2 * HID + HID]
                for r in range(3):
                    gi2 = 2 * HCN + 3 * dc + r
                    sl3 = gi2 % 3
                    ps = p_tr[gi2 % 4]
                    P.wait("sync", vc, M.get(("tfc", gi2 - 3), 0))
                    P.op("sync", lambda e, dc=dc, r=r, sl3=sl3: e.dma_start(
                        out=tfi[:, D * sl3:D * sl3 + D],
                        in_=wd_d[128 * dc:128 * dc + 128, D * r:D * r + D]),
                        dtf[sl3], 16)
                    P.wait("vector", dtf[sl3])
                    P.wait("vector", tc, M.get(("wdT", dc - 2), 0))
                    P.op("vector", lambda v, nat=nat, r=r, sl3=sl3: v.tensor_copy(
                        nat[:, D * r:D * r + D], tfi[:, D * sl3:D * sl3 + D]), vc)
                    M[("tfc", gi2)] = vc.n
                    P.wait("tensor", vc, M[("tfc", gi2)])
                    P.wait("tensor", vc, M.get(("tfE", gi2 - 4), 0))
                    for i in range(8):
                        hc = 8 * r + i
                        P.op("tensor",
                             lambda t, ps=ps, i=i, hc=hc, nat=nat: t.transpose(
                                 ps[:, 128 * i:128 * i + 128],
                                 nat[:, 128 * hc:128 * hc + 128], identb[:]),
                             tc if i == 7 else None)
                    M[("tfT", gi2)] = tc.n
                    P.wait("vector", tc, M[("tfT", gi2)])
                    P.wait("vector", dst2[2 * sl2])
                    P.op("vector", lambda v, ps=ps, stg=stg, r=r: v.tensor_copy(
                        stg[:, D * r:D * r + D], ps[:]), vc)
                    M[("tfE", gi2)] = vc.n
                M[("wdT", dc)] = tc.n
                P.wait("scalar", vc)
                P.op("scalar", lambda s, dc=dc, stg=stg: s.dma_start(
                    out=scr_d[:, :, 128 * dc:128 * dc + 128].rearrange("h p d -> p h d"),
                    in_=stg.rearrange("p (h d) -> p h d", d=128)), dst2[2 * sl2], 16)

        startup_dst = dstc.n
        startup_v = vc.n
        ost_war = {}
        tail_pending = [None]
        M[("ptr_evac", 0)] = vc.n
        M[("ptr_evac", 1)] = vc.n

        for u in range(n_units):
            b_local, half = u // 2, u % 2
            nbase = 2 * half

            with nc.psum_tensor(f"ptr{u}", [128, 2048], BF16) as ptrU, \
                 contextlib.ExitStack() as pctx:
                pq = pctx.enter_context(nc.psum_tensor(f"pq{u}", [128, 512], F32))
                pkv = pctx.enter_context(nc.psum_tensor(f"pkv{u}", [128, 512], F32))
                psc = pctx.enter_context(nc.psum_tensor(f"psc{u}", [2, 512], F32))
                ppb = pctx.enter_context(nc.psum_tensor(f"ppb{u}", [128, 512], F32))
                pao = pctx.enter_context(nc.psum_tensor(f"pao{u}", [128, D], F32))

                def load_wqk(rr, u=u):
                    si = (u * 4 + rr) % 2
                    slot = si * 8192
                    if u * 4 + rr >= 2:
                        P.wait("sync", tc, M[("qkv_mm_done", u * 4 + rr - 2)])
                        if u == 0:
                            P.wait("sync", dstc, 64)
                    else:
                        P.wait("sync", dstc, 64)
                    P.op("sync", lambda sy, rr=rr, slot=slot: sy.dma_start(
                        out=qkvo[:, slot:slot + 4096], in_=scr_q[rr][:, 0:4096]),
                        dq[si], 16)

                def load_wv(rr, u=u):
                    si = (u * 4 + rr) % 2
                    slot = si * 8192
                    if u * 4 + rr >= 2:
                        P.wait("sync", tc, M[("vmm", u * 4 + rr - 2)])
                        if u == 0:
                            P.wait("sync", dstc, 64)
                    else:
                        P.wait("sync", dstc, 64)
                    P.op("sync", lambda sy, rr=rr, slot=slot: sy.dma_start(
                        out=qkvo[:, slot + 4096:slot + 6144], in_=scr_q[rr][:, 4096:6144]),
                        dq[si], 16)

                def load_wo_part(rr, u=u):
                    si = (u * 4 + rr) % 2
                    slot = si * 8192
                    if u * 4 + rr >= 2:
                        P.wait("sync", tc, M[("wo_mm_done", u * 4 + rr - 2)])
                        if u == 0:
                            P.wait("sync", dstc, 64)
                    else:
                        P.wait("sync", dstc, 64)
                    P.op("sync", lambda sy, rr=rr, slot=slot: sy.dma_start(
                        out=qkvo[:, slot + 6144:slot + 8192], in_=scr_q[rr][:, 6144:8192]),
                        dq[si], 16)

                def emit_one_proj(rr, wi, dst_ap, u=u):
                    si = (u * 4 + rr) % 2
                    gslot = si * 8192
                    for ch in range(2):
                        for kcc in range(KC):
                            P.op("tensor", lambda t, d2=dst_ap, wi=wi, ch=ch, kcc=kcc,
                                 gslot=gslot, rr=rr: t.matmul(
                                     d2[:, 256 * ch:256 * ch + 256],
                                     qkvo[:, gslot + 2048 * wi + 256 * kcc + 128 * ch:
                                          gslot + 2048 * wi + 256 * kcc + 128 * ch + 128],
                                     xhB[:, 1024 * kcc + 256 * rr:
                                          1024 * kcc + 256 * rr + 256],
                                     start=(ch == 0 and kcc == 0),
                                     stop=(ch == 1 and kcc == KC - 1),
                                     skip_group_check=True),
                                 tc if (ch == 1 and kcc == KC - 1) else None)
                    M[("mm", "qkv"[wi], rr)] = tc.n

                def emit_qk_mm(rr, u=u, pq=pq, pkv=pkv):
                    si = (u * 4 + rr) % 2
                    P.wait("tensor", vc, M[("xhtile", 2 * rr + 1)])
                    P.wait("tensor", dq[si])
                    P.wait("tensor", ac, M.get(("evQ", u * 4 + rr - 1), 0))
                    P.wait("tensor", ac, M.get(("evV", u * 4 + rr - 1), 0))
                    emit_one_proj(rr, 0, pq[:])
                    emit_one_proj(rr, 1, pkv[:])
                    M[("qkv_mm_done", u * 4 + rr)] = tc.n

                load_wqk(0)
                load_wv(0)
                load_wo_part(0)
                load_wqk(1)
                load_wv(1)
                load_wo_part(1)
                b2war = M.get(("yevacG1", u - 1), 0)
                if u == 0:
                    for dsl in range(4):
                        P.wait("vector", dst2[dsl])
                for j in (0, 1):
                    emit_xhat_tile(j, ptrU, M[("rst1a", u)], M.get(("b1_done", u - 1), 0), b2war)
                emit_qk_mm(0)
                load_wqk(2)
                for j in (2, 3):
                    emit_xhat_tile(j, ptrU, M[("rst1a", u)], M.get(("b1_done", u - 1), 0), b2war)
                if tail_pending[0] is not None:
                    tail_pending[0]()
                    tail_pending[0] = None
                P.wait("scalar", tc, M[("mm", "q", 0)])
                P.wait("scalar", gc, M.get(("qkprod", -1), 0))
                for ch in range(2):
                    P.op("scalar", lambda s, ch=ch, pq=pq: s.activation(
                        q_sb[:, 256 * ch:256 * ch + 256], pq[:, 256 * ch:256 * ch + 256],
                        AF.Identity, bias=bqT[:, ch:ch + 1]), ac)
                M[("evQ", u * 4)] = ac.n
                P.wait("scalar", tc, M[("mm", "k", 0)])
                for ch in range(2):
                    P.op("scalar", lambda s, ch=ch, pkv=pkv: s.activation(
                        k_sb[:, 256 * ch:256 * ch + 256], pkv[:, 256 * ch:256 * ch + 256],
                        AF.Identity, bias=bkT[:, ch:ch + 1]), ac)
                M[("evK", u * 4)] = ac.n
                if u > 0:
                    for j in (4, 5, 6, 7):
                        emit_ln1_stats(j)
                    M[("rst1b", u)] = emit_batch_rstd(MV, 4, 8)

                def emit_resid2(j, u=u):
                    jsl = j % 2
                    P.wait("vector", ac, M[("ln2apply", j)])
                    P.op("vector", lambda v, j=j, jsl=jsl: v.tensor_tensor(
                        x_tm[:, D * j:D * j + D], x_tm[:, D * j:D * j + D],
                        t_sb[:, D * jsl:D * jsl + D], ALU.add), vc)
                    M[("resid2", j)] = vc.n
                    P.wait("vector", vc)
                    xj = x_tm[:, D * j:D * j + D]
                    P.op("vector", lambda v, xj=xj: v.bn_stats(bn2_sb[:, 0:6], xj[:, 0:512]))
                    P.op("vector", lambda v, xj=xj: v.bn_stats(bn2_sb[:, 6:12], xj[:, 512:1024]), vc)
                    P.wait("vector", vc)
                    P.op("vector", lambda v, j=j: v.bn_aggr(
                        stats[:, MV3 + 2 * j:MV3 + 2 * j + 2], bn2_sb[:, 0:12]), vc)
                    if j == 3:
                        M[("rst3a", u)] = emit_batch_rstd(MV3, 0, 4)
                    elif j == 5:
                        M[("rst3b1", u)] = emit_batch_rstd(MV3, 4, 6)
                    elif j == 7:
                        M[("rst3b2", u)] = emit_batch_rstd(MV3, 6, 8)

                for rr in range(4):
                    si = (u * 4 + rr) % 2
                    gslot = si * 8192
                    vsl = rr % 2
                    P.wait("tensor", ac, M[("evK", u * 4 + rr)])
                    P.wait("tensor", dq[si])
                    emit_one_proj(rr, 2, pkv[:])
                    M[("vmm", u * 4 + rr)] = tc.n
                    if rr + 2 < 4:
                        load_wv(rr + 2)
                    P.wait("scalar", tc, M[("mm", "v", rr)])
                    P.wait("scalar", vc, M.get(("Amul", rr - 2), 0))
                    for ch in range(2):
                        P.op("scalar", lambda s, ch=ch, rr=rr, vsl=vsl, pkv=pkv: s.activation(
                            v_sb[:, 512 * vsl + 256 * ch:512 * vsl + 256 * ch + 256],
                            pkv[:, 256 * ch:256 * ch + 256],
                            AF.Identity, bias=bvT[:, 2 * rr + ch:2 * rr + ch + 1]), ac)
                    M[("evV", u * 4 + rr)] = ac.n

                    P.wait("gpsimd", ac, M[("evK", u * 4 + rr)])
                    P.wait("gpsimd", tc, M.get(("sc", rr - 2), 0))
                    P.op("gpsimd", lambda g, vsl=vsl: g.tensor_mul(
                        qk_bf[:, 512 * vsl:512 * vsl + 512], q_sb[:], k_sb[:]), gc)
                    M[("qkprod", rr)] = gc.n

                    P.wait("tensor", gc, M[("qkprod", rr)])
                    P.wait("tensor", ac, M.get(("exps", rr - 1), 0))
                    for ch in range(2):
                        P.op("tensor", lambda t, ch=ch, vsl=vsl, psc=psc: t.matmul(
                            psc[:, 256 * ch:256 * ch + 256], selh[:],
                            qk_bf[:, 512 * vsl + 256 * ch:512 * vsl + 256 * ch + 256],
                            start=(ch == 0), stop=(ch == 1), skip_group_check=True),
                            tc if ch == 1 else None)
                    M[("sc", rr)] = tc.n

                    so = 16 * vsl
                    P.wait("vector", tc, M[("sc", rr)])
                    P.wait("vector", ac, M.get(("exps", rr - 2), 0))
                    P.op("vector", lambda v, so=so, psc=psc: v.reduce_max(
                        smx[:, so:so + 4], psc[:].rearrange("p (b s) -> p b s", s=128),
                        axis=AX.X), vc)
                    P.wait("vector", vc)
                    P.op("vector", lambda v, so=so: v.tensor_scalar(
                        smx[:, so + 4:so + 8], smx[:, so:so + 4], -SCL, None, ALU.mult), vc)
                    P.wait("scalar", vc)
                    P.wait("scalar", tc, M.get(("pb", rr - 2), 0))
                    for c in range(4):
                        P.op("scalar", lambda s, c=c, so=so, vsl=vsl, psc=psc: s.activation(
                            p_sb[:, 512 * vsl + 128 * c:512 * vsl + 128 * c + 128],
                            psc[:, 128 * c:128 * c + 128],
                            AF.Exp, scale=SCL, bias=smx[:, so + 4 + c:so + 5 + c],
                            accum_out=smx[:, so + 8 + c:so + 9 + c]), ac)
                    M[("exps", rr)] = ac.n
                    if rr + 1 < 4:
                        emit_qk_mm(rr + 1)
                        P.wait("scalar", tc, M[("mm", "q", rr + 1)])
                        P.wait("scalar", gc, M.get(("qkprod", rr), 0))
                        for ch in range(2):
                            P.op("scalar", lambda s, ch=ch, rr2=rr + 1, pq=pq: s.activation(
                                q_sb[:, 256 * ch:256 * ch + 256], pq[:, 256 * ch:256 * ch + 256],
                                AF.Identity, bias=bqT[:, 2 * rr2 + ch:2 * rr2 + ch + 1]), ac)
                        M[("evQ", u * 4 + rr + 1)] = ac.n
                        P.wait("scalar", tc, M[("mm", "k", rr + 1)])
                        for ch in range(2):
                            P.op("scalar", lambda s, ch=ch, rr2=rr + 1, pkv=pkv: s.activation(
                                k_sb[:, 256 * ch:256 * ch + 256], pkv[:, 256 * ch:256 * ch + 256],
                                AF.Identity, bias=bkT[:, 2 * rr2 + ch:2 * rr2 + ch + 1]), ac)
                        M[("evK", u * 4 + rr + 1)] = ac.n
                        if rr < 2:
                            b2war = M.get(("yevacG1", u - 1), 0)
                            for j in (4 + 2 * rr, 5 + 2 * rr):
                                emit_xhat_tile(j, ptrU, M[("rst1b", u)],
                                               M.get(("b1_done", u - 1), 0), b2war)
                        if rr + 3 < 4:
                            load_wqk(rr + 3)
                    if rr == 3:
                        for j3 in (0, 1, 2, 3):
                            emit_xhat_tile(j3, ptrU, M[("rst3a", u)],
                                           M[("qkv_mm_done", u * 4 + j3 // 2)])
                        M[("xh3nt", 0)] = vc.n

                    P.wait("vector", ac, M[("exps", rr)])
                    P.op("vector", lambda v, so=so: v.reciprocal(
                        smx[:, so + 12:so + 16], smx[:, so + 8:so + 12]), vc)
                    P.wait("vector", vc)
                    for c in range(4):
                        P.op("vector", lambda v, c=c, so=so, vsl=vsl: v.tensor_scalar(
                            p_sb[:, 512 * vsl + 128 * c:512 * vsl + 128 * c + 128],
                            p_sb[:, 512 * vsl + 128 * c:512 * vsl + 128 * c + 128],
                            smx[:, so + 12 + c:so + 13 + c], None, ALU.mult),
                            vc if c == 3 else None)
                    pnorm_v = vc.n

                    P.wait("tensor", vc, pnorm_v)
                    P.wait("tensor", vc, M.get(("Amul", rr - 1), 0))
                    for ch in range(2):
                        P.op("tensor", lambda t, ch=ch, vsl=vsl, ppb=ppb: t.matmul(
                            ppb[:, 256 * ch:256 * ch + 256], bch[:],
                            p_sb[:, 512 * vsl + 256 * ch:512 * vsl + 256 * ch + 256],
                            start=(ch == 0), stop=(ch == 1), skip_group_check=True),
                            tc if ch == 1 else None)
                    M[("pb", rr)] = tc.n

                    P.wait("vector", tc, M[("pb", rr)])
                    P.wait("vector", ac, M[("evV", u * 4 + rr)])
                    P.wait("vector", tc, M.get(("wo_mm_done", u * 4 + rr - 2), 0))
                    P.op("vector", lambda v, vsl=vsl, ppb=ppb: v.tensor_mul(
                        a_bf[:, 512 * vsl:512 * vsl + 512], ppb[:],
                        v_sb[:, 512 * vsl:512 * vsl + 512]), vc)
                    M[("Amul", rr)] = vc.n

                    for s2 in range(2):
                        j = 2 * rr + s2
                        jsl = j % 2
                        mvp, ltp, rsp, nmp = MV2P(jsl), LT2P(jsl), RS2P(jsl), NM2P(jsl)
                        P.wait("tensor", vc, M[("Amul", rr)])
                        for hf in range(2):
                            P.wait("tensor", ac, M.get(("aoc", 2 * rr + s2 - 1, hf), 0))
                            P.op("tensor", lambda t, hf=hf, pao=pao: t.matmul(
                                pao[:, 512 * hf:512 * hf + 512], onesb[:],
                                bo_rowb[:, 512 * hf:512 * hf + 512],
                                start=True, stop=False, skip_group_check=True), None)
                            for ch in range(2):
                                P.op("tensor", lambda t, hf=hf, ch=ch, s2=s2, vsl=vsl,
                                     gslot=gslot, pao=pao: t.matmul(
                                         pao[:, 512 * hf:512 * hf + 512],
                                         a_bf[:, 512 * vsl + 256 * ch + 128 * s2:
                                              512 * vsl + 256 * ch + 128 * s2 + 128],
                                         qkvo[:, gslot + 6144 + 1024 * ch + 512 * hf:
                                              gslot + 6144 + 1024 * ch + 512 * hf + 512],
                                         start=False, stop=(ch == 1),
                                         skip_group_check=True),
                                     tc if ch == 1 else None)
                            M[("wo_h", j, hf)] = tc.n
                        wo_t = tc.n
                        if s2 == 1:
                            M[("wo_mm_done", u * 4 + rr)] = wo_t

                        P.wait("scalar", vc, M.get(("aostats", j - 2), 0))
                        for hf in range(2):
                            P.wait("scalar", tc, M[("wo_h", j, hf)])
                            P.op("scalar", lambda s, jsl=jsl, hf=hf, pao=pao: s.activation(
                                ao_sb[:, D * jsl + 512 * hf:D * jsl + 512 * hf + 512],
                                pao[:, 512 * hf:512 * hf + 512], AF.Copy), ac)
                            M[("aoc", 2 * rr + s2, hf)] = ac.n
                        M[("aocopy", 2 * rr + s2)] = ac.n
                        P.wait("vector", ac, M[("aocopy", 2 * rr + s2)])
                        P.wait("vector", vc)
                        P.op("vector", lambda v, jsl=jsl: v.bn_stats(
                            bn_sb[:, 12 * jsl:12 * jsl + 6], ao_sb[:, D * jsl:D * jsl + 512]))
                        P.op("vector", lambda v, jsl=jsl: v.bn_stats(
                            bn_sb[:, 12 * jsl + 6:12 * jsl + 12],
                            ao_sb[:, D * jsl + 512:D * jsl + 1024]), vc)
                        M[("aostats", j)] = vc.n
                        P.wait("vector", vc)
                        P.wait("vector", ac, M.get(("lnexp2", j - 2), 0))
                        P.op("vector", lambda v, mvp=mvp, jsl=jsl: v.bn_aggr(
                            stats[:, mvp:mvp + 2], bn_sb[:, 12 * jsl:12 * jsl + 12]), vc)
                        P.wait("scalar", vc)
                        P.op("scalar", lambda s, mvp=mvp, ltp=ltp: s.activation(
                            stats[:, ltp:ltp + 1], stats[:, mvp + 1:mvp + 2],
                            AF.Ln, bias=eps_c[:, 0:1]), ac)
                        P.wait("scalar", ac)
                        P.op("scalar", lambda s, ltp=ltp, rsp=rsp: s.activation(
                            stats[:, rsp:rsp + 1], stats[:, ltp:ltp + 1], AF.Exp, scale=-0.5), ac)
                        M[("lnexp2", j)] = ac.n
                        P.wait("vector", ac)
                        P.wait("vector", ac, M.get(("ln2apply", j - 2), 0))
                        P.op("vector", lambda v, mvp=mvp, rsp=rsp, nmp=nmp: v.scalar_tensor_tensor(
                            stats[:, nmp:nmp + 1], stats[:, mvp:mvp + 1], -1.0,
                            stats[:, rsp:rsp + 1], ALU.mult, ALU.mult), vc)
                        P.wait("scalar", vc)
                        P.wait("scalar", vc, M.get(("resid2", j - 2), 0))
                        P.wait("scalar", vc, M.get(("r4t", jsl), 0))
                        P.op("scalar", lambda s, jsl=jsl, rsp=rsp, nmp=nmp: s.activation(
                            t_sb[:, D * jsl:D * jsl + D], ao_sb[:, D * jsl:D * jsl + D],
                            AF.Identity, bias=stats[:, nmp:nmp + 1],
                            scale=stats[:, rsp:rsp + 1]), ac)
                        M[("ln2apply", j)] = ac.n
                        emit_resid2(j)

                    if rr + 2 < 4:
                        load_wo_part(rr + 2)

            jobs = [(0, 0), (1, 0), (0, 1), (1, 1)] + \
                   [(hc, nt) for hc in range(2, HCN) for nt in range(2)]
            with nc.psum_tensor(f"ptr_{u}", [128, 2048], BF16) as ptrU2, \
                 nc.psum_tensor(f"pg{u}", [128, 3 * 512], F32) as pg, \
                 nc.psum_tensor(f"pu{u}", [128, 3 * 512], F32) as pu:
                loaded = set()
                for ji, (hc, nt) in enumerate(jobs):
                    slot = hc % 4
                    if hc not in loaded:
                        loaded.add(hc)
                        if ("rgu", u, hc - 4) in M:
                            P.wait("sync", tc, M[("rgu", u, hc - 4)])
                        elif u > 0 and hc < 4:
                            P.wait("sync", tc, M[("ring_gu", HCN - 4 + hc)])
                        if u == 0:
                            P.wait("sync", *M[("scrg", hc)])
                            P.wait("sync", *M[("scru", hc)])
                        P.op("sync", lambda sy, hc=hc, slot=slot: sy.dma_start(
                            out=ring_g[:, D * slot:D * slot + D], in_=scr_g[hc]), dsg[slot], 16)
                        P.op("sync", lambda sy, hc=hc, slot=slot: sy.dma_start(
                            out=ring_u[:, D * slot:D * slot + D], in_=scr_u[hc]), dsg[slot], 16)

                    P.wait("tensor", dsg[slot])
                    gsl = ji % 3
                    ssl = ji % 2
                    P.wait("tensor", vc, M.get(("xh3nt", nt), 0))
                    P.wait("tensor", vc, M.get(("hmulB", u, ji - 3), 0))
                    for kcc in range(KC):
                        P.op("tensor", lambda t, slot=slot, gsl=gsl, nt=nt, kcc=kcc, pg=pg: t.matmul(
                            pg[:, 512 * gsl:512 * gsl + 512],
                            ring_g[:, D * slot + 128 * kcc:D * slot + 128 * kcc + 128],
                            xhB[:, 1024 * kcc + 512 * nt:1024 * kcc + 512 * nt + 512],
                            start=(kcc == 0), stop=(kcc == KC - 1)), None)
                    for kcc in range(KC):
                        P.op("tensor", lambda t, slot=slot, gsl=gsl, nt=nt, kcc=kcc, pu=pu: t.matmul(
                            pu[:, 512 * gsl:512 * gsl + 512],
                            ring_u[:, D * slot + 128 * kcc:D * slot + 128 * kcc + 128],
                            xhB[:, 1024 * kcc + 512 * nt:1024 * kcc + 512 * nt + 512],
                            start=(kcc == 0), stop=(kcc == KC - 1)),
                            tc if kcc == KC - 1 else None)
                    if nt == 1:
                        M[("rgu", u, hc)] = tc.n
                    gu_t = tc.n
                    P.wait("scalar", tc, gu_t)
                    P.wait("scalar", vc, M.get(("hmulB", u, ji - 2), 0))
                    P.op("scalar", lambda s, gsl=gsl, ssl=ssl, pg=pg: s.activation(
                        silu_sb[:, 512 * ssl:512 * ssl + 512],
                        pg[:, 512 * gsl:512 * gsl + 512], AF.Silu, scale=SDQ), ac)
                    silu_a = ac.n
                    P.wait("vector", ac, silu_a)
                    P.op("vector", lambda v, gsl=gsl, ssl=ssl, hc=hc, nt=nt, pu=pu: v.tensor_mul(
                        h_sb[:, D * hc + 512 * nt:D * hc + 512 * nt + 512],
                        silu_sb[:, 512 * ssl:512 * ssl + 512],
                        pu[:, 512 * gsl:512 * gsl + 512]), vc)
                    M[("hmulB", u, ji)] = vc.n
                    if ji == 0:
                        for j in (4, 5):
                            emit_xhat_tile(j, ptrU2, M[("rst3b1", u)],
                                           M[("qkv_mm_done", u * 4 + j // 2)])
                    elif ji == 1:
                        for j in (6, 7):
                            emit_xhat_tile(j, ptrU2, M[("rst3b2", u)],
                                           M[("qkv_mm_done", u * 4 + j // 2)])
                        M[("xh3nt", 1)] = vc.n
                for back in range(4):
                    M[("ring_gu", HCN - 4 + back)] = M[("rgu", u, HCN - 4 + back)]
                M[("b1_done", u)] = tc.n
                h_ready_v = vc.n

            with contextlib.ExitStack() as pctx:
                pd = [pctx.enter_context(nc.psum_tensor(f"pd{u}_{i}", [128, D], F32))
                      for i in range(4)]
                for grp in range(2):
                    for kcc in range(HCN):
                        slot = kcc % 4
                        if (u, grp, kcc - 4) in M:
                            P.wait("sync", tc, M[(u, grp, kcc - 4)])
                        elif grp == 1 and kcc < 4:
                            P.wait("sync", tc, M[(u, 0, HCN - 4 + kcc)])
                        elif u > 0 and grp == 0 and kcc < 4:
                            P.wait("sync", tc, M[("ring_d", HCN - 4 + kcc)])
                        elif u == 0 and grp == 0 and kcc < 4:
                            for dsl in range(4):
                                P.wait("sync", dst2[dsl])
                        P.op("sync", lambda sy, kcc=kcc, slot=slot: sy.dma_start(
                            out=ring_d[:, D * slot:D * slot + D], in_=scr_d[kcc]), drd[slot], 16)
                        P.wait("tensor", drd[slot])
                        if kcc == 0:
                            P.wait("tensor", vc, h_ready_v)
                        for ji in range(4):
                            j = 4 * grp + ji
                            if kcc == 0 and grp == 1:
                                P.wait("tensor", vc, M[("yevac", ji)])
                            for hf in range(2):
                                P.op("tensor", lambda t, ji=ji, j=j, hf=hf, kcc=kcc, slot=slot, pd=pd: t.matmul(
                                    pd[ji][:, 512 * hf:512 * hf + 512],
                                    h_sb[:, D * kcc + 128 * j:D * kcc + 128 * j + 128],
                                    ring_d[:, D * slot + 512 * hf:D * slot + 512 * hf + 512],
                                    start=(kcc == 0), stop=(kcc == HCN - 1)),
                                    tc if (ji == 3 and hf == 1) else None)
                        M[(u, grp, kcc)] = tc.n
                        if grp == 1 and kcc >= HCN - 4:
                            M[("ring_d", kcc)] = tc.n
                    dn_done_t = tc.n
                    for ji in range(4):
                        P.wait("vector", tc, dn_done_t)
                        P.wait("vector", ac,
                               M.get(("ln4apply", (4 * grp - 4 + ji) % 8), 0))
                        P.op("vector", lambda v, ji=ji, pd=pd: v.tensor_copy(
                            y_sb[:, D * ji:D * ji + D], pd[ji][:]), vc)
                        M[("yevac", ji)] = vc.n
                        if grp == 1 and ji == 0:
                            M[("yevacG1", u)] = vc.n
                    if grp == 1:
                        M[("b2yevac", u)] = vc.n
                    if grp == 0:
                        emit_grp_tail(u, 0)
                    else:
                        tail_pending[0] = lambda u=u: emit_grp_tail(u, 1)

        if tail_pending[0] is not None:
            tail_pending[0]()
            tail_pending[0] = None
        for dsem in dos:
            P.wait("gpsimd", dsem, dsem.n)
            P.wait("sync", dsem, dsem.n)

        with nc.Block() as block:
            @block.tensor
            def _(eng):
                for fn in P.streams["tensor"]:
                    fn(eng)

            @block.vector
            def _(eng):
                for fn in P.streams["vector"]:
                    fn(eng)

            @block.scalar
            def _(eng):
                for fn in P.streams["scalar"]:
                    fn(eng)

            @block.sync
            def _(eng):
                for fn in P.streams["sync"]:
                    fn(eng)

            @block.gpsimd
            def _(eng):
                for fn in P.streams["gpsimd"]:
                    fn(eng)

    return nc


def _host_inputs(inputs):
    perm = np.concatenate([
        np.arange(64 * (rr + 4 * hj), 64 * (rr + 4 * hj) + 64)
        for rr in range(4) for hj in range(4)])
    selh = np.zeros((128, 2), dtype=BFNP)
    selh[0:64, 0] = 1
    selh[64:128, 1] = 1
    bch = np.zeros((2, 128), dtype=BFNP)
    bch[0, 0:64] = 1
    bch[1, 64:128] = 1
    return {
        "wq": np.ascontiguousarray(inputs["L0_attn_Wq"][:, perm]),
        "wk": np.ascontiguousarray(inputs["L0_attn_Wk"][:, perm]),
        "wv": np.ascontiguousarray(inputs["L0_attn_Wv"][:, perm]),
        "wo": np.ascontiguousarray(inputs["L0_attn_Wo"][perm, :]),
        "bq": np.ascontiguousarray(inputs["L0_attn_bq"][perm]),
        "bk": np.ascontiguousarray(inputs["L0_attn_bk"][perm]),
        "bv": np.ascontiguousarray(inputs["L0_attn_bv"][perm]),
        "bo": np.ascontiguousarray(inputs["L0_attn_bo"]),
        "wg": np.ascontiguousarray(inputs["L0_ffn_w_gate"]),
        "wu": np.ascontiguousarray(inputs["L0_ffn_w_up"]),
        "wd": np.ascontiguousarray(inputs["L0_ffn_w_down"]),
        "c_identb": np.eye(128, dtype=np.float32).astype(BFNP),
        "c_ident8": np.eye(8, dtype=np.float32),
        "c_selh": selh,
        "c_bch": bch,
        "c_onesb": np.ones((1, 128), dtype=BFNP),
    }


def kernel(**inputs):
    x = np.asarray(inputs["L0_attn_x"])
    common = _host_inputs({k: np.asarray(v) for k, v in inputs.items()})
    nc = build_nc(UNITS)
    in_maps = []
    for c in range(NCORES):
        m = dict(common)
        m["x"] = np.ascontiguousarray(x[4 * c:4 * c + 4])
        in_maps.append(m)
    res = run_bass_kernel_spmd(nc, in_maps, core_ids=list(range(NCORES)))
    return np.concatenate([res.results[c]["out"] for c in range(NCORES)], axis=0)


# version 22
# speedup vs baseline: 1.0399x; 1.0169x over previous
import contextlib

import numpy as np
import ml_dtypes

import concourse.bass as bass
import concourse.mybir as mybir
from concourse.bass_utils import run_bass_kernel_spmd

F32 = mybir.dt.float32
BF16 = mybir.dt.bfloat16
I32 = mybir.dt.int32
AF = mybir.ActivationFunctionType
ALU = mybir.AluOpType
AX = mybir.AxisListType
BFNP = ml_dtypes.bfloat16

D = 1024
HID = 3072
KC = 8
HCN = 24
EPS = 1e-5
SDQ = 1.0 / 127.0
SCL = 0.125
NCORES = 8
UNITS = 8
ENGS = ("tensor", "vector", "scalar", "sync", "gpsimd")

MV = 0
RST = 16
NMR = 24
MV3 = 48
LT1 = 64
LT4 = 72
RS4 = 76
NM4 = 80
MV4 = 84


def MV2P(p):
    return 32 + 2 * p


def LT2P(p):
    return 36 + p


def RS2P(p):
    return 38 + p


def NM2P(p):
    return 42 + p


class Ctr:
    def __init__(self, name):
        self.name = name
        self.n = 0
        self.sem = None


class Rec:
    def __init__(self):
        self.streams = {e: [] for e in ENGS}
        self.lastw = {}

    def op(self, e, f, inc=None, k=1):
        if inc is not None:
            inc.n += k

        def fn(eng, f=f, inc=inc, k=k):
            ins = f(eng)
            if inc is not None:
                ins.then_inc(inc.sem, k)

        self.streams[e].append(fn)
        return inc.n if inc is not None else None

    def wait(self, e, ctr, n=None):
        t = ctr.n if n is None else n
        if t <= 0 or t <= self.lastw.get((e, ctr.name), 0):
            return
        self.lastw[(e, ctr.name)] = t
        self.streams[e].append(lambda eng, c=ctr, t=t: eng.wait_ge(c.sem, t))


def build_nc(n_units=UNITS):
    nc = bass.Bass("TRN2", target_bir_lowering=False)

    x_d = nc.dram_tensor("x", [4, 2048, D], F32, kind="ExternalInput").ap()
    out_d = nc.dram_tensor("out", [4, 2048, D], F32, kind="ExternalOutput").ap()
    wq_d = nc.dram_tensor("wq", [D, D], F32, kind="ExternalInput").ap()
    wk_d = nc.dram_tensor("wk", [D, D], F32, kind="ExternalInput").ap()
    wv_d = nc.dram_tensor("wv", [D, D], F32, kind="ExternalInput").ap()
    wo_d = nc.dram_tensor("wo", [D, D], F32, kind="ExternalInput").ap()
    bq_d = nc.dram_tensor("bq", [D], F32, kind="ExternalInput").ap()
    bk_d = nc.dram_tensor("bk", [D], F32, kind="ExternalInput").ap()
    bv_d = nc.dram_tensor("bv", [D], F32, kind="ExternalInput").ap()
    bo_d = nc.dram_tensor("bo", [D], F32, kind="ExternalInput").ap()
    wg_d = nc.dram_tensor("wg", [HID, D], I32, kind="ExternalInput").ap()
    wu_d = nc.dram_tensor("wu", [HID, D], I32, kind="ExternalInput").ap()
    wd_d = nc.dram_tensor("wd", [D, HID], I32, kind="ExternalInput").ap()
    c_identb = nc.dram_tensor("c_identb", [128, 128], BF16, kind="ExternalInput").ap()
    c_ident8 = nc.dram_tensor("c_ident8", [8, 8], F32, kind="ExternalInput").ap()
    c_selh = nc.dram_tensor("c_selh", [128, 2], BF16, kind="ExternalInput").ap()
    c_bch = nc.dram_tensor("c_bch", [2, 128], BF16, kind="ExternalInput").ap()
    c_onesb = nc.dram_tensor("c_onesb", [1, 128], BF16, kind="ExternalInput").ap()
    scr_g = nc.dram_tensor("scr_g", [HCN, 128, D], BF16, kind="Internal").ap()
    scr_u = nc.dram_tensor("scr_u", [HCN, 128, D], BF16, kind="Internal").ap()
    scr_d = nc.dram_tensor("scr_d", [HCN, 128, D], BF16, kind="Internal").ap()
    scr_q = nc.dram_tensor("scr_q", [4, 128, 8192], BF16, kind="Internal").ap()

    with contextlib.ExitStack() as ctx:
        E = ctx.enter_context

        x_tm = E(nc.sbuf_tensor("x_tm", [128, 8 * D], F32))
        xh_tm = E(nc.sbuf_tensor("xh_tm", [128, 2 * D], BF16))
        xhB = E(nc.sbuf_tensor("xhB", [128, 8 * D], BF16))
        qkvo = E(nc.sbuf_tensor("qkvo", [128, 2 * 8192], BF16))
        ring_g = E(nc.sbuf_tensor("ring_g", [128, 4 * D], BF16))
        ring_u = E(nc.sbuf_tensor("ring_u", [128, 4 * D], BF16))
        ring_d = E(nc.sbuf_tensor("ring_d", [128, 4 * D], BF16))
        h_off = nc.sbuf_base
        h_sb = E(nc.sbuf_tensor("h_sb", [128, HCN * D], BF16))
        t4x = nc.alloc_sbuf_tensor_at("t4x", [128, 2 * D], F32, offset=h_off)
        q_sb = E(nc.sbuf_tensor("q_sb", [128, 512], BF16))
        k_sb = E(nc.sbuf_tensor("k_sb", [128, 512], BF16))
        v_sb = E(nc.sbuf_tensor("v_sb", [128, 2 * 512], BF16))
        qk_bf = E(nc.sbuf_tensor("qk_bf", [128, 2 * 512], BF16))
        a_bf = E(nc.sbuf_tensor("a_bf", [128, 2 * 512], BF16))
        p_sb = E(nc.sbuf_tensor("p_sb", [2, 2 * 512], BF16))
        smx = E(nc.sbuf_tensor("smx", [2, 32], F32))
        ao_sb = E(nc.sbuf_tensor("ao_sb", [128, 2 * D], BF16))
        t_sb = E(nc.sbuf_tensor("t_sb", [128, 2 * D], F32))
        y_off = nc.sbuf_base
        y_sb = E(nc.sbuf_tensor("y_sb", [128, 4 * D], BF16))
        silu_sb = E(nc.sbuf_tensor("silu_sb", [128, 2 * 512], BF16))
        ostg_off = nc.sbuf_base
        ostg = E(nc.sbuf_tensor("ostg", [128, 4 * D], F32))
        tfi = nc.alloc_sbuf_tensor_at("tfi32", [128, 3 * D], I32, offset=ostg_off)
        stats = E(nc.sbuf_tensor("stats", [128, 92], F32))
        bn_sb = E(nc.sbuf_tensor("bn_sb", [128, 48], F32))
        bn2_sb = E(nc.sbuf_tensor("bn2_sb", [128, 12], F32))
        bqT = E(nc.sbuf_tensor("bqT", [128, 8], F32))
        bkT = E(nc.sbuf_tensor("bkT", [128, 8], F32))
        bvT = E(nc.sbuf_tensor("bvT", [128, 8], F32))
        b8 = nc.alloc_sbuf_tensor_at("b8", [8, 128], F32, offset=y_off)
        bo_rowb = E(nc.sbuf_tensor("bo_rowb", [1, D], BF16))
        identb = E(nc.sbuf_tensor("identb", [128, 128], BF16))
        ident8 = E(nc.sbuf_tensor("ident8", [8, 8], F32))
        selh = E(nc.sbuf_tensor("selh", [128, 2], BF16))
        bch = E(nc.sbuf_tensor("bch", [2, 128], BF16))
        onesb = E(nc.sbuf_tensor("onesb", [1, 128], BF16))
        eps_c = E(nc.sbuf_tensor("eps_c", [128, 1], F32))

        def mk(name):
            c = Ctr(name)
            c.sem = E(nc.semaphore("sem_" + name))
            return c

        dxs = [mk(f"dx{j}") for j in range(8)]
        dq = [mk(f"dq{i}") for i in range(2)]
        dsg = [mk(f"dsg{i}") for i in range(4)]
        drd = [mk(f"drd{i}") for i in range(4)]
        dos = [mk(f"do{i}") for i in range(4)]
        dw = mk("dw")
        dwq = [mk(f"dwq{i}") for i in range(2)]
        dtf = [mk(f"dtf{i}") for i in range(3)]
        dst2 = [mk(f"dst2_{i}") for i in range(4)]
        dstc = mk("dst")
        tc = mk("t")
        vc = mk("v")
        ac = mk("a")
        gc = mk("g")

        P = Rec()
        M = {}

        P.op("vector", lambda v: v.memset(eps_c[:], EPS), vc)

        def emit_xload(un, j, war=None):
            b_l, hf = un // 2, un % 2
            n = 2 * hf + (j % 2)
            rr = j // 2
            if war is not None:
                P.wait("scalar", war[0], war[1])
            P.op("scalar", lambda s, b=b_l, n=n, rr=rr, j=j: s.dma_start(
                out=x_tm[:, D * j:D * j + D],
                in_=x_d[b, n * 512 + rr:(n + 1) * 512:4, :]), dxs[j], 16)

        def emit_ln1_stats(j):
            P.wait("vector", dxs[j])
            P.wait("vector", vc)
            xj = x_tm[:, D * j:D * j + D]
            P.op("vector", lambda v, xj=xj: v.bn_stats(bn2_sb[:, 0:6], xj[:, 0:512]))
            P.op("vector", lambda v, xj=xj: v.bn_stats(bn2_sb[:, 6:12], xj[:, 512:1024]), vc)
            P.wait("vector", vc)
            P.op("vector", lambda v, j=j: v.bn_aggr(
                stats[:, MV + 2 * j:MV + 2 * j + 2], bn2_sb[:, 0:12]), vc)

        def emit_batch_rstd(src_off, lo, hi, lt0=LT1, rs0=RST, nm0=NMR):
            w = hi - lo
            P.wait("scalar", vc)
            P.op("scalar", lambda s, so=src_off, lo=lo, hi=hi, w=w, lt0=lt0: s.activation(
                stats[:, lt0 + lo:lt0 + lo + w],
                stats[:, so + 2 * lo + 1:so + 2 * hi:2],
                AF.Ln, bias=eps_c[:, 0:1]), ac)
            P.wait("scalar", ac)
            P.op("scalar", lambda s, lo=lo, w=w, lt0=lt0, rs0=rs0: s.activation(
                stats[:, rs0 + lo:rs0 + lo + w], stats[:, lt0 + lo:lt0 + lo + w],
                AF.Exp, scale=-0.5), ac)
            P.wait("vector", ac)
            P.op("vector", lambda v, so=src_off, lo=lo, hi=hi, w=w, rs0=rs0, nm0=nm0: v.scalar_tensor_tensor(
                stats[:, nm0 + lo:nm0 + lo + w], stats[:, so + 2 * lo:so + 2 * hi:2],
                -1.0, stats[:, rs0 + lo:rs0 + lo + w], ALU.mult, ALU.mult), vc)
            return vc.n

        def emit_xhat_tile(j, ptr_ps, rst_ready_v, war_t, war_v=0):
            sl = j % 2
            po = 1024 * sl
            P.wait("scalar", vc, rst_ready_v)
            P.wait("scalar", tc, M.get(("xh_slot", sl), 0))
            P.op("scalar", lambda s, j=j, sl=sl: s.activation(
                xh_tm[:, D * sl:D * sl + D], x_tm[:, D * j:D * j + D],
                AF.Identity, bias=stats[:, NMR + j:NMR + j + 1],
                scale=stats[:, RST + j:RST + j + 1]), ac)
            P.wait("tensor", ac)
            P.wait("tensor", vc, M.get(("ptr_evac", sl), 0))
            if war_v:
                P.wait("tensor", vc, war_v)
            for kcc in range(KC):
                P.op("tensor", lambda t, sl=sl, kcc=kcc, ps=ptr_ps, po=po: t.transpose(
                    ps[:, po + 128 * kcc:po + 128 * kcc + 128],
                    xh_tm[:, D * sl + 128 * kcc:D * sl + 128 * kcc + 128],
                    identb[:]), tc if kcc == KC - 1 else None)
            M[("xh_slot", sl)] = tc.n
            P.wait("vector", tc)
            P.wait("vector", tc, war_t)
            P.op("vector", lambda v, j=j, ps=ptr_ps, po=po: v.tensor_copy(
                xhB[:].rearrange("p (c x) -> p c x", x=1024)[:, :, 128 * j:128 * j + 128],
                ps[:, po:po + 1024].rearrange("p (c x) -> p c x", x=128)), vc)
            M[("ptr_evac", sl)] = vc.n
            M[("xhtile", j)] = vc.n

        def emit_store(u, j, js4):
            rr, nrel = j // 2, j % 2
            b_l, n = u // 2, 2 * (u % 2) + nrel
            P.wait("scalar", vc, M[("resid4", j)])
            P.op("scalar", lambda s, b=b_l, n=n, rr=rr, js4=js4: s.dma_start(
                out=out_d[b, n * 512 + rr:(n + 1) * 512:4, :],
                in_=ostg[:, D * js4:D * js4 + D]), dos[js4], 16)
            M[("ostore", u * 8 + j)] = dos[js4].n
            ost_war[js4] = (dos[js4], dos[js4].n)

        def emit_grp_tail(u, grp):
            nu = n_units
            for ji in range(4):
                P.wait("vector", vc, M[("yevac", ji)])
                P.op("vector", lambda v, ji=ji: v.bn_stats(
                    bn_sb[:, 12 * ji:12 * ji + 6], y_sb[:, D * ji:D * ji + 512]))
                P.op("vector", lambda v, ji=ji: v.bn_stats(
                    bn_sb[:, 12 * ji + 6:12 * ji + 12],
                    y_sb[:, D * ji + 512:D * ji + 1024]), vc)
            P.wait("vector", vc)
            P.wait("vector", ac, M.get(("ln4tail",), 0))
            for ji in range(4):
                P.op("vector", lambda v, ji=ji: v.bn_aggr(
                    stats[:, MV4 + 2 * ji:MV4 + 2 * ji + 2],
                    bn_sb[:, 12 * ji:12 * ji + 12]), vc)
            rv = emit_batch_rstd(MV4, 0, 4, lt0=LT4, rs0=RS4, nm0=NM4)
            for ji in range(4):
                j = 4 * grp + ji
                jsl = j % 2
                js4 = j % 4
                if grp == 1 and ji < 2:
                    tb, toff = t4x, ji
                    P.wait("scalar", tc, M[(u, 1, HCN - 1)])
                else:
                    tb, toff = t_sb, jsl
                    P.wait("scalar", vc, M.get(("r4t", jsl), 0))
                    P.wait("scalar", vc, M.get(("resid2", 6 + jsl), 0))
                P.wait("scalar", vc, rv)
                P.op("scalar", lambda s, ji=ji, tb=tb, toff=toff: s.activation(
                    tb[:, D * toff:D * toff + D], y_sb[:, D * ji:D * ji + D],
                    AF.Identity, bias=stats[:, NM4 + ji:NM4 + ji + 1],
                    scale=stats[:, RS4 + ji:RS4 + ji + 1]), ac)
                M[("ln4apply", j)] = ac.n
                P.wait("vector", ac, M[("ln4apply", j)])
                P.wait("vector", vc, M[("resid2", j)])
                if js4 in ost_war:
                    P.wait("vector", ost_war[js4][0], ost_war[js4][1])
                P.op("vector", lambda v, j=j, tb=tb, toff=toff, js4=js4: v.tensor_tensor(
                    ostg[:, D * js4:D * js4 + D], tb[:, D * toff:D * toff + D],
                    x_tm[:, D * j:D * j + D], ALU.add), vc)
                M[("resid4", j)] = vc.n
                if tb is t_sb:
                    M[("r4t", jsl)] = vc.n
                if ji > 0:
                    emit_store(u, j - 1, (j - 1) % 4)
                    if u + 1 < nu:
                        emit_xload(u + 1, j - 1, (vc, M[("resid4", j - 1)]))
            M[("ln4tail",)] = ac.n
            emit_store(u, 4 * grp + 3, (4 * grp + 3) % 4)
            if u + 1 < nu:
                emit_xload(u + 1, 4 * grp + 3,
                           (vc, M[("resid4", 4 * grp + 3)]))
            if u + 1 < nu and grp == 0:
                for ji in range(4):
                    emit_ln1_stats(ji)
                M[("rst1a", u + 1)] = emit_batch_rstd(MV, 0, 4)

        P.op("gpsimd", lambda g: g.dma_start(out=identb[:], in_=c_identb[:, :]), dw, 16)
        P.op("gpsimd", lambda g: g.dma_start(out=ident8[:], in_=c_ident8[:, :]), dw, 16)
        P.op("gpsimd", lambda g: g.dma_start(out=selh[:], in_=c_selh[:, :]), dw, 16)
        P.op("gpsimd", lambda g: g.dma_start(out=bch[:], in_=c_bch[:, :]), dw, 16)
        P.op("gpsimd", lambda g: g.dma_start(out=onesb[:], in_=c_onesb[:, :]), dw, 16)
        P.op("gpsimd", lambda g: g.dma_start(
            out=bo_rowb[:], in_=bo_d.rearrange("(a d) -> a d", a=1)), dw, 16)
        for j in range(8):
            emit_xload(0, j)
            emit_ln1_stats(j)
        M[("rst1a", 0)] = emit_batch_rstd(MV, 0, 4)
        M[("rst1b", 0)] = emit_batch_rstd(MV, 4, 8)

        with nc.psum_tensor("p_b8", [128, 8], F32) as p_b8:
            for bias_src, bias_dst in ((bq_d, bqT), (bk_d, bkT), (bv_d, bvT)):
                P.op("gpsimd", lambda g, s=bias_src: g.dma_start(
                    out=b8[:], in_=s.rearrange("(g k) -> g k", k=128)), dw, 16)
                P.wait("tensor", dw)
                P.wait("tensor", vc)
                P.op("tensor", lambda t: t.transpose(p_b8[:], b8[:], ident8[:]), tc)
                P.wait("vector", tc)
                P.op("vector", lambda v, bdst=bias_dst: v.tensor_copy(bdst[:], p_b8[:]), vc)
                P.wait("gpsimd", tc)
            for sseed in range(-4, 0):
                M[("tfE", sseed)] = vc.n

        scrq_stores = []
        for rr in range(4):
            si = rr % 2
            slot = si * 8192
            if rr >= 2:
                P.wait("gpsimd", dstc, 32)
            for wi, wsrc in enumerate((wq_d, wk_d, wv_d)):
                P.op("gpsimd", lambda g, s=wsrc, rr=rr, slot=slot, wi=wi: g.dma_start(
                    out=qkvo[:, slot + 2048 * wi: slot + 2048 * wi + 2048]
                        .rearrange("p (c o) -> p c o", o=256),
                    in_=s.rearrange("(c p) o -> p c o", p=128)[:, :, 256 * rr:256 * rr + 256]),
                    dwq[si], 16)
            P.op("gpsimd", lambda g, rr=rr, slot=slot: g.dma_start(
                out=qkvo[:, slot + 6144: slot + 8192]
                    .rearrange("p (c o) -> p c o", o=1024),
                in_=wo_d.rearrange("(c p) o -> p c o", p=128)[:, 2 * rr:2 * rr + 2, :]),
                dwq[si], 16)
            def _sq(rr=rr, slot=slot, si=si, dn=None):
                P.wait("scalar", dwq[si], dn)
                P.op("scalar", lambda s, rr=rr, slot=slot: s.dma_start(
                    out=scr_q[rr], in_=qkvo[:, slot:slot + 8192]), dstc, 16)
            _sq.__defaults__ = (rr, slot, si, dwq[si].n)
            scrq_stores.append(_sq)
            M[("scrq", rr)] = 32 * ((rr // 2) + 1)

        with nc.psum_tensor("p_tr0", [128, D], BF16) as p_tr0, \
             nc.psum_tensor("p_tr1", [128, D], BF16) as p_tr1, \
             nc.psum_tensor("p_tr2", [128, D], BF16) as p_tr2, \
             nc.psum_tensor("p_tr3", [128, D], BF16) as p_tr3:
            p_tr = (p_tr0, p_tr1, p_tr2, p_tr3)
            for wsi, (w_src, w_scr, w_key) in enumerate(
                    ((wg_d, scr_g, "scrg"), (wu_d, scr_u, "scru"))):
                for hc in range(HCN):
                    gi = wsi * HCN + hc
                    sl4 = gi % 4
                    sl3i = gi % 3
                    nat = xhB[:, sl4 * D:sl4 * D + D]
                    stg = xhB[:, 4096 + sl4 * D: 4096 + sl4 * D + D]
                    ps = p_tr[sl4]
                    P.wait("sync", vc, M.get(("tfc", gi - 3), 0))
                    P.op("sync", lambda e, s=w_src, hc=hc, sl3i=sl3i: e.dma_start(
                        out=tfi[:, D * sl3i:D * sl3i + D],
                        in_=s[128 * hc:128 * hc + 128, :]), dtf[sl3i], 16)
                    P.wait("vector", dtf[sl3i])
                    P.wait("vector", tc, M.get(("tfT", gi - 4), 0))
                    P.op("vector", lambda v, nat=nat, sl3i=sl3i: v.tensor_copy(
                        nat, tfi[:, D * sl3i:D * sl3i + D]), vc)
                    M[("tfc", gi)] = vc.n
                    P.wait("tensor", vc, M[("tfc", gi)])
                    P.wait("tensor", vc, M.get(("tfE", gi - 4), 0))
                    for kcc in range(KC):
                        P.op("tensor",
                             lambda t, ps=ps, kcc=kcc, nat=nat: t.transpose(
                                 ps[:, 128 * kcc:128 * kcc + 128],
                                 nat[:, 128 * kcc:128 * kcc + 128], identb[:]),
                             tc if kcc == KC - 1 else None)
                    M[("tfT", gi)] = tc.n
                    P.wait("vector", tc, M[("tfT", gi)])
                    P.wait("vector", dst2[sl4])
                    P.op("vector", lambda v, ps=ps, stg=stg: v.tensor_copy(stg, ps[:]), vc)
                    M[("tfE", gi)] = vc.n
                    P.wait("scalar", vc, M[("tfE", gi)])
                    P.op("scalar", lambda s, scr=w_scr, hc=hc, stg=stg: s.dma_start(
                        out=scr[hc], in_=stg), dst2[sl4], 16)
                    M[(w_key, hc)] = (dst2[sl4], dst2[sl4].n)
                scrq_stores.pop(0)()
                scrq_stores.pop(0)()
            for dc in range(KC):
                sl2 = dc % 2
                nat = h_sb[:, sl2 * HID:sl2 * HID + HID]
                stg = h_sb[:, 2 * HID + sl2 * HID: 2 * HID + sl2 * HID + HID]
                for r in range(3):
                    gi2 = 2 * HCN + 3 * dc + r
                    sl3 = gi2 % 3
                    ps = p_tr[gi2 % 4]
                    P.wait("sync", vc, M.get(("tfc", gi2 - 3), 0))
                    P.op("sync", lambda e, dc=dc, r=r, sl3=sl3: e.dma_start(
                        out=tfi[:, D * sl3:D * sl3 + D],
                        in_=wd_d[128 * dc:128 * dc + 128, D * r:D * r + D]),
                        dtf[sl3], 16)
                    P.wait("vector", dtf[sl3])
                    P.wait("vector", tc, M.get(("wdT", dc - 2), 0))
                    P.op("vector", lambda v, nat=nat, r=r, sl3=sl3: v.tensor_copy(
                        nat[:, D * r:D * r + D], tfi[:, D * sl3:D * sl3 + D]), vc)
                    M[("tfc", gi2)] = vc.n
                    P.wait("tensor", vc, M[("tfc", gi2)])
                    P.wait("tensor", vc, M.get(("tfE", gi2 - 4), 0))
                    for i in range(8):
                        hc = 8 * r + i
                        P.op("tensor",
                             lambda t, ps=ps, i=i, hc=hc, nat=nat: t.transpose(
                                 ps[:, 128 * i:128 * i + 128],
                                 nat[:, 128 * hc:128 * hc + 128], identb[:]),
                             tc if i == 7 else None)
                    M[("tfT", gi2)] = tc.n
                    P.wait("vector", tc, M[("tfT", gi2)])
                    P.wait("vector", dst2[2 * sl2])
                    P.op("vector", lambda v, ps=ps, stg=stg, r=r: v.tensor_copy(
                        stg[:, D * r:D * r + D], ps[:]), vc)
                    M[("tfE", gi2)] = vc.n
                M[("wdT", dc)] = tc.n
                P.wait("scalar", vc)
                P.op("scalar", lambda s, dc=dc, stg=stg: s.dma_start(
                    out=scr_d[:, :, 128 * dc:128 * dc + 128].rearrange("h p d -> p h d"),
                    in_=stg.rearrange("p (h d) -> p h d", d=128)), dst2[2 * sl2], 16)

        startup_dst = dstc.n
        startup_v = vc.n
        ost_war = {}
        tail_pending = [None]
        M[("ptr_evac", 0)] = vc.n
        M[("ptr_evac", 1)] = vc.n

        for u in range(n_units):
            b_local, half = u // 2, u % 2
            nbase = 2 * half

            with nc.psum_tensor(f"ptr{u}", [128, 2048], BF16) as ptrU, \
                 contextlib.ExitStack() as pctx:
                pq = pctx.enter_context(nc.psum_tensor(f"pq{u}", [128, 512], F32))
                pkv = pctx.enter_context(nc.psum_tensor(f"pkv{u}", [128, 512], F32))
                psc = pctx.enter_context(nc.psum_tensor(f"psc{u}", [2, 512], F32))
                ppb = pctx.enter_context(nc.psum_tensor(f"ppb{u}", [128, 512], F32))
                pao = pctx.enter_context(nc.psum_tensor(f"pao{u}", [128, D], F32))

                def load_wqk(rr, u=u):
                    si = (u * 4 + rr) % 2
                    slot = si * 8192
                    if u * 4 + rr >= 2:
                        P.wait("sync", tc, M[("qkv_mm_done", u * 4 + rr - 2)])
                        if u == 0:
                            P.wait("sync", dstc, 64)
                    else:
                        P.wait("sync", dstc, 64)
                    P.op("sync", lambda sy, rr=rr, slot=slot: sy.dma_start(
                        out=qkvo[:, slot:slot + 4096], in_=scr_q[rr][:, 0:4096]),
                        dq[si], 16)

                def load_wv(rr, u=u):
                    si = (u * 4 + rr) % 2
                    slot = si * 8192
                    if u * 4 + rr >= 2:
                        P.wait("sync", tc, M[("vmm", u * 4 + rr - 2)])
                        if u == 0:
                            P.wait("sync", dstc, 64)
                    else:
                        P.wait("sync", dstc, 64)
                    P.op("sync", lambda sy, rr=rr, slot=slot: sy.dma_start(
                        out=qkvo[:, slot + 4096:slot + 6144], in_=scr_q[rr][:, 4096:6144]),
                        dq[si], 16)

                def load_wo_part(rr, u=u):
                    si = (u * 4 + rr) % 2
                    slot = si * 8192
                    if u * 4 + rr >= 2:
                        P.wait("sync", tc, M[("wo_mm_done", u * 4 + rr - 2)])
                        if u == 0:
                            P.wait("sync", dstc, 64)
                    else:
                        P.wait("sync", dstc, 64)
                    P.op("sync", lambda sy, rr=rr, slot=slot: sy.dma_start(
                        out=qkvo[:, slot + 6144:slot + 8192], in_=scr_q[rr][:, 6144:8192]),
                        dq[si], 16)

                def emit_one_proj(rr, wi, dst_ap, u=u):
                    si = (u * 4 + rr) % 2
                    gslot = si * 8192
                    for ch in range(2):
                        for kcc in range(KC):
                            P.op("tensor", lambda t, d2=dst_ap, wi=wi, ch=ch, kcc=kcc,
                                 gslot=gslot, rr=rr: t.matmul(
                                     d2[:, 256 * ch:256 * ch + 256],
                                     qkvo[:, gslot + 2048 * wi + 256 * kcc + 128 * ch:
                                          gslot + 2048 * wi + 256 * kcc + 128 * ch + 128],
                                     xhB[:, 1024 * kcc + 256 * rr:
                                          1024 * kcc + 256 * rr + 256],
                                     start=(ch == 0 and kcc == 0),
                                     stop=(ch == 1 and kcc == KC - 1),
                                     skip_group_check=True),
                                 tc if (ch == 1 and kcc == KC - 1) else None)
                    M[("mm", "qkv"[wi], rr)] = tc.n

                def emit_qk_mm(rr, u=u, pq=pq, pkv=pkv):
                    si = (u * 4 + rr) % 2
                    P.wait("tensor", vc, M[("xhtile", 2 * rr + 1)])
                    P.wait("tensor", dq[si])
                    P.wait("tensor", ac, M.get(("evQ", u * 4 + rr - 1), 0))
                    P.wait("tensor", ac, M.get(("evV", u * 4 + rr - 1), 0))
                    emit_one_proj(rr, 0, pq[:])
                    emit_one_proj(rr, 1, pkv[:])
                    M[("qkv_mm_done", u * 4 + rr)] = tc.n

                load_wqk(0)
                load_wv(0)
                load_wo_part(0)
                load_wqk(1)
                load_wv(1)
                load_wo_part(1)
                b2war = M.get(("yevacG1", u - 1), 0)
                if u == 0:
                    for dsl in range(4):
                        P.wait("vector", dst2[dsl])
                for j in (0, 1):
                    emit_xhat_tile(j, ptrU, M[("rst1a", u)], M.get(("b1_done", u - 1), 0), b2war)
                emit_qk_mm(0)
                load_wqk(2)
                for j in (2, 3):
                    emit_xhat_tile(j, ptrU, M[("rst1a", u)], M.get(("b1_done", u - 1), 0), b2war)
                P.wait("scalar", tc, M[("mm", "q", 0)])
                P.wait("scalar", gc, M.get(("qkprod", -1), 0))
                for ch in range(2):
                    P.op("scalar", lambda s, ch=ch, pq=pq: s.activation(
                        q_sb[:, 256 * ch:256 * ch + 256], pq[:, 256 * ch:256 * ch + 256],
                        AF.Identity, bias=bqT[:, ch:ch + 1]), ac)
                M[("evQ", u * 4)] = ac.n
                P.wait("scalar", tc, M[("mm", "k", 0)])
                for ch in range(2):
                    P.op("scalar", lambda s, ch=ch, pkv=pkv: s.activation(
                        k_sb[:, 256 * ch:256 * ch + 256], pkv[:, 256 * ch:256 * ch + 256],
                        AF.Identity, bias=bkT[:, ch:ch + 1]), ac)
                M[("evK", u * 4)] = ac.n
                if tail_pending[0] is not None:
                    tail_pending[0]()
                    tail_pending[0] = None
                if u > 0:
                    for j in (4, 5, 6, 7):
                        emit_ln1_stats(j)
                    M[("rst1b", u)] = emit_batch_rstd(MV, 4, 8)

                def emit_resid2(j, u=u):
                    jsl = j % 2
                    P.wait("vector", ac, M[("ln2apply", j)])
                    P.op("vector", lambda v, j=j, jsl=jsl: v.tensor_tensor(
                        x_tm[:, D * j:D * j + D], x_tm[:, D * j:D * j + D],
                        t_sb[:, D * jsl:D * jsl + D], ALU.add), vc)
                    M[("resid2", j)] = vc.n
                    P.wait("vector", vc)
                    xj = x_tm[:, D * j:D * j + D]
                    P.op("vector", lambda v, xj=xj: v.bn_stats(bn2_sb[:, 0:6], xj[:, 0:512]))
                    P.op("vector", lambda v, xj=xj: v.bn_stats(bn2_sb[:, 6:12], xj[:, 512:1024]), vc)
                    P.wait("vector", vc)
                    P.op("vector", lambda v, j=j: v.bn_aggr(
                        stats[:, MV3 + 2 * j:MV3 + 2 * j + 2], bn2_sb[:, 0:12]), vc)
                    if j == 3:
                        M[("rst3a", u)] = emit_batch_rstd(MV3, 0, 4)
                    elif j == 5:
                        M[("rst3b1", u)] = emit_batch_rstd(MV3, 4, 6)
                    elif j == 7:
                        M[("rst3b2", u)] = emit_batch_rstd(MV3, 6, 8)

                for rr in range(4):
                    si = (u * 4 + rr) % 2
                    gslot = si * 8192
                    vsl = rr % 2
                    P.wait("tensor", ac, M[("evK", u * 4 + rr)])
                    P.wait("tensor", dq[si])
                    emit_one_proj(rr, 2, pkv[:])
                    M[("vmm", u * 4 + rr)] = tc.n
                    if rr + 2 < 4:
                        load_wv(rr + 2)
                    P.wait("scalar", tc, M[("mm", "v", rr)])
                    P.wait("scalar", vc, M.get(("Amul", rr - 2), 0))
                    for ch in range(2):
                        P.op("scalar", lambda s, ch=ch, rr=rr, vsl=vsl, pkv=pkv: s.activation(
                            v_sb[:, 512 * vsl + 256 * ch:512 * vsl + 256 * ch + 256],
                            pkv[:, 256 * ch:256 * ch + 256],
                            AF.Identity, bias=bvT[:, 2 * rr + ch:2 * rr + ch + 1]), ac)
                    M[("evV", u * 4 + rr)] = ac.n

                    P.wait("gpsimd", ac, M[("evK", u * 4 + rr)])
                    P.wait("gpsimd", tc, M.get(("sc", rr - 2), 0))
                    P.op("gpsimd", lambda g, vsl=vsl: g.tensor_mul(
                        qk_bf[:, 512 * vsl:512 * vsl + 512], q_sb[:], k_sb[:]), gc)
                    M[("qkprod", rr)] = gc.n

                    P.wait("tensor", gc, M[("qkprod", rr)])
                    P.wait("tensor", ac, M.get(("exps", rr - 1), 0))
                    for ch in range(2):
                        P.op("tensor", lambda t, ch=ch, vsl=vsl, psc=psc: t.matmul(
                            psc[:, 256 * ch:256 * ch + 256], selh[:],
                            qk_bf[:, 512 * vsl + 256 * ch:512 * vsl + 256 * ch + 256],
                            start=(ch == 0), stop=(ch == 1), skip_group_check=True),
                            tc if ch == 1 else None)
                    M[("sc", rr)] = tc.n

                    so = 16 * vsl
                    P.wait("vector", tc, M[("sc", rr)])
                    P.wait("vector", ac, M.get(("exps", rr - 2), 0))
                    P.op("vector", lambda v, so=so, psc=psc: v.reduce_max(
                        smx[:, so:so + 4], psc[:].rearrange("p (b s) -> p b s", s=128),
                        axis=AX.X), vc)
                    P.wait("vector", vc)
                    P.op("vector", lambda v, so=so: v.tensor_scalar(
                        smx[:, so + 4:so + 8], smx[:, so:so + 4], -SCL, None, ALU.mult), vc)
                    P.wait("scalar", vc)
                    P.wait("scalar", tc, M.get(("pb", rr - 2), 0))
                    for c in range(4):
                        P.op("scalar", lambda s, c=c, so=so, vsl=vsl, psc=psc: s.activation(
                            p_sb[:, 512 * vsl + 128 * c:512 * vsl + 128 * c + 128],
                            psc[:, 128 * c:128 * c + 128],
                            AF.Exp, scale=SCL, bias=smx[:, so + 4 + c:so + 5 + c],
                            accum_out=smx[:, so + 8 + c:so + 9 + c]), ac)
                    M[("exps", rr)] = ac.n
                    if rr + 1 < 4:
                        emit_qk_mm(rr + 1)
                        P.wait("scalar", tc, M[("mm", "q", rr + 1)])
                        P.wait("scalar", gc, M.get(("qkprod", rr), 0))
                        for ch in range(2):
                            P.op("scalar", lambda s, ch=ch, rr2=rr + 1, pq=pq: s.activation(
                                q_sb[:, 256 * ch:256 * ch + 256], pq[:, 256 * ch:256 * ch + 256],
                                AF.Identity, bias=bqT[:, 2 * rr2 + ch:2 * rr2 + ch + 1]), ac)
                        M[("evQ", u * 4 + rr + 1)] = ac.n
                        P.wait("scalar", tc, M[("mm", "k", rr + 1)])
                        for ch in range(2):
                            P.op("scalar", lambda s, ch=ch, rr2=rr + 1, pkv=pkv: s.activation(
                                k_sb[:, 256 * ch:256 * ch + 256], pkv[:, 256 * ch:256 * ch + 256],
                                AF.Identity, bias=bkT[:, 2 * rr2 + ch:2 * rr2 + ch + 1]), ac)
                        M[("evK", u * 4 + rr + 1)] = ac.n
                        if rr < 2:
                            b2war = M.get(("yevacG1", u - 1), 0)
                            for j in (4 + 2 * rr, 5 + 2 * rr):
                                emit_xhat_tile(j, ptrU, M[("rst1b", u)],
                                               M.get(("b1_done", u - 1), 0), b2war)
                        if rr + 3 < 4:
                            load_wqk(rr + 3)
                    if rr == 3:
                        for j3 in (0, 1, 2, 3):
                            emit_xhat_tile(j3, ptrU, M[("rst3a", u)],
                                           M[("qkv_mm_done", u * 4 + j3 // 2)])
                        M[("xh3nt", 0)] = vc.n

                    P.wait("vector", ac, M[("exps", rr)])
                    P.op("vector", lambda v, so=so: v.reciprocal(
                        smx[:, so + 12:so + 16], smx[:, so + 8:so + 12]), vc)
                    P.wait("vector", vc)
                    for c in range(4):
                        P.op("vector", lambda v, c=c, so=so, vsl=vsl: v.tensor_scalar(
                            p_sb[:, 512 * vsl + 128 * c:512 * vsl + 128 * c + 128],
                            p_sb[:, 512 * vsl + 128 * c:512 * vsl + 128 * c + 128],
                            smx[:, so + 12 + c:so + 13 + c], None, ALU.mult),
                            vc if c == 3 else None)
                    pnorm_v = vc.n

                    P.wait("tensor", vc, pnorm_v)
                    P.wait("tensor", vc, M.get(("Amul", rr - 1), 0))
                    for ch in range(2):
                        P.op("tensor", lambda t, ch=ch, vsl=vsl, ppb=ppb: t.matmul(
                            ppb[:, 256 * ch:256 * ch + 256], bch[:],
                            p_sb[:, 512 * vsl + 256 * ch:512 * vsl + 256 * ch + 256],
                            start=(ch == 0), stop=(ch == 1), skip_group_check=True),
                            tc if ch == 1 else None)
                    M[("pb", rr)] = tc.n

                    P.wait("vector", tc, M[("pb", rr)])
                    P.wait("vector", ac, M[("evV", u * 4 + rr)])
                    P.wait("vector", tc, M.get(("wo_mm_done", u * 4 + rr - 2), 0))
                    P.op("vector", lambda v, vsl=vsl, ppb=ppb: v.tensor_mul(
                        a_bf[:, 512 * vsl:512 * vsl + 512], ppb[:],
                        v_sb[:, 512 * vsl:512 * vsl + 512]), vc)
                    M[("Amul", rr)] = vc.n

                    for s2 in range(2):
                        j = 2 * rr + s2
                        jsl = j % 2
                        mvp, ltp, rsp, nmp = MV2P(jsl), LT2P(jsl), RS2P(jsl), NM2P(jsl)
                        P.wait("tensor", vc, M[("Amul", rr)])
                        for hf in range(2):
                            P.wait("tensor", ac, M.get(("aoc", 2 * rr + s2 - 1, hf), 0))
                            P.op("tensor", lambda t, hf=hf, pao=pao: t.matmul(
                                pao[:, 512 * hf:512 * hf + 512], onesb[:],
                                bo_rowb[:, 512 * hf:512 * hf + 512],
                                start=True, stop=False, skip_group_check=True), None)
                            for ch in range(2):
                                P.op("tensor", lambda t, hf=hf, ch=ch, s2=s2, vsl=vsl,
                                     gslot=gslot, pao=pao: t.matmul(
                                         pao[:, 512 * hf:512 * hf + 512],
                                         a_bf[:, 512 * vsl + 256 * ch + 128 * s2:
                                              512 * vsl + 256 * ch + 128 * s2 + 128],
                                         qkvo[:, gslot + 6144 + 1024 * ch + 512 * hf:
                                              gslot + 6144 + 1024 * ch + 512 * hf + 512],
                                         start=False, stop=(ch == 1),
                                         skip_group_check=True),
                                     tc if ch == 1 else None)
                            M[("wo_h", j, hf)] = tc.n
                        wo_t = tc.n
                        if s2 == 1:
                            M[("wo_mm_done", u * 4 + rr)] = wo_t

                        P.wait("scalar", vc, M.get(("aostats", j - 2), 0))
                        for hf in range(2):
                            P.wait("scalar", tc, M[("wo_h", j, hf)])
                            P.op("scalar", lambda s, jsl=jsl, hf=hf, pao=pao: s.activation(
                                ao_sb[:, D * jsl + 512 * hf:D * jsl + 512 * hf + 512],
                                pao[:, 512 * hf:512 * hf + 512], AF.Copy), ac)
                            M[("aoc", 2 * rr + s2, hf)] = ac.n
                        M[("aocopy", 2 * rr + s2)] = ac.n
                        P.wait("vector", ac, M[("aocopy", 2 * rr + s2)])
                        P.wait("vector", vc)
                        P.op("vector", lambda v, jsl=jsl: v.bn_stats(
                            bn_sb[:, 12 * jsl:12 * jsl + 6], ao_sb[:, D * jsl:D * jsl + 512]))
                        P.op("vector", lambda v, jsl=jsl: v.bn_stats(
                            bn_sb[:, 12 * jsl + 6:12 * jsl + 12],
                            ao_sb[:, D * jsl + 512:D * jsl + 1024]), vc)
                        M[("aostats", j)] = vc.n
                        P.wait("vector", vc)
                        P.wait("vector", ac, M.get(("lnexp2", j - 2), 0))
                        P.op("vector", lambda v, mvp=mvp, jsl=jsl: v.bn_aggr(
                            stats[:, mvp:mvp + 2], bn_sb[:, 12 * jsl:12 * jsl + 12]), vc)
                        P.wait("scalar", vc)
                        P.op("scalar", lambda s, mvp=mvp, ltp=ltp: s.activation(
                            stats[:, ltp:ltp + 1], stats[:, mvp + 1:mvp + 2],
                            AF.Ln, bias=eps_c[:, 0:1]), ac)
                        P.wait("scalar", ac)
                        P.op("scalar", lambda s, ltp=ltp, rsp=rsp: s.activation(
                            stats[:, rsp:rsp + 1], stats[:, ltp:ltp + 1], AF.Exp, scale=-0.5), ac)
                        M[("lnexp2", j)] = ac.n
                        P.wait("vector", ac)
                        P.wait("vector", ac, M.get(("ln2apply", j - 2), 0))
                        P.op("vector", lambda v, mvp=mvp, rsp=rsp, nmp=nmp: v.scalar_tensor_tensor(
                            stats[:, nmp:nmp + 1], stats[:, mvp:mvp + 1], -1.0,
                            stats[:, rsp:rsp + 1], ALU.mult, ALU.mult), vc)
                        P.wait("scalar", vc)
                        P.wait("scalar", vc, M.get(("resid2", j - 2), 0))
                        P.wait("scalar", vc, M.get(("r4t", jsl), 0))
                        P.op("scalar", lambda s, jsl=jsl, rsp=rsp, nmp=nmp: s.activation(
                            t_sb[:, D * jsl:D * jsl + D], ao_sb[:, D * jsl:D * jsl + D],
                            AF.Identity, bias=stats[:, nmp:nmp + 1],
                            scale=stats[:, rsp:rsp + 1]), ac)
                        M[("ln2apply", j)] = ac.n
                        emit_resid2(j)

                    if rr + 2 < 4:
                        load_wo_part(rr + 2)

            jobs = [(0, 0), (1, 0), (0, 1), (1, 1)] + \
                   [(hc, nt) for hc in range(2, HCN) for nt in range(2)]
            with nc.psum_tensor(f"ptr_{u}", [128, 2048], BF16) as ptrU2, \
                 nc.psum_tensor(f"pg{u}", [128, 3 * 512], F32) as pg, \
                 nc.psum_tensor(f"pu{u}", [128, 3 * 512], F32) as pu:
                loaded = set()
                for ji, (hc, nt) in enumerate(jobs):
                    slot = hc % 4
                    if hc not in loaded:
                        loaded.add(hc)
                        if ("rgu", u, hc - 4) in M:
                            P.wait("sync", tc, M[("rgu", u, hc - 4)])
                        elif u > 0 and hc < 4:
                            P.wait("sync", tc, M[("ring_gu", HCN - 4 + hc)])
                        if u == 0:
                            P.wait("sync", *M[("scrg", hc)])
                            P.wait("sync", *M[("scru", hc)])
                        P.op("sync", lambda sy, hc=hc, slot=slot: sy.dma_start(
                            out=ring_g[:, D * slot:D * slot + D], in_=scr_g[hc]), dsg[slot], 16)
                        P.op("sync", lambda sy, hc=hc, slot=slot: sy.dma_start(
                            out=ring_u[:, D * slot:D * slot + D], in_=scr_u[hc]), dsg[slot], 16)

                    P.wait("tensor", dsg[slot])
                    gsl = ji % 3
                    ssl = ji % 2
                    P.wait("tensor", vc, M.get(("xh3nt", nt), 0))
                    P.wait("tensor", vc, M.get(("hmulB", u, ji - 3), 0))
                    for kcc in range(KC):
                        P.op("tensor", lambda t, slot=slot, gsl=gsl, nt=nt, kcc=kcc, pg=pg: t.matmul(
                            pg[:, 512 * gsl:512 * gsl + 512],
                            ring_g[:, D * slot + 128 * kcc:D * slot + 128 * kcc + 128],
                            xhB[:, 1024 * kcc + 512 * nt:1024 * kcc + 512 * nt + 512],
                            start=(kcc == 0), stop=(kcc == KC - 1)), None)
                    for kcc in range(KC):
                        P.op("tensor", lambda t, slot=slot, gsl=gsl, nt=nt, kcc=kcc, pu=pu: t.matmul(
                            pu[:, 512 * gsl:512 * gsl + 512],
                            ring_u[:, D * slot + 128 * kcc:D * slot + 128 * kcc + 128],
                            xhB[:, 1024 * kcc + 512 * nt:1024 * kcc + 512 * nt + 512],
                            start=(kcc == 0), stop=(kcc == KC - 1)),
                            tc if kcc == KC - 1 else None)
                    if nt == 1:
                        M[("rgu", u, hc)] = tc.n
                    gu_t = tc.n
                    P.wait("scalar", tc, gu_t)
                    P.wait("scalar", vc, M.get(("hmulB", u, ji - 2), 0))
                    P.op("scalar", lambda s, gsl=gsl, ssl=ssl, pg=pg: s.activation(
                        silu_sb[:, 512 * ssl:512 * ssl + 512],
                        pg[:, 512 * gsl:512 * gsl + 512], AF.Silu, scale=SDQ), ac)
                    silu_a = ac.n
                    P.wait("vector", ac, silu_a)
                    P.op("vector", lambda v, gsl=gsl, ssl=ssl, hc=hc, nt=nt, pu=pu: v.tensor_mul(
                        h_sb[:, D * hc + 512 * nt:D * hc + 512 * nt + 512],
                        silu_sb[:, 512 * ssl:512 * ssl + 512],
                        pu[:, 512 * gsl:512 * gsl + 512]), vc)
                    M[("hmulB", u, ji)] = vc.n
                    if ji == 0:
                        for j in (4, 5):
                            emit_xhat_tile(j, ptrU2, M[("rst3b1", u)],
                                           M[("qkv_mm_done", u * 4 + j // 2)])
                    elif ji == 1:
                        for j in (6, 7):
                            emit_xhat_tile(j, ptrU2, M[("rst3b2", u)],
                                           M[("qkv_mm_done", u * 4 + j // 2)])
                        M[("xh3nt", 1)] = vc.n
                for back in range(4):
                    M[("ring_gu", HCN - 4 + back)] = M[("rgu", u, HCN - 4 + back)]
                M[("b1_done", u)] = tc.n
                h_ready_v = vc.n

            with contextlib.ExitStack() as pctx:
                pd = [pctx.enter_context(nc.psum_tensor(f"pd{u}_{i}", [128, D], F32))
                      for i in range(4)]
                for grp in range(2):
                    for kcc in range(HCN):
                        slot = kcc % 4
                        if (u, grp, kcc - 4) in M:
                            P.wait("sync", tc, M[(u, grp, kcc - 4)])
                        elif grp == 1 and kcc < 4:
                            P.wait("sync", tc, M[(u, 0, HCN - 4 + kcc)])
                        elif u > 0 and grp == 0 and kcc < 4:
                            P.wait("sync", tc, M[("ring_d", HCN - 4 + kcc)])
                        elif u == 0 and grp == 0 and kcc < 4:
                            for dsl in range(4):
                                P.wait("sync", dst2[dsl])
                        P.op("sync", lambda sy, kcc=kcc, slot=slot: sy.dma_start(
                            out=ring_d[:, D * slot:D * slot + D], in_=scr_d[kcc]), drd[slot], 16)
                        P.wait("tensor", drd[slot])
                        if kcc == 0:
                            P.wait("tensor", vc, h_ready_v)
                        for ji in range(4):
                            j = 4 * grp + ji
                            if kcc == 0 and grp == 1:
                                P.wait("tensor", vc, M[("yevac", ji)])
                            for hf in range(2):
                                P.op("tensor", lambda t, ji=ji, j=j, hf=hf, kcc=kcc, slot=slot, pd=pd: t.matmul(
                                    pd[ji][:, 512 * hf:512 * hf + 512],
                                    h_sb[:, D * kcc + 128 * j:D * kcc + 128 * j + 128],
                                    ring_d[:, D * slot + 512 * hf:D * slot + 512 * hf + 512],
                                    start=(kcc == 0), stop=(kcc == HCN - 1)),
                                    tc if (ji == 3 and hf == 1) else None)
                        M[(u, grp, kcc)] = tc.n
                        if grp == 1 and kcc >= HCN - 4:
                            M[("ring_d", kcc)] = tc.n
                    dn_done_t = tc.n
                    for ji in range(4):
                        P.wait("vector", tc, dn_done_t)
                        P.wait("vector", ac,
                               M.get(("ln4apply", (4 * grp - 4 + ji) % 8), 0))
                        P.op("vector", lambda v, ji=ji, pd=pd: v.tensor_copy(
                            y_sb[:, D * ji:D * ji + D], pd[ji][:]), vc)
                        M[("yevac", ji)] = vc.n
                        if grp == 1 and ji == 0:
                            M[("yevacG1", u)] = vc.n
                    if grp == 1:
                        M[("b2yevac", u)] = vc.n
                    if grp == 0:
                        emit_grp_tail(u, 0)
                    else:
                        tail_pending[0] = lambda u=u: emit_grp_tail(u, 1)

        if tail_pending[0] is not None:
            tail_pending[0]()
            tail_pending[0] = None
        for dsem in dos:
            P.wait("gpsimd", dsem, dsem.n)
            P.wait("sync", dsem, dsem.n)

        with nc.Block() as block:
            @block.tensor
            def _(eng):
                for fn in P.streams["tensor"]:
                    fn(eng)

            @block.vector
            def _(eng):
                for fn in P.streams["vector"]:
                    fn(eng)

            @block.scalar
            def _(eng):
                for fn in P.streams["scalar"]:
                    fn(eng)

            @block.sync
            def _(eng):
                for fn in P.streams["sync"]:
                    fn(eng)

            @block.gpsimd
            def _(eng):
                for fn in P.streams["gpsimd"]:
                    fn(eng)

    return nc


def _host_inputs(inputs):
    perm = np.concatenate([
        np.arange(64 * (rr + 4 * hj), 64 * (rr + 4 * hj) + 64)
        for rr in range(4) for hj in range(4)])
    selh = np.zeros((128, 2), dtype=BFNP)
    selh[0:64, 0] = 1
    selh[64:128, 1] = 1
    bch = np.zeros((2, 128), dtype=BFNP)
    bch[0, 0:64] = 1
    bch[1, 64:128] = 1
    return {
        "wq": np.ascontiguousarray(inputs["L0_attn_Wq"][:, perm]),
        "wk": np.ascontiguousarray(inputs["L0_attn_Wk"][:, perm]),
        "wv": np.ascontiguousarray(inputs["L0_attn_Wv"][:, perm]),
        "wo": np.ascontiguousarray(inputs["L0_attn_Wo"][perm, :]),
        "bq": np.ascontiguousarray(inputs["L0_attn_bq"][perm]),
        "bk": np.ascontiguousarray(inputs["L0_attn_bk"][perm]),
        "bv": np.ascontiguousarray(inputs["L0_attn_bv"][perm]),
        "bo": np.ascontiguousarray(inputs["L0_attn_bo"]),
        "wg": np.ascontiguousarray(inputs["L0_ffn_w_gate"]),
        "wu": np.ascontiguousarray(inputs["L0_ffn_w_up"]),
        "wd": np.ascontiguousarray(inputs["L0_ffn_w_down"]),
        "c_identb": np.eye(128, dtype=np.float32).astype(BFNP),
        "c_ident8": np.eye(8, dtype=np.float32),
        "c_selh": selh,
        "c_bch": bch,
        "c_onesb": np.ones((1, 128), dtype=BFNP),
    }


def kernel(**inputs):
    x = np.asarray(inputs["L0_attn_x"])
    common = _host_inputs({k: np.asarray(v) for k, v in inputs.items()})
    nc = build_nc(UNITS)
    in_maps = []
    for c in range(NCORES):
        m = dict(common)
        m["x"] = np.ascontiguousarray(x[4 * c:4 * c + 4])
        in_maps.append(m)
    res = run_bass_kernel_spmd(nc, in_maps, core_ids=list(range(NCORES)))
    return np.concatenate([res.results[c]["out"] for c in range(NCORES)], axis=0)
